# version 06
# baseline (speedup 1.0000x reference)
import numpy as np
import ml_dtypes
import concourse.bass as bass
import concourse.mybir as mybir
from concourse.bass_utils import run_bass_kernel_spmd
from contextlib import ExitStack

F32 = mybir.dt.float32
BF16 = mybir.dt.bfloat16
AF = mybir.ActivationFunctionType
ALU = mybir.AluOpType
AX = mybir.AxisListType

N_CORES = 8
D = 2048
KC = 16
T_LOC = 1024
TW = 512
S_FULL = 2048
NH = 16
HD = 128
CH = 64
EPS = 1e-5

C_EPS, C_NEGHALF, C_G16, C_NEGG16, C_ONE, C_QKSCALE, C_R1E6, C_SCALE25,     C_NEG1, C_HALF = range(10)
_CONSTS = np.zeros(16, np.float32)
_CONSTS[C_EPS] = 1e-5
_CONSTS[C_NEGHALF] = -0.5
_CONSTS[C_G16] = 1.0 / 16.0
_CONSTS[C_NEGG16] = -1.0 / 16.0
_CONSTS[C_ONE] = 1.0
_CONSTS[C_QKSCALE] = 1.0 / np.sqrt(128.0)
_CONSTS[C_R1E6] = 1e-6
_CONSTS[C_SCALE25] = 2.5
_CONSTS[C_NEG1] = -1.0
_CONSTS[C_HALF] = 0.5
_CONSTS_IN = np.ascontiguousarray(np.broadcast_to(_CONSTS, (128, 16)))


class _Op:
    __slots__ = ("engine", "fn", "dma", "coll", "idx", "deps", "needed", "tag",
                 "inc_no", "sem")

    def __init__(self, engine, fn, dma, coll, tag):
        self.engine = engine
        self.fn = fn
        self.dma = dma
        self.coll = coll
        self.deps = []
        self.needed = False
        self.tag = tag
        self.inc_no = None


class Prog:
    def __init__(self, nc):
        self.nc = nc
        self.q = {e: [] for e in ("tensor", "vector", "scalar", "sync", "gpsimd")}
        self.res = {}
        self._sems = {}

    def op(self, engine, fn, reads=(), writes=(), dma=False, coll=False, tag="",
           sem=None):
        o = _Op(engine, fn, dma, coll, tag)
        o.sem = sem
        if dma:
            assert sem is not None, f"dma op {tag} needs sem="
        if getattr(self, "fence_op", None) is not None:
            fo = self.fence_op
        else:
            fo = None
        o.idx = len(self.q[engine])
        self.q[engine].append(o)
        deps = set()
        for r in reads:
            e = self.res.get(r)
            if e is not None and e[0] is not None:
                deps.add(e[0])
        for w in writes:
            e = self.res.get(w)
            if e is not None:
                if e[0] is not None:
                    deps.add(e[0])
                deps.update(e[1])
        for r in reads:
            self.res.setdefault(r, [None, []])[1].append(o)
        for w in writes:
            self.res[w] = [o, []]
        if fo is not None:
            deps.add(fo)
        for d in deps:
            if d is o:
                continue
            if (o.engine == "tensor" and d.engine == "tensor"
                    and not d.dma and not d.coll):
                continue
            o.deps.append(d)
            d.needed = True
        return o

    def fence(self, fence_tile):
        deps = set()
        for key, (wr, readers) in self.res.items():
            if wr is not None:
                deps.add(wr)
            deps.update(readers)
        old_fence = getattr(self, "fence_op", None)
        self.fence_op = None
        o = self.op("vector", lambda e: e.memset(fence_tile, 0.0),
                    reads=(), writes=(), tag="fence")
        for d in deps:
            if d is o:
                continue
            o.deps.append(d)
            d.needed = True
        if old_fence is not None and old_fence not in deps:
            o.deps.append(old_fence)
            old_fence.needed = True
        self.res = {}
        self.fence_op = o
        return o

    def dma(self, engine, fn, sem, reads=(), writes=(), tag=""):
        return self.op(engine, fn, reads=reads, writes=writes, dma=True, tag=tag,
                       sem=sem)

    def _sem_key(self, o):
        if o.coll:
            return "coll"
        if o.dma:
            return "q_" + o.sem
        return "c_" + o.engine

    def finalize(self):
        nc = self.nc
        counts = {}
        for eng in self.q:
            for o in self.q[eng]:
                if o.dma or o.coll:
                    o.needed = True
                if o.needed:
                    k = self._sem_key(o)
                    step = 16 if o.dma else 1
                    counts[k] = counts.get(k, 0) + step
                    o.inc_no = counts[k]
        stack = ExitStack()
        for k in counts:
            self._sems[k] = stack.enter_context(nc.semaphore(k))
        blk = stack.enter_context(nc.Block())
        prog = self

        def run_engine(ops):
            def body(e):
                hw = {}
                for o in ops:
                    for d in o.deps:
                        k = prog._sem_key(d)
                        v = d.inc_no
                        assert v is not None, (d.tag, d.engine)
                        if hw.get(k, -1) < v:
                            e.wait_ge(prog._sems[k], v)
                            hw[k] = v
                    ins = o.fn(e)
                    if o.needed:
                        assert ins is not None, (o.tag, o.engine)
                        step = 16 if o.dma else 1
                        ins.then_inc(prog._sems[prog._sem_key(o)], step)
            return body

        for eng, deco in (("sync", blk.sync), ("tensor", blk.tensor),
                          ("vector", blk.vector), ("scalar", blk.scalar),
                          ("gpsimd", blk.gpsimd)):
            if self.q[eng]:
                deco(run_engine(self.q[eng]))

        def fin(e):
            for k, v in counts.items():
                e.wait_ge(self._sems[k], v)
        blk.gpsimd(fin)
        stack.close()
        return nc

    def stats(self):
        return {e: len(v) for e, v in self.q.items()}


class Ctx:
    def __init__(self, nc, stack):
        self.nc = nc
        self.P = Prog(nc)
        self.stack = stack
        self.pb = [stack.enter_context(nc.psum_tensor(f"pb{i}", [128, 512], F32))
                   for i in range(8)]
        self.ident = stack.enter_context(nc.sbuf_tensor("ident", [128, 128], F32))
        self.ones1 = stack.enter_context(nc.sbuf_tensor("ones1", [1, 128], F32))
        self.onesc = stack.enter_context(nc.sbuf_tensor("onesc", [128, 1], F32))
        self.ln_x = stack.enter_context(nc.sbuf_tensor("ln_x", [128, 4 * TW], F32))
        self.ln_sq = stack.enter_context(nc.sbuf_tensor("ln_sq", [128, 2 * TW], F32))
        self.ln_row = stack.enter_context(nc.sbuf_tensor("ln_row", [1, 4 * T_LOC], F32))
        self.ln_rst = stack.enter_context(nc.sbuf_tensor("ln_rst", [128, T_LOC], F32))
        self.ln_mr = stack.enter_context(nc.sbuf_tensor("ln_mr", [128, T_LOC], F32))
        self.ln_y = stack.enter_context(nc.sbuf_tensor("ln_y", [128, 4 * TW], F32))
        self.consts = stack.enter_context(nc.sbuf_tensor("consts", [128, 16], F32))
        self.ln_bf = stack.enter_context(nc.sbuf_tensor("ln_bf", [128, 4 * TW], BF16))
        self.sbuf_used = 0

    def sb(self, name, shape, dtype):
        return self.stack.enter_context(self.nc.sbuf_tensor(name, shape, dtype))

    def cst(self, j):
        return self.consts[:, j:j + 1]

    def cstr(self, j):
        return self.consts[0:1, j:j + 1]

    def load_consts(self, ident_dram, consts_dram):
        P = self.P
        P.dma("sync", lambda e: e.dma_start(out=self.ident[:, :],
                                            in_=ident_dram[:, :]),
              sem="ident", writes=["ident"], tag="ld_ident")
        P.dma("sync", lambda e: e.dma_start(out=self.consts[:, :],
                                            in_=consts_dram[:, :]),
              sem="consts", writes=["consts"], tag="ld_consts")
        P.op("vector", lambda e: e.memset(self.ones1[:, :], 1.0),
             writes=["ones1"], tag="ones1")
        P.op("vector", lambda e: e.memset(self.onesc[:, :], 1.0),
             writes=["onesc"], tag="onesc")



class LNStats:

    def __init__(self, ctx, tag, t_tiles=2, tw=TW):
        self.ctx, self.tag = ctx, tag
        self.t_tiles, self.tw = t_tiles, tw
        self.T = t_tiles * tw
        self.n = 0

    def add(self, src, keys):
        ctx, P, tag, tw = self.ctx, self.ctx.P, self.tag, self.tw
        idx = self.n
        self.n += 1
        t, k = divmod(idx, KC)
        pbS, pbQ = ctx.pb[5], ctx.pb[6]
        pb_base = 32 * t
        sqk = ctx.ln_sq[:, (idx % 2) * tw:(idx % 2 + 1) * tw]
        P.op("scalar",
             (lambda src=src, sqk=sqk: lambda e: e.activation(
                 sqk, src, AF.Square))(),
             reads=keys, writes=["lnsq.%d" % (idx % 2)], tag=f"{tag}.sq{idx}")
        P.op("tensor",
             (lambda src=src, k=k, b=pb_base: lambda e: e.matmul(
                 pbS[b:b + 1, 0:tw], ctx.onesc[:, :], src,
                 start=(k == 0), stop=(k == KC - 1)))(),
             reads=list(keys) + ["onesc"], writes=["pb5"], tag=f"{tag}.mmS{idx}")
        P.op("tensor",
             (lambda sqk=sqk, k=k, b=pb_base: lambda e: e.matmul(
                 pbQ[b:b + 1, 0:tw], ctx.onesc[:, :], sqk,
                 start=(k == 0), stop=(k == KC - 1)))(),
             reads=["lnsq.%d" % (idx % 2), "onesc"], writes=["pb6"],
             tag=f"{tag}.mmQ{idx}")

    def finish(self):
        assert self.n == self.t_tiles * KC
        ctx, P, tag, tw, T = self.ctx, self.ctx.P, self.tag, self.tw, self.T
        pbS, pbQ, pbB = ctx.pb[5], ctx.pb[6], ctx.pb[7]
        row = ctx.ln_row
        m_r, v_r = row[:, 0:T], row[:, T:2 * T]
        rstd_r, mr_r = row[:, 2 * T:3 * T], row[:, 3 * T:4 * T]
        for t in range(self.t_tiles):
            b = 32 * t
            P.op("vector",
                 (lambda t=t, b=b: lambda e: e.tensor_scalar(
                     m_r[:, t * tw:(t + 1) * tw], pbS[b:b + 1, 0:tw], 1.0 / D,
                     None, ALU.mult))(),
                 reads=["pb5"], writes=["lnrow.m"], tag=f"{tag}.m{t}")
        P.op("vector", lambda e: e.tensor_mul(v_r, m_r, m_r),
             reads=["lnrow.m"], writes=["lnrow.v"], tag=f"{tag}.msq")
        for t in range(self.t_tiles):
            b = 32 * t
            P.op("vector",
                 (lambda t=t, b=b: lambda e: e.scalar_tensor_tensor(
                     v_r[:, t * tw:(t + 1) * tw], pbQ[b:b + 1, 0:tw], 1.0 / D,
                     v_r[:, t * tw:(t + 1) * tw], ALU.mult, ALU.subtract))(),
                 reads=["pb6", "lnrow.v"], writes=["lnrow.v"],
                 tag=f"{tag}.var{t}")
        P.op("scalar", lambda e: e.activation(rstd_r, v_r, AF.Ln,
                                              bias=ctx.cstr(C_EPS)),
             reads=["lnrow.v", "consts"], writes=["lnrow.rstd"],
             tag=f"{tag}.lnv")
        P.op("scalar", lambda e: e.activation(rstd_r, rstd_r, AF.Exp,
                                              scale=ctx.cstr(C_NEGHALF)),
             reads=["lnrow.rstd", "consts"], writes=["lnrow.rstd"],
             tag="lnrow.rstd")
        P.op("vector", lambda e: e.tensor_mul(mr_r, m_r, rstd_r),
             reads=["lnrow.m", "lnrow.rstd"], writes=["lnrow.mr"],
             tag="lnrow.mr")
        for t in range(self.t_tiles):
            P.op("tensor",
                 (lambda t=t: lambda e: e.matmul(
                     pbB[:, 0:tw], ctx.ones1[:, :],
                     rstd_r[:, t * tw:(t + 1) * tw], start=True, stop=True))(),
                 reads=["lnrow.rstd", "ones1"], writes=["pb5"],
                 tag=f"{tag}.bcR{t}")
            P.op("scalar",
                 (lambda t=t: lambda e: e.copy(
                     self.ctx.ln_rst[:, t * tw:(t + 1) * tw], pbB[:, 0:tw]))(),
                 reads=["pb5"], writes=[f"lnrst.{t}"], tag=f"{tag}.evR{t}")
            P.op("tensor",
                 (lambda t=t: lambda e: e.matmul(
                     pbB[:, 0:tw], ctx.ones1[:, :],
                     mr_r[:, t * tw:(t + 1) * tw], start=True, stop=True))(),
                 reads=["lnrow.mr", "ones1"], writes=["pb5"],
                 tag=f"{tag}.bcM{t}")
            P.op("scalar",
                 (lambda t=t: lambda e: e.copy(
                     self.ctx.ln_mr[:, t * tw:(t + 1) * tw], pbB[:, 0:tw]))(),
                 reads=["pb5"], writes=[f"lnmr.{t}"], tag=f"{tag}.evM{t}")


def emit_ln_x_to_xn(ctx, x_dram, xn_sb, xkey, tag, xdram_keys=None):
    P = ctx.P
    T = T_LOC
    xk = xdram_keys or (lambda k: [])
    st = LNStats(ctx, tag)
    for t in range(2):
        for k in range(KC):
            r = k % 2
            P.dma("sync",
                  (lambda k=k, r=r, t=t: lambda e: e.dma_start(
                      out=ctx.ln_x[:, r * TW:(r + 1) * TW],
                      in_=x_dram[k * 128:(k + 1) * 128, t * TW:(t + 1) * TW]))(),
                  sem=f"lnx.{r}", reads=xk(k), writes=[f"lnx.{r}"],
                  tag=f"{tag}.ld1.{k}.{t}")
            st.add(ctx.ln_x[:, r * TW:(r + 1) * TW], [f"lnx.{r}"])
    st.finish()
    for k in range(KC):
        for t in range(2):
            r = (2 * k + t) % 4
            src = ctx.ln_x[:, r * TW:(r + 1) * TW]
            col = k * T + t * TW
            P.dma("sync",
                  (lambda k=k, t=t, src=src: lambda e: e.dma_start(
                      out=src,
                      in_=x_dram[k * 128:(k + 1) * 128, t * TW:(t + 1) * TW]))(),
                  sem=f"lnx.{r}", reads=xk(k), writes=[f"lnx.{r}"],
                  tag=f"{tag}.ld2.{k}.{t}")
            P.op("vector",
                 (lambda src=src, t=t: lambda e: e.tensor_mul(
                     src, src, ctx.ln_rst[:, t * TW:(t + 1) * TW]))(),
                 reads=[f"lnx.{r}", f"lnrst.{t}"],
                 writes=[f"lnx.{r}"], tag=f"{tag}.ap1.{k}.{t}")
            P.op("vector",
                 (lambda src=src, col=col, t=t: lambda e: e.tensor_sub(
                     xn_sb[:, col:col + TW], src,
                     ctx.ln_mr[:, t * TW:(t + 1) * TW]))(),
                 reads=[f"lnx.{r}", f"lnmr.{t}"],
                 writes=[f"{xkey}.{k}"], tag=f"{tag}.ap2.{k}.{t}")


def emit_apply_residual(ctx, y_dram, ykeys, x_dram, xout_dram, tag,
                        xout_key=None, xdram_keys=None):
    P = ctx.P
    xk = xdram_keys or (lambda k: [])
    for k in range(KC):
        for t in range(2):
            r = (2 * k + t) % 4
            yv = ctx.ln_x[:, r * TW:(r + 1) * TW]
            xv = ctx.ln_y[:, r * TW:(r + 1) * TW]
            P.dma("sync",
                  (lambda k=k, t=t, yv=yv: lambda e: e.dma_start(
                      out=yv, in_=y_dram[k * 128:(k + 1) * 128,
                                         t * TW:(t + 1) * TW]))(),
                  sem=f"lnx.{r}", reads=ykeys(k, t), writes=[f"lnx.{r}"],
                  tag=f"{tag}.ldy.{k}.{t}")
            P.dma("sync",
                  (lambda k=k, t=t, xv=xv: lambda e: e.dma_start(
                      out=xv, in_=x_dram[k * 128:(k + 1) * 128,
                                         t * TW:(t + 1) * TW]))(),
                  sem=f"lny.{r}", reads=xk(k), writes=[f"lny.{r}"],
                  tag=f"{tag}.ldx.{k}.{t}")
            P.op("vector",
                 (lambda yv=yv, t=t: lambda e: e.tensor_mul(
                     yv, yv, ctx.ln_rst[:, t * TW:(t + 1) * TW]))(),
                 reads=[f"lnx.{r}", f"lnrst.{t}"],
                 writes=[f"lnx.{r}"], tag=f"{tag}.a1.{k}.{t}")
            P.op("vector",
                 (lambda yv=yv, t=t: lambda e: e.tensor_sub(
                     yv, yv, ctx.ln_mr[:, t * TW:(t + 1) * TW]))(),
                 reads=[f"lnx.{r}", f"lnmr.{t}"],
                 writes=[f"lnx.{r}"], tag=f"{tag}.a2.{k}.{t}")
            P.op("vector",
                 (lambda yv=yv, xv=xv: lambda e: e.tensor_add(xv, yv, xv))(),
                 reads=[f"lnx.{r}", f"lny.{r}"],
                 writes=[f"lny.{r}"], tag=f"{tag}.a3.{k}.{t}")
            P.dma("gpsimd",
                  (lambda k=k, t=t, xv=xv: lambda e: e.dma_start(
                      out=xout_dram[k * 128:(k + 1) * 128, t * TW:(t + 1) * TW],
                      in_=xv))(),
                  sem=f"{tag}.xost", reads=[f"lny.{r}"],
                  writes=[f"{tag}.xog"] + ([f"{xout_key}.{k}"] if xout_key else []),
                  tag=f"{tag}.st{k}.{t}")



IN0 = 5632
IC0 = IN0 // 128


def emit_l0_ffn(ctx, x_dram, xout_dram, w, y_dram, uid="f0", xkeys=None,
                xout_key=None):
    nc, P = ctx.nc, ctx.P
    u = uid
    xn = ctx.sb(f"{u}_xn", [128, KC * T_LOC], BF16)
    h_all = ctx.sb(f"{u}_h", [128, IC0 * T_LOC], BF16)
    ctx._dbg_xn, ctx._dbg_h = xn, h_all
    yev = ctx.sb(f"{u}_yev", [128, 2 * TW], F32)
    wg_s = ctx.sb(f"{u}_wg", [128, 2 * KC * 128], BF16)
    wu_s = ctx.sb(f"{u}_wu", [128, 2 * KC * 128], BF16)
    wd_s = ctx.sb(f"{u}_wd", [128, IC0 * 128], BF16)
    gs = ctx.sb(f"{u}_gs", [128, IC0], F32)
    us = ctx.sb(f"{u}_us", [128, IC0], F32)
    ds = ctx.sb(f"{u}_ds", [128, KC], F32)
    silu_t = ctx.sb(f"{u}_st", [128, 2 * TW], BF16)

    for nm, t in ((f"{u}_gate_s", gs), (f"{u}_up_s", us), (f"{u}_down_s", ds)):
        P.dma("sync",
              (lambda nm=nm, t=t: lambda e: e.dma_start(out=t[:, :],
                                                        in_=w[nm][:, :]))(),
              sem=f"{u}.scl.{nm}", writes=[f"{u}.{nm}"], tag=f"{u}.{nm}")

    emit_ln_x_to_xn(ctx, x_dram, xn, f"{u}.xn", f"{u}ln", xdram_keys=xkeys)

    st = LNStats(ctx, f"{u}pl")
    for i in range(IC0):
        buf = i % 2
        P.dma("gpsimd",
              (lambda i=i, buf=buf: lambda e: e.dma_start(
                  out=wg_s[:, buf * KC * 128:(buf + 1) * KC * 128],
                  in_=w[f"{u}_gate_wP"][i, :, :]))(),
              sem=f"{u}.wgb.{buf}", writes=[f"{u}.wgb.{buf}"],
              tag=f"{u}.ldg{i}")
        P.dma("gpsimd",
              (lambda i=i, buf=buf: lambda e: e.dma_start(
                  out=wu_s[:, buf * KC * 128:(buf + 1) * KC * 128],
                  in_=w[f"{u}_up_wP"][i, :, :]))(),
              sem=f"{u}.wub.{buf}", writes=[f"{u}.wub.{buf}"],
              tag=f"{u}.ldu{i}")
        for t in range(2):
            tcol = t * TW
            pg, pu = ctx.pb[0 + t], ctx.pb[2 + t]
            for k in range(KC):
                P.op("tensor",
                     (lambda k=k, buf=buf, pg=pg, tcol=tcol: lambda e:
                      e.matmul(pg[:, 0:TW],
                               wg_s[:, (buf * KC + k) * 128:(buf * KC + k + 1) * 128],
                               xn[:, k * T_LOC + tcol:k * T_LOC + tcol + TW],
                               start=(k == 0), stop=(k == KC - 1)))(),
                     reads=[f"{u}.wgb.{buf}", f"{u}.xn.{k}"],
                     writes=[f"pb{0 + t}"], tag=f"{u}.g{i}.{k}.{t}")
                P.op("tensor",
                     (lambda k=k, buf=buf, pu=pu, tcol=tcol: lambda e:
                      e.matmul(pu[:, 0:TW],
                               wu_s[:, (buf * KC + k) * 128:(buf * KC + k + 1) * 128],
                               xn[:, k * T_LOC + tcol:k * T_LOC + tcol + TW],
                               start=(k == 0), stop=(k == KC - 1)))(),
                     reads=[f"{u}.wub.{buf}", f"{u}.xn.{k}"],
                     writes=[f"pb{2 + t}"], tag=f"{u}.u{i}.{k}.{t}")
            sti = silu_t[:, t * TW:(t + 1) * TW]
            P.op("scalar",
                 (lambda sti=sti, pg=pg, i=i: lambda e: e.activation(
                     sti, pg[:, 0:TW], AF.Silu, scale=gs[:, i:i + 1]))(),
                 reads=[f"pb{0 + t}", f"{u}.{u}_gate_s"],
                 writes=[f"{u}.st.{t}"], tag=f"{u}.silu{i}.{t}")
            P.op("vector",
                 (lambda sti=sti, pu=pu, i=i, tcol=tcol: lambda e:
                  e.scalar_tensor_tensor(
                     h_all[:, i * T_LOC + tcol:i * T_LOC + tcol + TW],
                     pu[:, 0:TW], us[:, i:i + 1], sti, ALU.mult, ALU.mult))(),
                 reads=[f"pb{2 + t}", f"{u}.st.{t}", f"{u}.{u}_up_s"],
                 writes=[f"{u}.h.{i}.{t}"], tag=f"{u}.h{i}.{t}")
    for o in range(KC):
        P.dma("gpsimd",
              (lambda o=o: lambda e: e.dma_start(
                  out=wd_s[:, 0:IC0 * 128],
                  in_=w[f"{u}_down_wP"][o, :, :]))(),
              sem=f"{u}.wdb", writes=[f"{u}.wdb"], tag=f"{u}.lddw{o}")
        for t in range(2):
            tcol = t * TW
            po = ctx.pb[4 if t == 0 else 6]
            for i in range(IC0):
                P.op("tensor",
                     (lambda i=i, po=po, tcol=tcol: lambda e: e.matmul(
                         po[:, 0:TW],
                         wd_s[:, i * 128:(i + 1) * 128],
                         h_all[:, i * T_LOC + tcol:i * T_LOC + tcol + TW],
                         start=(i == 0), stop=(i == IC0 - 1)))(),
                     reads=[f"{u}.wdb", f"{u}.h.{i}.{t}"],
                     writes=[f"pb{4 if t == 0 else 6}"],
                     tag=f"{u}.d{o}.{i}.{t}")
            r = (2 * o + t) % 2
            yv = yev[:, r * TW:(r + 1) * TW]
            P.op("scalar",
                 (lambda o=o, po=po, yv=yv: lambda e: e.activation(
                     yv, po[:, 0:TW], AF.Identity, scale=ds[:, o:o + 1]))(),
                 reads=[f"pb{4 if t == 0 else 6}", f"{u}.{u}_down_s"],
                 writes=[f"{u}.ye.{r}"], tag=f"{u}.ev{o}.{t}")
            P.dma("gpsimd",
                  (lambda o=o, tcol=tcol, yv=yv: lambda e: e.dma_start(
                      out=y_dram[o * 128:(o + 1) * 128, tcol:tcol + TW],
                      in_=yv))(),
                  sem=f"{u}.yst", reads=[f"{u}.ye.{r}"],
                  writes=[f"{u}.ydrg"], tag=f"{u}.sty{o}.{t}")
    for t in range(2):
        for k in range(KC):
            r = k % 2
            P.dma("sync",
                  (lambda k=k, r=r, t=t: lambda e: e.dma_start(
                      out=ctx.ln_x[:, r * TW:(r + 1) * TW],
                      in_=y_dram[k * 128:(k + 1) * 128, t * TW:(t + 1) * TW]))(),
                  sem=f"lnx.{r}", reads=[f"{u}.ydrg"], writes=[f"lnx.{r}"],
                  tag=f"{u}pl.ldst.{k}.{t}")
            st.add(ctx.ln_x[:, r * TW:(r + 1) * TW], [f"lnx.{r}"])
    st.finish()
    emit_apply_residual(ctx, y_dram, lambda k, t: [f"{u}.ydrg"], x_dram,
                        xout_dram, f"{u}pl", xout_key=xout_key,
                        xdram_keys=xkeys)


NCH = 32


def emit_ln_to_dram(ctx, x_dram, xn_dram, tag, xkeys=None, out_group=None):
    P = ctx.P
    xk = xkeys or (lambda k: [])
    st = LNStats(ctx, tag)
    for t in range(2):
        for k in range(KC):
            r = k % 2
            P.dma("sync",
                  (lambda k=k, r=r, t=t: lambda e: e.dma_start(
                      out=ctx.ln_x[:, r * TW:(r + 1) * TW],
                      in_=x_dram[k * 128:(k + 1) * 128, t * TW:(t + 1) * TW]))(),
                  sem=f"lnx.{r}", reads=xk(k), writes=[f"lnx.{r}"],
                  tag=f"{tag}.ld1.{k}.{t}")
            st.add(ctx.ln_x[:, r * TW:(r + 1) * TW], [f"lnx.{r}"])
    st.finish()
    for k in range(KC):
        for t in range(2):
            r = (2 * k + t) % 4
            src_ = ctx.ln_x[:, r * TW:(r + 1) * TW]
            dst = ctx.ln_y[:, (r % 2) * TW + 2 * TW:(r % 2) * TW + 2 * TW]
            P.dma("sync",
                  (lambda k=k, t=t, src_=src_: lambda e: e.dma_start(
                      out=src_,
                      in_=x_dram[k * 128:(k + 1) * 128, t * TW:(t + 1) * TW]))(),
                  sem=f"lnx.{r}", reads=xk(k), writes=[f"lnx.{r}"],
                  tag=f"{tag}.ld2.{k}.{t}")
            P.op("vector",
                 (lambda src_=src_, t=t: lambda e: e.tensor_mul(
                     src_, src_, ctx.ln_rst[:, t * TW:(t + 1) * TW]))(),
                 reads=[f"lnx.{r}", f"lnrst.{t}"], writes=[f"lnx.{r}"],
                 tag=f"{tag}.ap1.{k}.{t}")
            bdst = ctx.ln_bf[:, (r % 4) * TW:(r % 4 + 1) * TW]
            P.op("vector",
                 (lambda src_=src_, bdst=bdst, t=t: lambda e: e.tensor_sub(
                     bdst, src_, ctx.ln_mr[:, t * TW:(t + 1) * TW]))(),
                 reads=[f"lnx.{r}", f"lnmr.{t}"], writes=[f"lnbf.{r % 4}"],
                 tag=f"{tag}.ap2.{k}.{t}")
            P.dma("gpsimd",
                  (lambda k=k, t=t, bdst=bdst: lambda e: e.dma_start(
                      out=xn_dram[k * 128:(k + 1) * 128, t * TW:(t + 1) * TW],
                      in_=bdst))(),
                  sem=f"{tag}.xnst", reads=[f"lnbf.{r % 4}"],
                  writes=[f"{tag}.xng"] + ([out_group] if out_group else []),
                  tag=f"{tag}.st{k}.{t}")


def emit_l0_attn(ctx, x_dram, xout_dram, w, scratch, uid="a0", xkeys=None,
                 xout_key=None):
    nc, P = ctx.nc, ctx.P
    u = uid
    xng_loc, xng_gath = scratch["xng_loc"], scratch["xng_gath"]
    wop_loc, wor_out = scratch["wop_loc"], scratch["wor_out"]

    og = ctx.sb(f"{u}_og", [128, 8 * S_FULL], BF16)
    ctx._a0_og = og
    xs = ctx.sb(f"{u}_xs", [128, KC * T_LOC], BF16)
    xlow = ctx.sb(f"{u}_xlow", [16, S_FULL], BF16)
    qf = ctx.sb(f"{u}_qf", [128, S_FULL], BF16)
    kf = ctx.sb(f"{u}_kf", [128, S_FULL], BF16)
    gf = ctx.sb(f"{u}_gf", [128, S_FULL], BF16)
    vtok = ctx.sb(f"{u}_vtok", [64, NCH * 128], BF16)
    zb = ctx.sb(f"{u}_zb", [128, S_FULL], F32)
    of_b = zb
    t0b = ctx.sb(f"{u}_t0b", [128, S_FULL], F32)
    eb = t0b
    gb = ctx.sb(f"{u}_gb", [128, S_FULL], F32)
    dqb = ctx.sb(f"{u}_dqb", [128, S_FULL], F32)
    qh_b = ctx.sb(f"{u}_qhb", [128, S_FULL], BF16)
    kh_b = ctx.sb(f"{u}_khb", [128, S_FULL], BF16)
    qg_b = ctx.sb(f"{u}_qgb", [128, S_FULL], BF16)
    dec = ctx.sb(f"{u}_dec", [128, 2 * NCH], F32)
    S_sb = ctx.sb(f"{u}_S", [128, 128], F32)
    S_bf = ctx.sb(f"{u}_Sbf", [128, 128], BF16)
    atb = ctx.sb(f"{u}_atb", [64, 2 * CH], BF16)
    kst = ctx.sb(f"{u}_kst", [64, 2 * 128], BF16)
    sqs = ctx.sb(f"{u}_sqs", [128, TW], F32)
    rrow = ctx.sb(f"{u}_rrow", [1, TW], F32)
    rb = ctx.sb(f"{u}_rb", [128, TW], F32)
    silb = ctx.sb(f"{u}_silb", [128, TW], BF16)
    wqkg = ctx.sb(f"{u}_wqkg", [128, 2 * 3 * 128], BF16)
    wvs = ctx.ln_bf
    wg1s = ctx.sb(f"{u}_wg1s", [128, 256], BF16)
    wg2s = ctx.sb(f"{u}_wg2s", [16, 8 * 128], BF16)
    bg2s = ctx.sb(f"{u}_bg2s", [128, 8], F32)
    nrmw = ctx.sb(f"{u}_nrmw", [128, 1], F32)
    trilT = ctx.sb(f"{u}s_trilT", [64, CH], F32)
    wos = ctx.sb(f"{u}_wos", [128, 2 * TW], BF16)
    yst = ctx.sb(f"{u}_yst", [128, 4 * TW], F32)
    ksf = dqb
    ctx._a0_kst = kst
    ctx._a0_d = dict(qf=qf, kf=kf, gf=gf, gb=gb, qh_b=qh_b, kh_b=kh_b,
                     qg_b=qg_b, ksf=ksf, of_b=of_b, S_sb=S_sb, vtok=vtok,
                     dec=dec)

    P.dma("sync", lambda e: e.dma_start(out=trilT[:, :], in_=w[f"{u}_trilT"][:, :]),
          sem=f"{u}.trilT", writes=[f"{u}.trilT"], tag=f"{u}.ldtril")
    P.dma("gpsimd", lambda e: e.dma_start(out=wg1s[:, :], in_=w[f"{u}_wg1P"][0, :, :]),
          sem=f"{u}.wg1", writes=[f"{u}.wg1"], tag=f"{u}.ldwg1")
    P.dma("gpsimd", lambda e: e.dma_start(
        out=wg2s[:, :].rearrange("p (h m) -> p h m", m=128),
        in_=w[f"{u}_wg2"].rearrange("h p m -> p h m")),
        sem=f"{u}.wg2", writes=[f"{u}.wg2"], tag=f"{u}.ldwg2")
    P.dma("sync", lambda e: e.dma_start(out=bg2s[:, :], in_=w[f"{u}_bg2"][:, :]),
          sem=f"{u}.bg2", writes=[f"{u}.bg2"], tag=f"{u}.ldbg2")
    P.dma("sync", lambda e: e.dma_start(out=nrmw[:, :], in_=w[f"{u}_norm_w"][:, :]),
          sem=f"{u}.nrmw", writes=[f"{u}.nrmw"], tag=f"{u}.ldnrmw")

    emit_ln_to_dram(ctx, x_dram, xng_loc, f"{u}ln", xkeys=xkeys)
    P.op("gpsimd",
         lambda e: e.collective_compute(
             "AllGather", ALU.bypass,
             replica_groups=[[0, 1], [2, 3], [4, 5], [6, 7]],
             ins=[xng_loc.opt()], outs=[xng_gath.opt()]),
         reads=[f"{u}ln.xng"], writes=[f"{u}.xg"], coll=True, tag=f"{u}.ccxn")

    for k in range(KC):
        for half in range(2):
            s = (2 * k + half) % 4
            P.dma("sync",
                  (lambda k=k, s=s, half=half: lambda e: e.dma_start(
                      out=xs[:, s * T_LOC:(s + 1) * T_LOC],
                      in_=xng_gath[half * D + k * 128:half * D + (k + 1) * 128,
                                   :]))(),
                  sem=f"{u}.xs.{s}", reads=[f"{u}.xg"], writes=[f"{u}.xs.{s}"],
                  tag=f"{u}.xl.ldx{k}.{half}")
            for tc in range(2):
                P.op("tensor",
                     (lambda k=k, s=s, half=half, tc=tc: lambda e: e.matmul(
                         ctx.pb[half * 2 + tc][0:16, 0:TW],
                         wg1s[:, k * 16:(k + 1) * 16],
                         xs[:, s * T_LOC + tc * TW:s * T_LOC + (tc + 1) * TW],
                         start=(k == 0), stop=(k == KC - 1)))(),
                     reads=[f"{u}.wg1", f"{u}.xs.{s}"],
                     writes=[f"pb{half * 2 + tc}"], tag=f"{u}.xl{k}.{half}.{tc}")
    for g4 in range(4):
        P.op("scalar",
             (lambda g4=g4: lambda e: e.copy(
                 xlow[:, g4 * TW:(g4 + 1) * TW], ctx.pb[g4][0:16, 0:TW]))(),
             reads=[f"pb{g4}"], writes=[f"{u}.xlow"], tag=f"{u}.xlev{g4}")

    for g in range(8):
        for th in range(2):
            tc0 = th * T_LOC
            for k in range(KC):
                P.dma("sync",
                      (lambda k=k, th=th: lambda e: e.dma_start(
                          out=xs[:, k * T_LOC:(k + 1) * T_LOC],
                          in_=xng_gath[th * D + k * 128:th * D + (k + 1) * 128,
                                       :]))(),
                      sem=f"{u}.xs.{k}", reads=[f"{u}.xg"],
                      writes=[f"{u}.xs.{k}"], tag=f"{u}.pj{g}.{th}.ldx{k}")
                buf = k % 2
                P.dma("gpsimd",
                      (lambda g=g, k=k, buf=buf: lambda e: e.dma_start(
                          out=wqkg[:, buf * 384:buf * 384 + 384],
                          in_=w[f"{u}_wqkgP"][g, k, :, :]))(),
                      sem=f"{u}.wqkg.{buf}", writes=[f"{u}.wqkg.{buf}"],
                      tag=f"{u}.ldw{g}.{th}.{k}")
                for tc in range(2):
                    for j, nmq in ((0, "q"), (1, "k"), (2, "g")):
                        P.op("tensor",
                             (lambda k=k, tc=tc, j=j, buf=buf: lambda e:
                              e.matmul(
                                 ctx.pb[j * 2 + tc][:, 0:TW],
                                 wqkg[:, buf * 384 + j * 128:buf * 384 + (j + 1) * 128],
                                 xs[:, k * T_LOC + tc * TW:k * T_LOC + (tc + 1) * TW],
                                 start=(k == 0), stop=(k == KC - 1)))(),
                             reads=[f"{u}.wqkg.{buf}", f"{u}.xs.{k}"],
                             writes=[f"pb{j * 2 + tc}"],
                             tag=f"{u}.p{nmq}{g}.{th}.{k}.{tc}")
            for tc in range(2):
                col = tc0 + tc * TW
                for j, (dst, key) in enumerate(
                        ((qf, "qf"), (kf, "kf"), (gf, "gf"))):
                    P.op("scalar",
                         (lambda j=j, tc=tc, dst=dst, col=col: lambda e:
                          e.copy(dst[:, col:col + TW], ctx.pb[j * 2 + tc][:, 0:TW]))(),
                         reads=[f"pb{j * 2 + tc}"], writes=[f"{u}.{key}"],
                         tag=f"{u}.ev{key}{g}.{th}.{tc}")
            if th == 0:
                P.dma("gpsimd",
                      (lambda g=g: lambda e: e.dma_start(
                          out=wvs[:, :].rearrange("p (k m) -> p k m", m=128),
                          in_=w[f"{u}_wv"].rearrange("(k p) m -> p k m", p=128)
                          [:, :, g * 128:(g + 1) * 128]))(),
                      sem=f"{u}.wvs", writes=[f"{u}.wvs", "lnbf.0", "lnbf.1",
                                              "lnbf.2", "lnbf.3"],
                      tag=f"{u}.ldwv{g}")
            for vt in range(8):
                pbk = 6 + (vt % 2)
                for k in range(KC):
                    P.op("tensor",
                         (lambda k=k, vt=vt, pbk=pbk: lambda e:
                          e.matmul(
                             ctx.pb[pbk][:, 0:128],
                             xs[:, k * T_LOC + vt * 128:k * T_LOC + (vt + 1) * 128],
                             wvs[:, k * 128:(k + 1) * 128],
                             start=(k == 0), stop=(k == KC - 1)))(),
                         reads=[f"{u}.wvs", f"{u}.xs.{k}"],
                         writes=[f"pb{pbk}"], tag=f"{u}.pv{g}.{th}.{vt}.{k}")
                gtile = th * 8 + vt
                for hh2 in range(2):
                    cch = gtile * 2 + hh2
                    P.op("scalar",
                         (lambda hh2=hh2, cch=cch, pbk=pbk: lambda e: e.copy(
                             vtok[:, cch * 128:(cch + 1) * 128],
                             ctx.pb[pbk][hh2 * 64:(hh2 + 1) * 64, 0:128]))(),
                         reads=[f"pb{pbk}"], writes=[f"{u}.vtok"],
                         tag=f"{u}.evv{g}.{th}.{vt}.{hh2}")

        for tc in range(4):
            P.op("tensor",
                 (lambda g=g, tc=tc: lambda e: e.matmul(
                     ctx.pb[0][:, 0:TW], wg2s[:, g * 128:(g + 1) * 128],
                     xlow[:, tc * TW:(tc + 1) * TW], start=True, stop=True))(),
                 reads=[f"{u}.wg2", f"{u}.xlow"], writes=["pb0"],
                 tag=f"{u}.zmm{g}.{tc}")
            P.op("scalar",
                 (lambda g=g, tc=tc: lambda e: e.activation(
                     zb[:, tc * TW:(tc + 1) * TW], ctx.pb[0][:, 0:TW],
                     AF.Identity, bias=bg2s[:, g:g + 1]))(),
                 reads=["pb0", f"{u}.bg2"], writes=[f"{u}.zb", f"{u}.ofb"],
                 tag=f"{u}.zev{g}.{tc}")
        P.op("scalar", lambda e: e.activation(t0b[:, :], zb[:, :], AF.Abs),
             reads=[f"{u}.zb"], writes=[f"{u}.t0b"], tag=f"{u}.abs{g}")
        P.op("scalar", lambda e: e.activation(t0b[:, :], t0b[:, :], AF.Exp,
                                              scale=ctx.cst(C_NEG1)),
             reads=[f"{u}.t0b", "consts"], writes=[f"{u}.t0b"], tag=f"{u}.ex{g}")
        P.op("scalar", lambda e: e.activation(t0b[:, :], t0b[:, :], AF.Ln,
                                              bias=ctx.cst(C_ONE)),
             reads=[f"{u}.t0b", "consts"], writes=[f"{u}.t0b"], tag=f"{u}.ln{g}")
        P.op("vector", lambda e: e.tensor_scalar(zb[:, :], zb[:, :], 0.0, None,
                                                 ALU.min),
             reads=[f"{u}.zb"], writes=[f"{u}.zb"], tag=f"{u}.min{g}")
        P.op("vector", lambda e: e.tensor_sub(zb[:, :], zb[:, :], t0b[:, :]),
             reads=[f"{u}.zb", f"{u}.t0b"], writes=[f"{u}.zb"], tag=f"{u}.d{g}")
        P.op("vector", lambda e: e.tensor_tensor_scan(gb[:, :], zb[:, :],
                                                      zb[:, :], 0.0, ALU.add,
                                                      ALU.bypass),
             reads=[f"{u}.zb"], writes=[f"{u}.gb"], tag=f"{u}.scan{g}")
        for c in range(NCH - 1, 0, -1):
            P.op("vector",
                 (lambda c=c: lambda e: e.tensor_scalar(
                     gb[:, c * CH:(c + 1) * CH], gb[:, c * CH:(c + 1) * CH],
                     gb[:, c * CH - 1:c * CH], None, ALU.subtract))(),
                 reads=[f"{u}.gb"], writes=[f"{u}.gb"], tag=f"{u}.gsub{g}.{c}")
        P.op("vector",
             (lambda g=g: lambda e: e.tensor_copy(
                 dec[:, (g % 2) * NCH:(g % 2 + 1) * NCH],
                 gb[:, CH - 1::CH]))(),
             reads=[f"{u}.gb"], writes=[f"{u}.dec.{g % 2}"], tag=f"{u}.dcp{g}")
        P.op("scalar",
             (lambda g=g: lambda e: e.activation(
                 dec[:, (g % 2) * NCH:(g % 2 + 1) * NCH],
                 dec[:, (g % 2) * NCH:(g % 2 + 1) * NCH], AF.Exp,
                 scale=ctx.cst(C_G16)))(),
             reads=[f"{u}.dec.{g % 2}", "consts"], writes=[f"{u}.dec.{g % 2}"],
             tag=f"{u}.dex{g}")
        for c in range(NCH):
            P.op("vector",
                 (lambda c=c: lambda e: e.tensor_scalar(
                     dqb[:, c * CH:(c + 1) * CH], gb[:, c * CH:(c + 1) * CH],
                     gb[:, c * CH:c * CH + 1], None, ALU.subtract))(),
                 reads=[f"{u}.gb"], writes=[f"{u}.dqb"], tag=f"{u}.dq{g}.{c}")
        P.op("scalar", lambda e: e.activation(eb[:, :], dqb[:, :], AF.Exp,
                                              scale=ctx.cst(C_G16)),
             reads=[f"{u}.dqb", "consts"], writes=[f"{u}.t0b"], tag=f"{u}.e1{g}")
        P.op("vector", lambda e: e.scalar_tensor_tensor(qh_b[:, :], qf[:, :],
                                                        1.0 / np.sqrt(128.0),
                                                        eb[:, :], ALU.mult,
                                                        ALU.mult),
             reads=[f"{u}.qf", f"{u}.t0b"], writes=[f"{u}.qhb"], tag=f"{u}.qh{g}")
        P.op("scalar", lambda e: e.activation(eb[:, :], dqb[:, :], AF.Exp,
                                              scale=ctx.cst(C_NEGG16)),
             reads=[f"{u}.dqb", "consts"], writes=[f"{u}.t0b"], tag=f"{u}.e2{g}")
        P.op("vector", lambda e: e.tensor_mul(kh_b[:, :], kf[:, :], eb[:, :]),
             reads=[f"{u}.kf", f"{u}.t0b"], writes=[f"{u}.khb"], tag=f"{u}.kh{g}")
        P.op("scalar", lambda e: e.activation(eb[:, :], gb[:, :], AF.Exp,
                                              scale=ctx.cst(C_G16)),
             reads=[f"{u}.gb", "consts"], writes=[f"{u}.t0b"], tag=f"{u}.e3{g}")
        P.op("vector", lambda e: e.scalar_tensor_tensor(qg_b[:, :], qf[:, :],
                                                        1.0 / np.sqrt(128.0),
                                                        eb[:, :], ALU.mult,
                                                        ALU.mult),
             reads=[f"{u}.qf", f"{u}.t0b"], writes=[f"{u}.qgb"], tag=f"{u}.qg{g}")
        for c in range(NCH):
            P.op("vector",
                 (lambda c=c: lambda e: e.tensor_scalar(
                     dqb[:, c * CH:(c + 1) * CH], gb[:, c * CH:(c + 1) * CH],
                     gb[:, c * CH + CH - 1:c * CH + CH], None, ALU.subtract))(),
                 reads=[f"{u}.gb"], writes=[f"{u}.dqb"], tag=f"{u}.dl{g}.{c}")
        P.op("scalar", lambda e: e.activation(eb[:, :], dqb[:, :], AF.Exp,
                                              scale=ctx.cst(C_NEGG16)),
             reads=[f"{u}.dqb", "consts"], writes=[f"{u}.t0b"], tag=f"{u}.e4{g}")
        P.op("vector", lambda e: e.tensor_mul(ksf[:, :], kf[:, :], eb[:, :]),
             reads=[f"{u}.kf", f"{u}.t0b"], writes=[f"{u}.dqb"], tag=f"{u}.ks{g}")

        P.op("vector", lambda e: e.memset(S_sb[:, :], 0.0),
             writes=[f"{u}.S"], tag=f"{u}.S0{g}")
        P.op("vector", lambda e: e.memset(S_bf[:, :], 0.0),
             writes=[f"{u}.Sbf"], tag=f"{u}.Sb0{g}")
        for c in range(NCH):
            cb = c % 2
            vsl = vtok[:, c * 128:(c + 1) * 128]
            P.op("tensor",
                 (lambda c=c: lambda e: e.matmul(
                     ctx.pb[0][0:64, 0:CH], kh_b[:, c * CH:(c + 1) * CH],
                     qh_b[:, c * CH:(c + 1) * CH], start=True, stop=True))(),
                 reads=[f"{u}.khb", f"{u}.qhb"], writes=["pb0"],
                 tag=f"{u}.at{g}.{c}")
            P.op("vector",
                 (lambda cb=cb: lambda e: e.tensor_mul(
                     atb[:, cb * CH:(cb + 1) * CH], ctx.pb[0][0:64, 0:CH],
                     trilT[:, :]))(),
                 reads=["pb0", f"{u}.trilT"], writes=[f"{u}.atb.{cb}"],
                 tag=f"{u}.atm{g}.{c}")
            ocol = (c % 8) * CH
            P.op("tensor",
                 (lambda c=c, cb=cb, ocol=ocol, vsl=vsl: lambda e: e.matmul(
                     ctx.pb[1][:, ocol:ocol + CH], vsl,
                     atb[:, cb * CH:(cb + 1) * CH], start=True, stop=False))(),
                 reads=[f"{u}.atb.{cb}", f"{u}.vtok"], writes=["pb1"],
                 tag=f"{u}.oi{g}.{c}")
            P.op("tensor",
                 (lambda c=c, ocol=ocol: lambda e: e.matmul(
                     ctx.pb[1][:, ocol:ocol + CH], S_bf[:, :],
                     qg_b[:, c * CH:(c + 1) * CH], start=False, stop=True))(),
                 reads=[f"{u}.Sbf", f"{u}.qgb"], writes=["pb1"],
                 tag=f"{u}.oe{g}.{c}")
            P.op("tensor",
                 (lambda c=c: lambda e: e.transpose(
                     ctx.pb[2][0:64, 0:128], ksf[:, c * CH:(c + 1) * CH],
                     ctx.ident[:, :]))(),
                 reads=[f"{u}.dqb", "ident"], writes=["pb2"],
                 tag=f"{u}.kt{g}.{c}")
            P.op("scalar",
                 (lambda cb=cb: lambda e: e.copy(
                     kst[:, cb * 128:(cb + 1) * 128], ctx.pb[2][0:64, 0:128]))(),
                 reads=["pb2"], writes=[f"{u}.kst.{cb}"], tag=f"{u}.ktc{g}.{c}")
            P.op("tensor",
                 (lambda cb=cb, vsl=vsl: lambda e: e.matmul(
                     ctx.pb[3][:, 0:128], kst[:, cb * 128:(cb + 1) * 128], vsl,
                     start=True, stop=True))(),
                 reads=[f"{u}.kst.{cb}", f"{u}.vtok"], writes=["pb3"],
                 tag=f"{u}.sc{g}.{c}")
            P.op("vector",
                 (lambda c=c, g=g: lambda e: e.scalar_tensor_tensor(
                     S_sb[:, :], S_sb[:, :],
                     dec[:, (g % 2) * NCH + c:(g % 2) * NCH + c + 1],
                     ctx.pb[3][:, 0:128], ALU.mult, ALU.add))(),
                 reads=[f"{u}.S", f"{u}.dec.{g % 2}", "pb3"], writes=[f"{u}.S"],
                 tag=f"{u}.su{g}.{c}")
            P.op("vector", lambda e: e.tensor_copy(S_bf[:, :], S_sb[:, :]),
                 reads=[f"{u}.S"], writes=[f"{u}.Sbf"], tag=f"{u}.sb{g}.{c}")
            if c % 8 == 7:
                blk8 = c // 8
                P.op("scalar",
                     (lambda blk8=blk8: lambda e: e.copy(
                         of_b[:, blk8 * TW:(blk8 + 1) * TW],
                         ctx.pb[1][:, 0:TW]))(),
                     reads=["pb1"], writes=[f"{u}.ofb"], tag=f"{u}.oev{g}.{blk8}")

        for tc in range(4):
            col = tc * TW
            P.op("scalar",
                 (lambda col=col: lambda e: e.activation(
                     sqs[:, :], of_b[:, col:col + TW], AF.Square))(),
                 reads=[f"{u}.ofb"], writes=[f"{u}.sqs"], tag=f"{u}.sq{g}.{tc}")
            P.op("tensor",
                 (lambda: lambda e: e.matmul(
                     ctx.pb[4][0:1, 0:TW], ctx.onesc[:, :], sqs[:, :],
                     start=True, stop=True))(),
                 reads=[f"{u}.sqs", "onesc"], writes=["pb4"],
                 tag=f"{u}.sqm{g}.{tc}")
            P.op("vector",
                 (lambda: lambda e: e.tensor_scalar(
                     rrow[:, :], ctx.pb[4][0:1, 0:TW], 1.0 / 128.0, None,
                     ALU.mult))(),
                 reads=["pb4"], writes=[f"{u}.rrow"], tag=f"{u}.ms{g}.{tc}")
            P.op("scalar",
                 (lambda: lambda e: e.activation(rrow[:, :], rrow[:, :], AF.Ln,
                                                 bias=ctx.cstr(C_EPS)))(),
                 reads=[f"{u}.rrow", "consts"], writes=[f"{u}.rrow"],
                 tag=f"{u}.msl{g}.{tc}")
            P.op("scalar",
                 (lambda: lambda e: e.activation(rrow[:, :], rrow[:, :], AF.Exp,
                                                 scale=ctx.cstr(C_NEGHALF)))(),
                 reads=[f"{u}.rrow", "consts"], writes=[f"{u}.rrow"],
                 tag=f"{u}.mse{g}.{tc}")
            P.op("tensor",
                 (lambda: lambda e: e.matmul(
                     ctx.pb[5][:, 0:TW], ctx.ones1[:, :], rrow[:, :],
                     start=True, stop=True))(),
                 reads=[f"{u}.rrow", "ones1"], writes=["pb5"],
                 tag=f"{u}.msb{g}.{tc}")
            P.op("scalar",
                 (lambda: lambda e: e.copy(rb[:, :], ctx.pb[5][:, 0:TW]))(),
                 reads=["pb5"], writes=[f"{u}.rb"], tag=f"{u}.msc{g}.{tc}")
            P.op("scalar",
                 (lambda col=col: lambda e: e.activation(
                     silb[:, :], gf[:, col:col + TW], AF.Silu))(),
                 reads=[f"{u}.gf"], writes=[f"{u}.silb"], tag=f"{u}.sil{g}.{tc}")
            P.op("vector",
                 (lambda col=col: lambda e: e.scalar_tensor_tensor(
                     sqs[:, :], of_b[:, col:col + TW], nrmw[:, 0:1], rb[:, :],
                     ALU.mult, ALU.mult))(),
                 reads=[f"{u}.ofb", f"{u}.nrmw", f"{u}.rb"], writes=[f"{u}.sqs"],
                 tag=f"{u}.on{g}.{tc}")
            P.op("vector",
                 (lambda g=g, col=col: lambda e: e.tensor_mul(
                     og[:, g * S_FULL + col:g * S_FULL + col + TW], sqs[:, :],
                     silb[:, :]))(),
                 reads=[f"{u}.sqs", f"{u}.silb"], writes=[f"{u}.og.{g}"],
                 tag=f"{u}.ogm{g}.{tc}")

    for tt in range(KC):
        for oc in range(4):
            for fk in range(8):
                buf = fk % 2
                P.dma("gpsimd",
                      (lambda fk=fk, oc=oc, buf=buf: lambda e: e.dma_start(
                          out=wos[:, buf * TW:(buf + 1) * TW],
                          in_=w[f"{u}_wo"][fk * 128:(fk + 1) * 128,
                                           oc * TW:(oc + 1) * TW]))(),
                      sem=f"{u}.wos.{buf}", writes=[f"{u}.wos.{buf}"],
                      tag=f"{u}.ldwo{tt}.{oc}.{fk}")
                P.op("tensor",
                     (lambda fk=fk, tt=tt, buf=buf: lambda e: e.matmul(
                         ctx.pb[6][:, 0:TW],
                         og[:, fk * S_FULL + tt * 128:fk * S_FULL + tt * 128 + 128],
                         wos[:, buf * TW:(buf + 1) * TW],
                         start=(fk == 0), stop=(fk == 7)))(),
                     reads=[f"{u}.wos.{buf}", f"{u}.og.{fk}"], writes=["pb6"],
                     tag=f"{u}.wo{tt}.{oc}.{fk}")
            r = (4 * tt + oc) % 4
            yv = yst[:, r * TW:(r + 1) * TW]
            P.op("scalar",
                 (lambda yv=yv: lambda e: e.copy(yv, ctx.pb[6][:, 0:TW]))(),
                 reads=["pb6"], writes=[f"{u}.yste.{r}"], tag=f"{u}.wev{tt}.{oc}")
            P.dma("gpsimd",
                  (lambda tt=tt, oc=oc, yv=yv: lambda e: e.dma_start(
                      out=wop_loc[tt * 128:(tt + 1) * 128, oc * TW:(oc + 1) * TW],
                      in_=yv))(),
                  sem=f"{u}.wopst", reads=[f"{u}.yste.{r}"],
                  writes=[f"{u}.wopg"], tag=f"{u}.wst{tt}.{oc}")
    P.op("gpsimd",
         lambda e: e.collective_compute(
             "ReduceScatter", ALU.add,
             replica_groups=[[0, 1], [2, 3], [4, 5], [6, 7]],
             ins=[wop_loc.opt()], outs=[wor_out.opt()]),
         reads=[f"{u}.wopg"], writes=[f"{u}.rsc"], coll=True, tag=f"{u}.ccrs")

    y_dram = scratch["y_dram"]
    st = LNStats(ctx, f"{u}pl")
    for t in range(2):
        for k in range(KC):
            for j in range(4):
                sslot = j % 2
                P.dma("sync",
                      (lambda t=t, k=k, j=j, sslot=sslot: lambda e: e.dma_start(
                          out=yst[:, sslot * TW:sslot * TW + 128],
                          in_=wor_out[(t * 4 + j) * 128:(t * 4 + j + 1) * 128,
                                      k * 128:(k + 1) * 128]))(),
                      sem=f"{u}.ytl.{sslot}", reads=[f"{u}.rsc"],
                      writes=[f"{u}.ytl.{sslot}"], tag=f"{u}.ldy{t}.{k}.{j}")
                P.op("tensor",
                     (lambda j=j, sslot=sslot: lambda e: e.transpose(
                         ctx.pb[7][:, j * 128:(j + 1) * 128],
                         yst[:, sslot * TW:sslot * TW + 128],
                         ctx.ident[:, :]))(),
                     reads=[f"{u}.ytl.{sslot}", "ident"], writes=["pb7"],
                     tag=f"{u}.ytr{t}.{k}.{j}")
            r = 2 + (k % 2)
            yv = yst[:, r * TW:(r + 1) * TW]
            P.op("scalar",
                 (lambda yv=yv: lambda e: e.copy(yv, ctx.pb[7][:, 0:TW]))(),
                 reads=["pb7"], writes=[f"{u}.yste.{r}"], tag=f"{u}.yev{t}.{k}")
            st.add(yv, [f"{u}.yste.{r}"])
            P.dma("gpsimd",
                  (lambda t=t, k=k, yv=yv: lambda e: e.dma_start(
                      out=y_dram[k * 128:(k + 1) * 128, t * TW:(t + 1) * TW],
                      in_=yv))(),
                  sem=f"{u}.yst2", reads=[f"{u}.yste.{r}"], writes=[f"{u}.ydrg"],
                  tag=f"{u}.sty{t}.{k}")
    st.finish()
    emit_apply_residual(ctx, y_dram, lambda k, t: [f"{u}.ydrg"], x_dram,
                        xout_dram, f"{u}pl", xout_key=xout_key, xdram_keys=xkeys)


def _prep_a0(inputs, c):
    half = c % 2
    sl = slice(half * 1024, (half + 1) * 1024)
    w = {}
    wq = inputs["L0_attn_Wq"][:, sl]
    wk = inputs["L0_attn_Wk"][:, sl]
    wg = inputs["L0_attn_Wgate"][:, sl]
    qkg = np.zeros((8, KC, 128, 384), np.float32)
    for g in range(8):
        for k in range(KC):
            qkg[g, k, :, 0:128] = wq[k * 128:(k + 1) * 128, g * 128:(g + 1) * 128]
            qkg[g, k, :, 128:256] = wk[k * 128:(k + 1) * 128, g * 128:(g + 1) * 128]
            qkg[g, k, :, 256:384] = wg[k * 128:(k + 1) * 128, g * 128:(g + 1) * 128]
    w["a0_wqkgP"] = np.ascontiguousarray(qkg)
    w["a0_wv"] = np.ascontiguousarray(inputs["L0_attn_Wv"][:, sl])
    w["a0_wg1P"] = pack_lhsT(inputs["L0_attn_Wg1"], 16)
    w["a0_wg2"] = np.ascontiguousarray(
        inputs["L0_attn_Wg2"][:, sl].reshape(16, 8, 128).transpose(1, 0, 2))
    w["a0_bg2"] = col128(inputs["L0_attn_bg2"][sl])
    w["a0_norm_w"] = np.ascontiguousarray(
        inputs["L0_attn_norm_w"].reshape(128, 1))
    w["a0_wo"] = np.ascontiguousarray(inputs["L0_attn_Wo"][sl, :])
    w["a0_trilT"] = np.ascontiguousarray(np.triu(np.ones((64, 64), np.float32)))
    return w


def _dram_a0(nc):
    w = {}
    w["a0_wqkgP"] = nc.dram_tensor("a0_wqkgP", [8, KC, 128, 384], F32,
                                   kind="ExternalInput").ap()
    w["a0_wv"] = nc.dram_tensor("a0_wv", [D, T_LOC], F32,
                                kind="ExternalInput").ap()
    w["a0_wg1P"] = nc.dram_tensor("a0_wg1P", [1, 128, KC * 16], F32,
                                  kind="ExternalInput").ap()
    w["a0_wg2"] = nc.dram_tensor("a0_wg2", [8, 16, 128], F32,
                                 kind="ExternalInput").ap()
    w["a0_bg2"] = nc.dram_tensor("a0_bg2", [128, 8], F32,
                                 kind="ExternalInput").ap()
    w["a0_norm_w"] = nc.dram_tensor("a0_norm_w", [128, 1], F32,
                                    kind="ExternalInput").ap()
    w["a0_wo"] = nc.dram_tensor("a0_wo", [T_LOC, D], F32,
                                kind="ExternalInput").ap()
    w["a0_trilT"] = nc.dram_tensor("a0_trilT", [64, 64], F32,
                                   kind="ExternalInput").ap()
    return w


def _scratch_a0(nc, uid="a0"):
    s = {}
    s["xng_loc"] = nc.dram_tensor(f"{uid}_xng_loc", [D, T_LOC], BF16,
                                  kind="Internal").ap()
    s["xng_gath"] = nc.dram_tensor(f"{uid}_xng_gath", [2 * D, T_LOC], BF16,
                                   kind="Internal").ap()
    s["wop_loc"] = nc.dram_tensor(f"{uid}_wop_loc", [S_FULL, D], F32,
                                  kind="Internal").ap()
    s["wor_out"] = nc.dram_tensor(f"{uid}_wor_out", [T_LOC, D], F32,
                                  kind="Internal").ap()
    s["y_dram"] = nc.dram_tensor(f"{uid}_y_scratch", [D, T_LOC], F32,
                                 kind="Internal").ap()
    return s


def build_a0_only(debug=False):
    nc = bass.Bass("TRN2", target_bir_lowering=False)
    x_in = nc.dram_tensor("x_in", [D, T_LOC], F32, kind="ExternalInput").ap()
    x_out = nc.dram_tensor("x_out", [D, T_LOC], F32, kind="ExternalOutput").ap()
    ident = nc.dram_tensor("ident_in", [128, 128], F32, kind="ExternalInput").ap()
    consts = nc.dram_tensor("consts_in", [128, 16], F32, kind="ExternalInput").ap()
    with ExitStack() as stack:
        ctx = Ctx(nc, stack)
        ctx.load_consts(ident, consts)
        w = _dram_a0(nc)
        scratch = _scratch_a0(nc)
        emit_l0_attn(ctx, x_in, x_out, w, scratch)
        if debug:
            P = ctx.P
            og_dbg = nc.dram_tensor("dbg_og", [128, 8 * S_FULL], BF16,
                                    kind="ExternalOutput").ap()
            P.dma("gpsimd", lambda e: e.dma_start(out=og_dbg[:, :],
                                                  in_=ctx._a0_og[:, :]),
                  sem="dbg", reads=[f"a0.og.{g}" for g in range(8)], tag="dbg.og")
            for nm, tile, key, dt in (
                    ("dbg_qf", ctx._a0_d["qf"], "a0.qf", BF16),
                    ("dbg_kf", ctx._a0_d["kf"], "a0.kf", BF16),
                    ("dbg_gf", ctx._a0_d["gf"], "a0.gf", BF16),
                    ("dbg_gb", ctx._a0_d["gb"], "a0.gb", F32),
                    ("dbg_qhb", ctx._a0_d["qh_b"], "a0.qhb", BF16),
                    ("dbg_khb", ctx._a0_d["kh_b"], "a0.khb", BF16),
                    ("dbg_qgb", ctx._a0_d["qg_b"], "a0.qgb", BF16),
                    ("dbg_ksf", ctx._a0_d["ksf"], "a0.ksf", F32),
                    ("dbg_ofb", ctx._a0_d["of_b"], "a0.ofb", F32),
                    ("dbg_S", ctx._a0_d["S_sb"], "a0.S", F32)):
                dd = nc.dram_tensor(nm, list(tile.shape), dt,
                                    kind="ExternalOutput").ap()
                P.dma("gpsimd",
                      (lambda dd=dd, tile=tile: lambda e: e.dma_start(
                          out=dd[:, :], in_=tile[:, :]))(),
                      sem="dbg", reads=[key], tag=f"dbg.{nm}")
            dv = nc.dram_tensor("dbg_vtok", [64, NCH * 128], BF16,
                                kind="ExternalOutput").ap()
            P.dma("gpsimd", lambda e: e.dma_start(out=dv[:, :],
                                                  in_=ctx._a0_d["vtok"][:, :]),
                  sem="dbg", reads=["a0.vtok"], tag="dbg.vtok")
            dd2 = nc.dram_tensor("dbg_dec", [128, 2 * NCH], F32,
                                 kind="ExternalOutput").ap()
            P.dma("gpsimd", lambda e: e.dma_start(out=dd2[:, :],
                                                  in_=ctx._a0_d["dec"][:, :]),
                  sem="dbg", reads=["a0.dec.1"], tag="dbg.dec")
            dk2 = nc.dram_tensor("dbg_kst", [64, 256], BF16,
                                 kind="ExternalOutput").ap()
            P.dma("gpsimd", lambda e: e.dma_start(out=dk2[:, :],
                                                  in_=ctx._a0_kst[:, :]),
                  sem="dbg", reads=["a0.kst.0", "a0.kst.1"], tag="dbg.kst")
        ctx.P.finalize()
    return nc, ctx.P.stats()



def emit_l1_attn(ctx, x_dram, xout_dram, w, y_dram, scratch, uid="a1",
                 xkeys=None, xout_key=None):
    nc, P = ctx.nc, ctx.P
    u = uid
    xn = ctx.sb(f"{u}_xn", [128, KC * T_LOC], BF16)
    o_sb = ctx.sb(f"{u}_o", [128, KC * T_LOC], BF16)
    q_all = ctx.sb(f"{u}_qa", [128, NH * T_LOC], BF16)
    rden2 = ctx.sb(f"{u}_rden2", [1, TW], F32)
    kh = ctx.sb(f"{u}_kh", [128, 2 * S_FULL], BF16)
    vh = ctx.sb(f"{u}_vh", [128, 2 * S_FULL], BF16)
    p_sb = ctx.sb(f"{u}_p", [128, 4 * TW], BF16)
    kv_st = ctx.sb(f"{u}_kvst", [128, 2 * TW], BF16)
    yst = ctx.sb(f"{u}_yst", [128, 2 * TW], F32)
    wl = ctx.sb(f"{u}_wl", [128, 2 * KC * 128], BF16)
    wv_s = ctx.sb(f"{u}_wv", [128, KC * TW], BF16)
    bq = ctx.sb(f"{u}s_bq", [128, KC], F32)
    bk = ctx.sb(f"{u}s_bk", [128, KC], F32)
    bo = ctx.sb(f"{u}s_bo", [128, KC], F32)
    bvr = ctx.ln_row[0:1, 0:D]
    onesb = ctx.sb(f"{u}_onesb", [128, 1], BF16)
    colsc = ctx.sb(f"{u}_cols", [128, 2], F32)
    rden = ctx.sb(f"{u}_rden", [1, TW], F32)
    rdenb = ctx.sb(f"{u}_rdenb", [128, TW], F32)
    ctx._a1_osb, ctx._a1_qh, ctx._a1_kh, ctx._a1_vh, ctx._a1_colsc = \
        o_sb, q_all, kh, vh, colsc
    k_loc, k_gath = scratch["k_loc"], scratch["k_gath"]
    v_loc, v_gath = scratch["v_loc"], scratch["v_gath"]

    P.op("vector", lambda e: e.memset(onesb[:, :], 1.0), writes=[f"{u}.onesb"],
         tag=f"{u}.onesb")
    for nm, t in ((f"{u}_bq", bq), (f"{u}_bk", bk), (f"{u}_bo", bo)):
        P.dma("sync",
              (lambda nm=nm, t=t: lambda e: e.dma_start(out=t[:, :],
                                                        in_=w[nm][:, :]))(),
              sem=f"{u}.b.{nm}", writes=[f"{u}.{nm}"], tag=f"{u}.ld{nm}")
    emit_ln_x_to_xn(ctx, x_dram, xn, f"{u}.xn", f"{u}ln", xdram_keys=xkeys)
    P.dma("sync", lambda e: e.dma_start(out=bvr[:, :], in_=w[f"{u}_bv_row"][:, :]),
          sem=f"{u}.bvr", writes=[f"{u}.bvr", "lnrow.m", "lnrow.v"],
          tag=f"{u}.ldbvr")

    def proj_fm(nm, which, bias_t, bias_key, out_cb, n_groups=KC):
        for o in range(n_groups):
            buf = o % 2
            P.dma("gpsimd",
                  (lambda o=o, buf=buf, which=which: lambda e: e.dma_start(
                      out=wl[:, buf * KC * 128:(buf + 1) * KC * 128],
                      in_=w[which][o, :, :]))(),
                  sem=f"{u}.wl.{buf}", writes=[f"{u}.wl.{buf}"],
                  tag=f"{u}.{nm}.ldw{o}")
            for t in range(2):
                pq = ctx.pb[t]
                for k in range(KC):
                    P.op("tensor",
                         (lambda k=k, buf=buf, pq=pq, o=o, t=t: lambda e:
                          e.matmul(pq[:, 0:TW],
                                   wl[:, (buf * KC + k) * 128:(buf * KC + k + 1) * 128],
                                   xn[:, k * T_LOC + t * TW:k * T_LOC + t * TW + TW],
                                   start=(k == 0), stop=(k == KC - 1)))(),
                         reads=[f"{u}.wl.{buf}", f"{u}.xn.{k}"],
                         writes=[f"pb{t}"], tag=f"{u}.{nm}.mm{o}.{k}.{t}")
                out_cb(o, t, pq)

    def k_out(o, t, pq):
        r = (2 * o + t) % 2
        yv = kv_st[:, r * TW:(r + 1) * TW]
        P.op("scalar",
             (lambda yv=yv, pq=pq, o=o: lambda e: e.activation(
                 yv, pq[:, 0:TW], AF.Identity, bias=bk[:, o:o + 1]))(),
             reads=[f"pb{t}", f"{u}.{u}_bk"], writes=[f"{u}.kvst.{r}"],
             tag=f"{u}.kev{o}.{t}")
        P.dma("gpsimd",
              (lambda o=o, t=t, yv=yv: lambda e: e.dma_start(
                  out=k_loc[o * 128:(o + 1) * 128, t * TW:(t + 1) * TW],
                  in_=yv))(),
              sem=f"{u}.kst", reads=[f"{u}.kvst.{r}"], writes=[f"{u}.kstg"],
              tag=f"{u}.kst{o}.{t}")
    proj_fm("k", f"{u}_wkP", bk, f"{u}.{u}_bk", k_out)

    for fc in range(4):
        P.dma("gpsimd",
              (lambda fc=fc: lambda e: e.dma_start(
                  out=wv_s[:, :].rearrange("p (k m) -> p k m", m=TW),
                  in_=w[f"{u}_wvT"].rearrange("(k p) m -> p k m", p=128)
                  [:, :, fc * TW:(fc + 1) * TW]))(),
              sem=f"{u}.wv", writes=[f"{u}.wv"], tag=f"{u}.vldw{fc}")
        for tt in range(8):
            pv = ctx.pb[2 + (tt % 2)]
            for k in range(KC):
                P.op("tensor",
                     (lambda k=k, tt=tt, pv=pv: lambda e: e.matmul(
                         pv[:, 0:TW],
                         xn[:, k * T_LOC + tt * 128:k * T_LOC + (tt + 1) * 128],
                         wv_s[:, k * TW:(k + 1) * TW],
                         start=(k == 0), stop=False))(),
                     reads=[f"{u}.wv", f"{u}.xn.{k}"],
                     writes=[f"pb{2 + (tt % 2)}"], tag=f"{u}.vmm{tt}.{fc}.{k}")
            P.op("tensor",
                 (lambda fc=fc, pv=pv: lambda e: e.matmul(
                     pv[:, 0:TW], ctx.ones1[:, :], bvr[:, fc * TW:(fc + 1) * TW],
                     start=False, stop=True))(),
                 reads=["ones1", f"{u}.bvr", "lnrow.m", "lnrow.v"],
                 writes=[f"pb{2 + (tt % 2)}"],
                 tag=f"{u}.vbias{tt}.{fc}")
            r = tt % 2
            yv = kv_st[:, r * TW:(r + 1) * TW]
            P.op("scalar",
                 (lambda yv=yv, pv=pv: lambda e: e.copy(yv, pv[:, 0:TW]))(),
                 reads=[f"pb{2 + (tt % 2)}"], writes=[f"{u}.kvst.{r}"],
                 tag=f"{u}.vev{tt}.{fc}")
            P.dma("gpsimd",
                  (lambda tt=tt, fc=fc, yv=yv: lambda e: e.dma_start(
                      out=v_loc[tt * 128:(tt + 1) * 128, fc * TW:(fc + 1) * TW],
                      in_=yv))(),
                  sem=f"{u}.vst", reads=[f"{u}.kvst.{r}"], writes=[f"{u}.vstg"],
                  tag=f"{u}.vst{tt}.{fc}")

    P.op("gpsimd",
         lambda e: e.collective_compute(
             "AllGather", ALU.bypass,
             replica_groups=[[0, 1], [2, 3], [4, 5], [6, 7]],
             ins=[k_loc.opt()], outs=[k_gath.opt()]),
         reads=[f"{u}.kstg"], writes=[f"{u}.kvg"], coll=True, tag=f"{u}.cck")
    P.op("gpsimd",
         lambda e: e.collective_compute(
             "AllGather", ALU.bypass,
             replica_groups=[[0, 1], [2, 3], [4, 5], [6, 7]],
             ins=[v_loc.opt()], outs=[v_gath.opt()]),
         reads=[f"{u}.vstg"], writes=[f"{u}.kvg"], coll=True, tag=f"{u}.ccv")

    ACC = 1.0 / np.sqrt(128.0)
    nm_dram = scratch["nm_dram"]

    def ld_kh(h, hb, tg):
        P.dma("sync",
              (lambda h=h, hb=hb: lambda e: e.dma_start(
                  out=kh[:, hb * S_FULL:hb * S_FULL + T_LOC],
                  in_=k_gath[h * 128:(h + 1) * 128, :]))(),
              sem=f"{u}.kh.{hb}", reads=[f"{u}.kvg"], writes=[f"{u}.kh.{hb}"],
              tag=f"{u}.{tg}ldkh{h}")
        P.dma("sync",
              (lambda h=h, hb=hb: lambda e: e.dma_start(
                  out=kh[:, hb * S_FULL + T_LOC:(hb + 1) * S_FULL],
                  in_=k_gath[D + h * 128:D + (h + 1) * 128, :]))(),
              sem=f"{u}.kh2.{hb}", reads=[f"{u}.kvg"], writes=[f"{u}.kh2.{hb}"],
              tag=f"{u}.{tg}ldkh2{h}")
        return (kh[:, hb * S_FULL:(hb + 1) * S_FULL],
                [f"{u}.kh.{hb}", f"{u}.kh2.{hb}"])

    for h in range(NH):
        hb = h % 2
        khs, khkeys = ld_kh(h, hb, "A")
        P.dma("gpsimd",
              (lambda h=h, hb=hb: lambda e: e.dma_start(
                  out=wl[:, hb * KC * 128:(hb + 1) * KC * 128],
                  in_=w[f"{u}_wqP"][h, :, :]))(),
              sem=f"{u}.wl.{hb}", writes=[f"{u}.wl.{hb}"], tag=f"{u}.qldw{h}")
        for t in range(2):
            for k in range(KC):
                P.op("tensor",
                     (lambda k=k, hb=hb, t=t: lambda e: e.matmul(
                         ctx.pb[4][:, 0:TW],
                         wl[:, (hb * KC + k) * 128:(hb * KC + k + 1) * 128],
                         xn[:, k * T_LOC + t * TW:k * T_LOC + t * TW + TW],
                         start=(k == 0), stop=(k == KC - 1)))(),
                     reads=[f"{u}.wl.{hb}", f"{u}.xn.{k}"], writes=["pb4"],
                     tag=f"{u}.qmm{h}.{k}.{t}")
            P.op("scalar",
                 (lambda t=t, h=h: lambda e: e.activation(
                     q_all[:, h * T_LOC + t * TW:h * T_LOC + (t + 1) * TW],
                     ctx.pb[4][:, 0:TW], AF.Identity, bias=bq[:, h:h + 1]))(),
                 reads=["pb4", f"{u}.{u}_bq"], writes=[f"{u}.qh.{h}"],
                 tag=f"{u}.qev{h}.{t}")
        qhs = q_all[:, h * T_LOC:(h + 1) * T_LOC]
        for qt in range(2):
            for qc in range(4):
                mcol = colsc[:, 0:1]
                for kg in range(4):
                    pm = ctx.pb[0 + (kg % 2)]
                    P.op("tensor",
                         (lambda qc=qc, kg=kg, pm=pm, qt=qt, qhs=qhs, khs=khs:
                          lambda e: e.matmul(
                             pm[:, 0:TW],
                             qhs[:, qt * TW + qc * 128:qt * TW + (qc + 1) * 128],
                             khs[:, kg * TW:(kg + 1) * TW],
                             start=True, stop=True))(),
                         reads=khkeys + [f"{u}.qh.{h}"],
                         writes=[f"pb{0 + (kg % 2)}"],
                         tag=f"{u}.qmx{h}.{qt}.{qc}.{kg}")
                    P.op("vector",
                         (lambda pm=pm: lambda e: e.reduce_max(
                             colsc[:, 1:2], pm[:, 0:TW], AX.X))(),
                         reads=[f"pb{0 + (kg % 2)}"], writes=[f"{u}.r1"],
                         tag=f"{u}.qrm{h}.{qt}.{qc}.{kg}")
                    if kg == 0:
                        P.op("vector",
                             (lambda mcol=mcol: lambda e: e.tensor_copy(
                                 mcol, colsc[:, 1:2]))(),
                             reads=[f"{u}.r1"], writes=[f"{u}.mcol"],
                             tag=f"{u}.qmc{h}.{qt}.{qc}.{kg}")
                    else:
                        P.op("vector",
                             (lambda mcol=mcol: lambda e: e.tensor_max(
                                 mcol, mcol, colsc[:, 1:2]))(),
                             reads=[f"{u}.r1", f"{u}.mcol"], writes=[f"{u}.mcol"],
                             tag=f"{u}.qmc{h}.{qt}.{qc}.{kg}")
                P.op("vector",
                     (lambda mcol=mcol: lambda e: e.tensor_scalar(
                         mcol, mcol, -1.0, None, ALU.mult))(),
                     reads=[f"{u}.mcol"], writes=[f"{u}.mcol"],
                     tag=f"{u}.qng{h}.{qt}.{qc}")
                P.op("tensor",
                     (lambda qc=qc, mcol=mcol: lambda e: e.transpose(
                         ctx.pb[5][0:1, qc * 128:(qc + 1) * 128], mcol,
                         ctx.ident[:, :]))(),
                     reads=[f"{u}.mcol", "ident"], writes=["pb5"],
                     tag=f"{u}.qtr{h}.{qt}.{qc}")
            P.op("scalar",
                 (lambda: lambda e: e.copy(rden[:, :], ctx.pb[5][0:1, 0:TW]))(),
                 reads=["pb5"], writes=[f"{u}.rrow"], tag=f"{u}.nmc{h}.{qt}")
            P.dma("gpsimd",
                  (lambda h=h, qt=qt: lambda e: e.dma_start(
                      out=nm_dram[h:h + 1, qt * TW:(qt + 1) * TW],
                      in_=rden[:, :]))(),
                  sem=f"{u}.nmst", reads=[f"{u}.rrow"], writes=[f"{u}.nmg"],
                  tag=f"{u}.nmst{h}.{qt}")

    for h in range(NH):
        hb = h % 2
        khs, khkeys = ld_kh(h, hb, "B")
        P.dma("sync",
              (lambda h=h, hb=hb: lambda e: e.dma_start(
                  out=vh[:, hb * S_FULL:(hb + 1) * S_FULL]
                  .rearrange("p (c f) -> p c f", f=128),
                  in_=v_gath[:, h * 128:(h + 1) * 128]
                  .rearrange("(c p) f -> p c f", p=128)))(),
              sem=f"{u}.vh.{hb}", reads=[f"{u}.kvg"], writes=[f"{u}.vh.{hb}"],
              tag=f"{u}.ldvh{h}")
        vhs = vh[:, hb * S_FULL:(hb + 1) * S_FULL]
        qhs = q_all[:, h * T_LOC:(h + 1) * T_LOC]
        for qt in range(2):
            qv = qhs[:, qt * TW:(qt + 1) * TW]
            nmr = rden2[:, 0:TW]
            P.dma("sync",
                  (lambda h=h, qt=qt, nmr=nmr: lambda e: e.dma_start(
                      out=nmr, in_=nm_dram[h:h + 1, qt * TW:(qt + 1) * TW]))(),
                  sem=f"{u}.nml", reads=[f"{u}.nmg"],
                  writes=[f"{u}.nml"], tag=f"{u}.nml{h}.{qt}")
            for kc in range(KC):
                ps = ctx.pb[0 + (kc % 2)]
                P.op("tensor",
                     (lambda kc=kc, ps=ps, qv=qv, khs=khs: lambda e: e.matmul(
                         ps[:, 0:TW], khs[:, kc * 128:(kc + 1) * 128], qv,
                         start=True, stop=False))(),
                     reads=khkeys + [f"{u}.qh.{h}"],
                     writes=[f"pb{0 + (kc % 2)}"], tag=f"{u}.smm{h}.{qt}.{kc}")
                P.op("tensor",
                     (lambda kc=kc, ps=ps, nmr=nmr: lambda e: e.matmul(
                         ps[:, 0:TW], ctx.ones1[:, :], nmr,
                         start=False, stop=True))(),
                     reads=[f"{u}.nml", "ones1"],
                     writes=[f"pb{0 + (kc % 2)}"], tag=f"{u}.sin{h}.{qt}.{kc}")
                P.op("scalar",
                     (lambda kc=kc, ps=ps: lambda e: e.activation(
                         p_sb[:, (kc % 4) * TW:(kc % 4 + 1) * TW], ps[:, 0:TW],
                         AF.Exp, scale=ctx.cst(C_QKSCALE)))(),
                     reads=[f"pb{0 + (kc % 2)}", "consts"],
                     writes=[f"{u}.p.{kc % 4}"], tag=f"{u}.exp{h}.{qt}.{kc}")
                P.op("tensor",
                     (lambda kc=kc: lambda e: e.matmul(
                         ctx.pb[3][0:1, 0:TW], onesb[:, :],
                         p_sb[:, (kc % 4) * TW:(kc % 4 + 1) * TW],
                         start=(kc == 0), stop=(kc == KC - 1)))(),
                     reads=[f"{u}.p.{kc % 4}", f"{u}.onesb"], writes=["pb3"],
                     tag=f"{u}.den{h}.{qt}.{kc}")
                P.op("tensor",
                     (lambda kc=kc, vhs=vhs: lambda e: e.matmul(
                         ctx.pb[2][:, 0:TW], vhs[:, kc * 128:(kc + 1) * 128],
                         p_sb[:, (kc % 4) * TW:(kc % 4 + 1) * TW],
                         start=(kc == 0), stop=(kc == KC - 1)))(),
                     reads=[f"{u}.p.{kc % 4}", f"{u}.vh.{hb}"], writes=["pb2"],
                     tag=f"{u}.omm{h}.{qt}.{kc}")
            P.op("scalar",
                 (lambda: lambda e: e.activation(rden[:, :], ctx.pb[3][0:1, 0:TW],
                                                 AF.Ln))(),
                 reads=["pb3"], writes=[f"{u}.rrow"], tag=f"{u}.rdl{h}.{qt}")
            P.op("scalar",
                 (lambda: lambda e: e.activation(rden[:, :], rden[:, :], AF.Exp,
                                                 scale=ctx.cstr(C_NEG1)))(),
                 reads=[f"{u}.rrow", "consts"], writes=[f"{u}.rrow"],
                 tag=f"{u}.rde{h}.{qt}")
            P.op("tensor",
                 (lambda: lambda e: e.matmul(ctx.pb[5][:, 0:TW], ctx.ones1[:, :],
                                             rden[:, :], start=True, stop=True))(),
                 reads=[f"{u}.rrow", "ones1"], writes=["pb5"],
                 tag=f"{u}.rdb{h}.{qt}")
            P.op("scalar",
                 (lambda: lambda e: e.copy(rdenb[:, :], ctx.pb[5][:, 0:TW]))(),
                 reads=["pb5"], writes=[f"{u}.rdenb"], tag=f"{u}.rdbe{h}.{qt}")
            P.op("vector",
                 (lambda h=h, qt=qt: lambda e: e.tensor_mul(
                     o_sb[:, h * T_LOC + qt * TW:h * T_LOC + (qt + 1) * TW],
                     ctx.pb[2][:, 0:TW], rdenb[:, :]))(),
                 reads=["pb2", f"{u}.rdenb"], writes=[f"{u}.o.{h}"],
                 tag=f"{u}.onrm{h}.{qt}")

    st = LNStats(ctx, f"{u}pl")
    for t in range(2):
        for o in range(KC):
            buf = o % 2
            P.dma("gpsimd",
                  (lambda o=o, buf=buf: lambda e: e.dma_start(
                      out=wl[:, buf * KC * 128:(buf + 1) * KC * 128],
                      in_=w[f"{u}_woP"][o, :, :]))(),
                  sem=f"{u}.wl.{buf}", writes=[f"{u}.wl.{buf}"],
                  tag=f"{u}.oldw{o}.{t}")
            po = ctx.pb[6 if False else 4]
            for k in range(KC):
                P.op("tensor",
                     (lambda k=k, buf=buf, t=t: lambda e: e.matmul(
                         ctx.pb[4][:, 0:TW],
                         wl[:, (buf * KC + k) * 128:(buf * KC + k + 1) * 128],
                         o_sb[:, k * T_LOC + t * TW:k * T_LOC + t * TW + TW],
                         start=(k == 0), stop=(k == KC - 1)))(),
                     reads=[f"{u}.wl.{buf}", f"{u}.o.{k}"], writes=["pb4"],
                     tag=f"{u}.wmm{o}.{k}.{t}")
            r = (2 * o + t) % 2
            yv = yst[:, r * TW:(r + 1) * TW]
            P.op("scalar",
                 (lambda o=o, yv=yv: lambda e: e.activation(
                     yv, ctx.pb[4][:, 0:TW], AF.Identity, bias=bo[:, o:o + 1]))(),
                 reads=["pb4", f"{u}.{u}_bo"], writes=[f"{u}.yste.{r}"],
                 tag=f"{u}.oev{o}.{t}")
            st.add(yv, [f"{u}.yste.{r}"])
            P.dma("gpsimd",
                  (lambda o=o, t=t, yv=yv: lambda e: e.dma_start(
                      out=y_dram[o * 128:(o + 1) * 128, t * TW:(t + 1) * TW],
                      in_=yv))(),
                  sem=f"{u}.yst", reads=[f"{u}.yste.{r}"], writes=[f"{u}.ydrg"],
                  tag=f"{u}.sty{o}.{t}")
    st.finish()
    emit_apply_residual(ctx, y_dram, lambda k, t: [f"{u}.ydrg"], x_dram,
                        xout_dram, f"{u}pl", xout_key=xout_key, xdram_keys=xkeys)


def _prep_a1(inputs):
    w = {}
    w["a1_wqP"] = pack_lhsT(np.ascontiguousarray(inputs["L1_attn_Wq"].T), 128)
    w["a1_wkP"] = pack_lhsT(np.ascontiguousarray(inputs["L1_attn_Wk"].T), 128)
    w["a1_wvT"] = np.ascontiguousarray(inputs["L1_attn_Wv"].T)
    w["a1_woP"] = pack_lhsT(np.ascontiguousarray(inputs["L1_attn_Wo"].T), 128)
    w["a1_bq"] = col128(inputs["L1_attn_bq"])
    w["a1_bk"] = col128(inputs["L1_attn_bk"])
    w["a1_bv_row"] = np.ascontiguousarray(inputs["L1_attn_bv"].reshape(1, -1))
    w["a1_bo"] = col128(inputs["L1_attn_bo"])
    return w


def _dram_a1(nc):
    w = {}
    w["a1_wqP"] = nc.dram_tensor("a1_wqP", [KC, 128, KC * 128], F32,
                                 kind="ExternalInput").ap()
    w["a1_wkP"] = nc.dram_tensor("a1_wkP", [KC, 128, KC * 128], F32,
                                 kind="ExternalInput").ap()
    w["a1_wvT"] = nc.dram_tensor("a1_wvT", [D, D], F32, kind="ExternalInput").ap()
    w["a1_woP"] = nc.dram_tensor("a1_woP", [KC, 128, KC * 128], F32,
                                 kind="ExternalInput").ap()
    w["a1_bq"] = nc.dram_tensor("a1_bq", [128, KC], F32, kind="ExternalInput").ap()
    w["a1_bk"] = nc.dram_tensor("a1_bk", [128, KC], F32, kind="ExternalInput").ap()
    w["a1_bv_row"] = nc.dram_tensor("a1_bv_row", [1, D], F32,
                                    kind="ExternalInput").ap()
    w["a1_bo"] = nc.dram_tensor("a1_bo", [128, KC], F32, kind="ExternalInput").ap()
    return w


def _scratch_a1(nc, uid="a1"):
    s = {}
    s["k_loc"] = nc.dram_tensor(f"{uid}_k_loc", [D, T_LOC], BF16, kind="Internal").ap()
    s["k_gath"] = nc.dram_tensor(f"{uid}_k_gath", [2 * D, T_LOC], BF16,
                                 kind="Internal").ap()
    s["v_loc"] = nc.dram_tensor(f"{uid}_v_loc", [T_LOC, D], BF16, kind="Internal").ap()
    s["v_gath"] = nc.dram_tensor(f"{uid}_v_gath", [S_FULL, D], BF16,
                                 kind="Internal").ap()
    s["nm_dram"] = nc.dram_tensor(f"{uid}_nm", [NH, T_LOC], F32,
                                  kind="Internal").ap()
    return s


def build_a1_only(debug=False):
    nc = bass.Bass("TRN2", target_bir_lowering=False)
    x_in = nc.dram_tensor("x_in", [D, T_LOC], F32, kind="ExternalInput").ap()
    x_out = nc.dram_tensor("x_out", [D, T_LOC], F32, kind="ExternalOutput").ap()
    ident = nc.dram_tensor("ident_in", [128, 128], F32, kind="ExternalInput").ap()
    consts = nc.dram_tensor("consts_in", [128, 16], F32, kind="ExternalInput").ap()
    with ExitStack() as stack:
        ctx = Ctx(nc, stack)
        ctx.load_consts(ident, consts)
        w = _dram_a1(nc)
        scratch = _scratch_a1(nc)
        y_dram = nc.dram_tensor("a1_y_scratch", [D, T_LOC], F32,
                                kind="ExternalOutput" if debug else "Internal").ap()
        emit_l1_attn(ctx, x_in, x_out, w, y_dram, scratch)
        if debug:
            P = ctx.P
            o_dbg = nc.dram_tensor("dbg_o", [128, KC * T_LOC], BF16,
                                   kind="ExternalOutput").ap()
            P.dma("gpsimd", lambda e: e.dma_start(out=o_dbg[:, :],
                                                  in_=ctx._a1_osb[:, :]),
                  sem="dbg", reads=[f"a1.o.{h}" for h in range(NH)], tag="dbg.o")
            q_dbg = nc.dram_tensor("dbg_q", [128, NH * T_LOC], BF16,
                                   kind="ExternalOutput").ap()
            P.dma("gpsimd", lambda e: e.dma_start(out=q_dbg[:, :],
                                                  in_=ctx._a1_qh[:, :]),
                  sem="dbg", reads=[f"a1.qh.{h}" for h in range(NH)], tag="dbg.q")
            kh_dbg = nc.dram_tensor("dbg_kh", [128, 2 * S_FULL], BF16,
                                    kind="ExternalOutput").ap()
            P.dma("gpsimd", lambda e: e.dma_start(out=kh_dbg[:, :],
                                                  in_=ctx._a1_kh[:, :]),
                  sem="dbg", reads=["a1.kh.0", "a1.kh.1", "a1.kh2.0", "a1.kh2.1"],
                  tag="dbg.kh")
            vh_dbg = nc.dram_tensor("dbg_vh", [128, 2 * S_FULL], BF16,
                                    kind="ExternalOutput").ap()
            P.dma("gpsimd", lambda e: e.dma_start(out=vh_dbg[:, :],
                                                  in_=ctx._a1_vh[:, :]),
                  sem="dbg", reads=["a1.vh.0", "a1.vh.1"], tag="dbg.vh")
            c_dbg = nc.dram_tensor("dbg_c", [128, 2], F32,
                                   kind="ExternalOutput").ap()
            P.dma("gpsimd", lambda e: e.dma_start(out=c_dbg[:, :],
                                                  in_=ctx._a1_colsc[:, :]),
                  sem="dbg", reads=["a1.negcol"], tag="dbg.c")
        ctx.P.finalize()
    return nc, ctx.P.stats()



NE = 16
IM = 1024
IME = IM // 128
ISC = 16


def emit_l1_moe(ctx, x_dram, xout_dram, w, comb_dram, uid="m1", xkeys=None,
                xout_key=None):
    nc, P = ctx.nc, ctx.P
    u = uid
    crow = ctx.sb(f"{u}_crow", [1, T_LOC], F32)
    xn = ctx.sb(f"{u}_xn", [128, KC * T_LOC], BF16)
    acc = ctx.sb(f"{u}_acc", [128, KC * T_LOC], F32)
    hsl = ctx.sb(f"{u}_hsl", [128, ISC * T_LOC], BF16)
    wg_s = ctx.sb(f"{u}_wg", [128, 2 * KC * 128], BF16)
    wu_s = ctx.sb(f"{u}_wu", [128, 2 * KC * 128], BF16)
    wd_s = ctx.sb(f"{u}_wd", [128, 2 * ISC * 128], BF16)
    silu_t = ctx.sb(f"{u}_st", [128, 2 * TW], BF16)
    wr = ctx.sb(f"{u}_wr", [128, 8 * 16], F32)
    wcur = ctx.sb(f"{u}_wcur", [128, 16], F32)
    eqs = ctx.sb(f"{u}_eqs", [128, 16], F32)
    tsum = ctx.sb(f"{u}_tsum", [128, 4], F32)
    comb_f = ctx.ln_rst[0:16, :]
    cbb = ctx.ln_mr

    ctx._m1_combf, ctx._m1_acc = comb_f, acc
    emit_ln_x_to_xn(ctx, x_dram, xn, f"{u}.xn", f"{u}ln", xdram_keys=xkeys)

    P.dma("gpsimd",
          lambda e: e.dma_start(out=wg_s[:, 0:256], in_=w[f"{u}_gwP"][0, :, :]),
          sem=f"{u}.gw", writes=[f"{u}.gw"], tag=f"{u}.ldgw")
    for t in range(2):
        for k in range(KC):
            P.op("tensor",
                 (lambda k=k, t=t: lambda e: e.matmul(
                     ctx.pb[0][0:16, t * TW:(t + 1) * TW] if False else
                     ctx.pb[0][0:16, 0:TW],
                     wg_s[:, k * 16:(k + 1) * 16],
                     xn[:, k * T_LOC + t * TW:k * T_LOC + t * TW + TW],
                     start=(k == 0), stop=(k == KC - 1)))(),
                 reads=[f"{u}.gw", f"{u}.xn.{k}"], writes=["pb0"],
                 tag=f"{u}.rmm{k}.{t}")
        P.op("scalar",
             (lambda t=t: lambda e: e.activation(
                 comb_f[:, t * TW:(t + 1) * TW], ctx.pb[0][0:16, 0:TW],
                 AF.Sigmoid))(),
             reads=["pb0"], writes=[f"{u}.wsig", "lnrst.0", "lnrst.1"],
             tag=f"{u}.sig{t}")
    for c in range(8):
        P.op("tensor",
             (lambda c=c: lambda e: e.transpose(
                 ctx.pb[1][0:128, 0:16], comb_f[:, c * 128:(c + 1) * 128],
                 ctx.ident[0:16, 0:16]))(),
             reads=[f"{u}.wsig", "ident"], writes=["pb1"], tag=f"{u}.rtr{c}")
        P.op("scalar",
             (lambda c=c: lambda e: e.copy(wr[:, c * 16:(c + 1) * 16],
                                           ctx.pb[1][0:128, 0:16]))(),
             reads=["pb1"], writes=[f"{u}.wr.{c}"], tag=f"{u}.rtc{c}")
    for c in range(8):
        P.op("vector",
             (lambda c=c: lambda e: e.tensor_copy(wcur[:, :],
                                                  wr[:, c * 16:(c + 1) * 16]))(),
             reads=[f"{u}.wr.{c}"], writes=[f"{u}.wcur"], tag=f"{u}.tc{c}")
        P.op("vector", lambda e: e.memset(tsum[:, 0:1], 0.0),
             writes=[f"{u}.tsum"], tag=f"{u}.ts0.{c}")
        for it in range(8):
            P.op("vector",
                 lambda e: e.reduce_max(tsum[:, 1:2], wcur[:, :], AX.X),
                 reads=[f"{u}.wcur"], writes=[f"{u}.m1r"], tag=f"{u}.rm{c}.{it}")
            P.op("vector",
                 lambda e: e.tensor_scalar(eqs[:, :], wcur[:, :], tsum[:, 1:2],
                                           None, ALU.is_equal),
                 reads=[f"{u}.wcur", f"{u}.m1r"], writes=[f"{u}.eqs"],
                 tag=f"{u}.eq{c}.{it}")
            P.op("vector",
                 lambda e: e.scalar_tensor_tensor(wcur[:, :], eqs[:, :], -1e30,
                                                  wcur[:, :], ALU.mult, ALU.add),
                 reads=[f"{u}.eqs", f"{u}.wcur"], writes=[f"{u}.wcur"],
                 tag=f"{u}.msk{c}.{it}")
            P.op("vector",
                 lambda e: e.tensor_add(tsum[:, 0:1], tsum[:, 0:1], tsum[:, 1:2]),
                 reads=[f"{u}.m1r", f"{u}.tsum"], writes=[f"{u}.tsum"],
                 tag=f"{u}.tsa{c}.{it}")
        P.op("vector",
             lambda e: e.tensor_scalar(eqs[:, :], wcur[:, :], -1e29, None,
                                       ALU.is_lt),
             reads=[f"{u}.wcur"], writes=[f"{u}.eqs"], tag=f"{u}.sel{c}")
        P.op("vector",
             (lambda c=c: lambda e: e.tensor_mul(eqs[:, :], eqs[:, :],
                                                 wr[:, c * 16:(c + 1) * 16]))(),
             reads=[f"{u}.eqs", f"{u}.wr.{c}"], writes=[f"{u}.eqs"],
             tag=f"{u}.cmb{c}")
        P.op("vector",
             lambda e: e.tensor_scalar(tsum[:, 2:3], tsum[:, 0:1], 1e-6, None,
                                       ALU.add),
             reads=[f"{u}.tsum"], writes=[f"{u}.t2"], tag=f"{u}.eps{c}")
        P.op("vector",
             lambda e: e.reciprocal(tsum[:, 3:4], tsum[:, 2:3]),
             reads=[f"{u}.t2"], writes=[f"{u}.t3"], tag=f"{u}.rcp{c}")
        P.op("vector",
             lambda e: e.tensor_scalar(tsum[:, 3:4], tsum[:, 3:4], 2.5, None,
                                       ALU.mult),
             reads=[f"{u}.t3"], writes=[f"{u}.t3"], tag=f"{u}.s25{c}")
        P.op("vector",
             lambda e: e.tensor_scalar(eqs[:, :], eqs[:, :], tsum[:, 3:4], None,
                                       ALU.mult),
             reads=[f"{u}.eqs", f"{u}.t3"], writes=[f"{u}.eqs"],
             tag=f"{u}.cf{c}")
        P.op("tensor",
             (lambda c=c: lambda e: e.transpose(
                 ctx.pb[1][0:16, 0:128], eqs[:, :], ctx.ident[:, :]))(),
             reads=[f"{u}.eqs", "ident"], writes=["pb1"], tag=f"{u}.ctr{c}")
        P.op("scalar",
             (lambda c=c: lambda e: e.copy(comb_f[:, c * 128:(c + 1) * 128],
                                           ctx.pb[1][0:16, 0:128]))(),
             reads=["pb1"], writes=[f"{u}.combf", "lnrst.0", "lnrst.1"],
             tag=f"{u}.cev{c}")

    P.dma("gpsimd",
          lambda e: e.dma_start(out=comb_dram[:, :], in_=comb_f[:, :]),
          sem=f"{u}.cst", reads=[f"{u}.combf", "lnrst.0", "lnrst.1"],
          writes=[f"{u}.combd"], tag=f"{u}.cst")

    for ex in range(-1, NE):
        nic = ISC if ex < 0 else IME
        if ex >= 0:
            cr = crow[:, 0:T_LOC]
            P.dma("sync",
                  (lambda ex=ex, cr=cr: lambda e: e.dma_start(
                      out=cr, in_=comb_dram[ex:ex + 1, :]))(),
                  sem=f"{u}.crow", reads=[f"{u}.combd"],
                  writes=[f"{u}.crow"], tag=f"{u}.ldcr{ex}")
            for t in range(2):
                tcol = t * TW
                P.op("tensor",
                     (lambda cr=cr, tcol=tcol: lambda e: e.matmul(
                         ctx.pb[5][:, 0:TW], ctx.ones1[:, :],
                         cr[:, tcol:tcol + TW], start=True, stop=True))(),
                     reads=[f"{u}.crow", "ones1"], writes=["pb5"],
                     tag=f"{u}.cbm{ex}.{t}")
                P.op("scalar",
                     (lambda tcol=tcol: lambda e: e.copy(
                         cbb[:, tcol:tcol + TW], ctx.pb[5][:, 0:TW]))(),
                     reads=["pb5"], writes=[f"{u}.cbb.{t}", f"lnmr.{t}"],
                     tag=f"{u}.cbe{ex}.{t}")
        for i in range(nic):
            buf = i % 2
            gsrc = (f"{u}_swgP" if ex < 0 else f"{u}_wgP")
            usrc = (f"{u}_swuP" if ex < 0 else f"{u}_wuP")
            gidx = (i,) if ex < 0 else (ex, i)
            P.dma("gpsimd",
                  (lambda gsrc=gsrc, gidx=gidx, buf=buf: lambda e:
                   e.dma_start(
                      out=wg_s[:, buf * KC * 128:(buf + 1) * KC * 128],
                      in_=w[gsrc][gidx]))(),
                  sem=f"{u}.wgb.{buf}", writes=[f"{u}.wgb.{buf}"],
                  tag=f"{u}.ldg{ex}.{i}")
            P.dma("gpsimd",
                  (lambda usrc=usrc, gidx=gidx, buf=buf: lambda e:
                   e.dma_start(
                      out=wu_s[:, buf * KC * 128:(buf + 1) * KC * 128],
                      in_=w[usrc][gidx]))(),
                  sem=f"{u}.wub.{buf}", writes=[f"{u}.wub.{buf}"],
                  tag=f"{u}.ldu{ex}.{i}")
            for t in range(2):
                tcol = t * TW
                pg, pu = ctx.pb[0 + t], ctx.pb[2 + t]
                for k in range(KC):
                    P.op("tensor",
                         (lambda k=k, buf=buf, pg=pg, tcol=tcol: lambda e:
                          e.matmul(pg[:, 0:TW],
                                   wg_s[:, (buf * KC + k) * 128:
                                        (buf * KC + k + 1) * 128],
                                   xn[:, k * T_LOC + tcol:k * T_LOC + tcol + TW],
                                   start=(k == 0), stop=(k == KC - 1)))(),
                         reads=[f"{u}.wgb.{buf}", f"{u}.xn.{k}"],
                         writes=[f"pb{0 + t}"], tag=f"{u}.g{ex}.{i}.{k}.{t}")
                    P.op("tensor",
                         (lambda k=k, buf=buf, pu=pu, tcol=tcol: lambda e:
                          e.matmul(pu[:, 0:TW],
                                   wu_s[:, (buf * KC + k) * 128:
                                        (buf * KC + k + 1) * 128],
                                   xn[:, k * T_LOC + tcol:k * T_LOC + tcol + TW],
                                   start=(k == 0), stop=(k == KC - 1)))(),
                         reads=[f"{u}.wub.{buf}", f"{u}.xn.{k}"],
                         writes=[f"pb{2 + t}"], tag=f"{u}.u{ex}.{i}.{k}.{t}")
                sti = silu_t[:, t * TW:(t + 1) * TW]
                P.op("scalar",
                     (lambda sti=sti, pg=pg: lambda e: e.activation(
                         sti, pg[:, 0:TW], AF.Silu))(),
                     reads=[f"pb{0 + t}"], writes=[f"{u}.st.{t}"],
                     tag=f"{u}.silu{ex}.{i}.{t}")
                hv = hsl[:, i * T_LOC + tcol:i * T_LOC + tcol + TW]
                P.op("vector",
                     (lambda sti=sti, pu=pu, hv=hv: lambda e: e.tensor_mul(
                         hv, pu[:, 0:TW], sti))(),
                     reads=[f"pb{2 + t}", f"{u}.st.{t}"],
                     writes=[f"{u}.h.{i}.{t}"], tag=f"{u}.hm{ex}.{i}.{t}")
                if ex >= 0:
                    P.op("vector",
                         (lambda hv=hv, tcol=tcol, t=t: lambda e: e.tensor_mul(
                             hv, hv, cbb[:, tcol:tcol + TW]))(),
                         reads=[f"{u}.h.{i}.{t}", f"{u}.cbb.{t}",
                                f"lnmr.{t}"],
                         writes=[f"{u}.h.{i}.{t}"], tag=f"{u}.hc{ex}.{i}.{t}")
        dsrc = f"{u}_swdP" if ex < 0 else f"{u}_wdP"
        for o in range(KC):
            buf = o % 2
            didx = (o,) if ex < 0 else (ex, o)
            P.dma("gpsimd",
                  (lambda dsrc=dsrc, didx=didx, buf=buf, nic=nic: lambda e:
                   e.dma_start(
                      out=wd_s[:, buf * ISC * 128:buf * ISC * 128 + nic * 128],
                      in_=w[dsrc][didx]))(),
                  sem=f"{u}.wdb.{buf}", writes=[f"{u}.wdb.{buf}"],
                  tag=f"{u}.lddw{ex}.{o}")
            for t in range(2):
                tcol = t * TW
                po = ctx.pb[4 if t == 0 else 6]
                for i in range(nic):
                    P.op("tensor",
                         (lambda i=i, buf=buf, po=po, nic=nic, tcol=tcol:
                          lambda e: e.matmul(
                             po[:, 0:TW],
                             wd_s[:, (buf * ISC + i) * 128:
                                  (buf * ISC + i + 1) * 128],
                             hsl[:, i * T_LOC + tcol:i * T_LOC + tcol + TW],
                             start=(i == 0), stop=(i == nic - 1)))(),
                         reads=[f"{u}.wdb.{buf}", f"{u}.h.{i}.{t}"],
                         writes=[f"pb{4 if t == 0 else 6}"],
                         tag=f"{u}.d{ex}.{o}.{i}.{t}")
                av = acc[:, o * T_LOC + tcol:o * T_LOC + tcol + TW]
                if ex < 0:
                    P.op("scalar",
                         (lambda av=av, po=po: lambda e: e.copy(av,
                                                                po[:, 0:TW]))(),
                         reads=[f"pb{4 if t == 0 else 6}"],
                         writes=[f"{u}.acc.{o}.{t}"],
                         tag=f"{u}.ai{ex}.{o}.{t}")
                else:
                    P.op("vector",
                         (lambda av=av, po=po: lambda e: e.tensor_add(
                             av, av, po[:, 0:TW]))(),
                         reads=[f"pb{4 if t == 0 else 6}", f"{u}.acc.{o}.{t}"],
                         writes=[f"{u}.acc.{o}.{t}"], tag=f"{u}.aa{ex}.{o}.{t}")

    st = LNStats(ctx, f"{u}pl")
    for t in range(2):
        for o in range(KC):
            st.add(acc[:, o * T_LOC + t * TW:o * T_LOC + (t + 1) * TW],
                   [f"{u}.acc.{o}.{t}"])
    st.finish()
    for k in range(KC):
        for t in range(2):
            r = (2 * k + t) % 4
            av = acc[:, k * T_LOC + t * TW:k * T_LOC + (t + 1) * TW]
            xv = ctx.ln_y[:, r * TW:(r + 1) * TW]
            P.dma("sync",
                  (lambda k=k, t=t, xv=xv: lambda e: e.dma_start(
                      out=xv, in_=x_dram[k * 128:(k + 1) * 128,
                                         t * TW:(t + 1) * TW]))(),
                  sem=f"lny.{r}", reads=(xkeys(k) if xkeys else []),
                  writes=[f"lny.{r}"], tag=f"{u}pl.ldx.{k}.{t}")
            P.op("vector",
                 (lambda av=av, t=t: lambda e: e.tensor_mul(
                     av, av, ctx.ln_rst[:, t * TW:(t + 1) * TW]))(),
                 reads=[f"{u}.acc.{k}.{t}", f"lnrst.{t}"],
                 writes=[f"{u}.acc.{k}.{t}"], tag=f"{u}pl.a1.{k}.{t}")
            P.op("vector",
                 (lambda av=av, t=t: lambda e: e.tensor_sub(
                     av, av, ctx.ln_mr[:, t * TW:(t + 1) * TW]))(),
                 reads=[f"{u}.acc.{k}.{t}", f"lnmr.{t}"],
                 writes=[f"{u}.acc.{k}.{t}"], tag=f"{u}pl.a2.{k}.{t}")
            P.op("vector",
                 (lambda av=av, xv=xv: lambda e: e.tensor_add(av, av, xv))(),
                 reads=[f"{u}.acc.{k}.{t}", f"lny.{r}"],
                 writes=[f"{u}.acc.{k}.{t}"], tag=f"{u}pl.a3.{k}.{t}")
            P.dma("gpsimd",
                  (lambda k=k, t=t, av=av: lambda e: e.dma_start(
                      out=xout_dram[k * 128:(k + 1) * 128, t * TW:(t + 1) * TW],
                      in_=av))(),
                  sem=f"{u}pl.xost", reads=[f"{u}.acc.{k}.{t}"],
                  writes=[f"{u}pl.xog"] + ([f"{xout_key}.{k}"] if xout_key else []),
                  tag=f"{u}pl.st{k}.{t}")


def _prep_m1(inputs):
    w = {}
    w["m1_gwP"] = pack_lhsT(np.ascontiguousarray(inputs["L1_ffn_gate_w"].T), 16)
    w["m1_swgP"] = pack_lhsT(inputs["L1_ffn_sWg"], 128)
    w["m1_swuP"] = pack_lhsT(inputs["L1_ffn_sWu"], 128)
    w["m1_swdP"] = pack_lhsT(inputs["L1_ffn_sWd"], 128)
    w["m1_wgP"] = np.stack([pack_lhsT(inputs["L1_ffn_Wg"][e], 128)
                            for e in range(NE)])
    w["m1_wuP"] = np.stack([pack_lhsT(inputs["L1_ffn_Wu"][e], 128)
                            for e in range(NE)])
    w["m1_wdP"] = np.stack([pack_lhsT(inputs["L1_ffn_Wd"][e], 128)
                            for e in range(NE)])
    return w


def _dram_m1(nc):
    w = {}
    w["m1_gwP"] = nc.dram_tensor("m1_gwP", [1, 128, KC * 16], F32,
                                 kind="ExternalInput").ap()
    w["m1_swgP"] = nc.dram_tensor("m1_swgP", [ISC, 128, KC * 128], F32,
                                  kind="ExternalInput").ap()
    w["m1_swuP"] = nc.dram_tensor("m1_swuP", [ISC, 128, KC * 128], F32,
                                  kind="ExternalInput").ap()
    w["m1_swdP"] = nc.dram_tensor("m1_swdP", [KC, 128, ISC * 128], F32,
                                  kind="ExternalInput").ap()
    w["m1_wgP"] = nc.dram_tensor("m1_wgP", [NE, IME, 128, KC * 128], F32,
                                 kind="ExternalInput").ap()
    w["m1_wuP"] = nc.dram_tensor("m1_wuP", [NE, IME, 128, KC * 128], F32,
                                 kind="ExternalInput").ap()
    w["m1_wdP"] = nc.dram_tensor("m1_wdP", [NE, KC, 128, IME * 128], F32,
                                 kind="ExternalInput").ap()
    return w


def build_m1_only(debug=False):
    nc = bass.Bass("TRN2", target_bir_lowering=False)
    x_in = nc.dram_tensor("x_in", [D, T_LOC], F32, kind="ExternalInput").ap()
    x_out = nc.dram_tensor("x_out", [D, T_LOC], F32, kind="ExternalOutput").ap()
    ident = nc.dram_tensor("ident_in", [128, 128], F32, kind="ExternalInput").ap()
    consts = nc.dram_tensor("consts_in", [128, 16], F32, kind="ExternalInput").ap()
    with ExitStack() as stack:
        ctx = Ctx(nc, stack)
        ctx.load_consts(ident, consts)
        w = _dram_m1(nc)
        comb_dram = nc.dram_tensor("m1_comb_scratch", [16, T_LOC], F32,
                                   kind="Internal").ap()
        emit_l1_moe(ctx, x_in, x_out, w, comb_dram)
        if debug:
            P = ctx.P
            cf_dbg = nc.dram_tensor("dbg_combf", [16, T_LOC], F32,
                                    kind="ExternalOutput").ap()
            P.dma("gpsimd", lambda e: e.dma_start(out=cf_dbg[:, :],
                                                  in_=ctx._m1_combf[:, :]),
                  sem="dbg", reads=["m1.combf"], tag="dbg.cf")
            acc_dbg = nc.dram_tensor("dbg_acc", [128, KC * T_LOC], F32,
                                     kind="ExternalOutput").ap()
            P.dma("gpsimd", lambda e: e.dma_start(out=acc_dbg[:, :],
                                                  in_=ctx._m1_acc[:, :]),
                  sem="dbg",
                  reads=[f"m1.acc.{o}.{t}" for o in range(KC) for t in range(2)],
                  tag="dbg.acc")
        ctx.P.finalize()
    return nc, ctx.P.stats()



def pack_lhsT(W, mg):
    K, M = W.shape
    kc = K // 128
    G = M // mg
    return np.ascontiguousarray(
        W.reshape(kc, 128, G, mg).transpose(2, 1, 0, 3).reshape(G, 128, kc * mg))


def col128(v):
    n = v.shape[0]
    return np.ascontiguousarray(v.reshape(n // 128, 128).T)


def _prep_f0(inputs):
    w = {}
    w["f0_gate_wP"] = pack_lhsT(np.ascontiguousarray(inputs["L0_ffn_gate_w"].T), 128)
    w["f0_up_wP"] = pack_lhsT(np.ascontiguousarray(inputs["L0_ffn_up_w"].T), 128)
    w["f0_down_wP"] = pack_lhsT(np.ascontiguousarray(inputs["L0_ffn_down_w"].T), 128)
    w["f0_gate_s"] = col128(inputs["L0_ffn_gate_s"])
    w["f0_up_s"] = col128(inputs["L0_ffn_up_s"])
    w["f0_down_s"] = col128(inputs["L0_ffn_down_s"])
    return w


def _dram_f0(nc):
    w = {}
    w["f0_gate_wP"] = nc.dram_tensor("f0_gate_wP", [IC0, 128, KC * 128], F32,
                                     kind="ExternalInput").ap()
    w["f0_up_wP"] = nc.dram_tensor("f0_up_wP", [IC0, 128, KC * 128], F32,
                                   kind="ExternalInput").ap()
    w["f0_down_wP"] = nc.dram_tensor("f0_down_wP", [KC, 128, IC0 * 128], F32,
                                     kind="ExternalInput").ap()
    w["f0_gate_s"] = nc.dram_tensor("f0_gate_s", [128, IC0], F32,
                                    kind="ExternalInput").ap()
    w["f0_up_s"] = nc.dram_tensor("f0_up_s", [128, IC0], F32,
                                  kind="ExternalInput").ap()
    w["f0_down_s"] = nc.dram_tensor("f0_down_s", [128, KC], F32,
                                    kind="ExternalInput").ap()
    return w



def build_f0_only(debug=False):
    nc = bass.Bass("TRN2", target_bir_lowering=False)
    x_in = nc.dram_tensor("x_in", [D, T_LOC], F32, kind="ExternalInput").ap()
    x_out = nc.dram_tensor("x_out", [D, T_LOC], F32, kind="ExternalOutput").ap()
    ident = nc.dram_tensor("ident_in", [128, 128], F32, kind="ExternalInput").ap()
    consts = nc.dram_tensor("consts_in", [128, 16], F32, kind="ExternalInput").ap()
    with ExitStack() as stack:
        ctx = Ctx(nc, stack)
        ctx.load_consts(ident, consts)
        w = _dram_f0(nc)
        y_dram = nc.dram_tensor("f0_y_scratch", [D, T_LOC], F32,
                                kind="ExternalOutput" if debug else "Internal").ap()
        emit_l0_ffn(ctx, x_in, x_out, w, y_dram)
        if debug:
            xn_t = ctx.nc.find_tensor if False else None
            dbg_xn = nc.dram_tensor("dbg_xn", [128, KC * T_LOC], BF16,
                                    kind="ExternalOutput").ap()
            xn_sb = ctx._dbg_xn
            for k in range(KC):
                ctx.P.dma("gpsimd",
                          (lambda k=k: lambda e: e.dma_start(
                              out=dbg_xn[:, k * T_LOC:(k + 1) * T_LOC],
                              in_=xn_sb[:, k * T_LOC:(k + 1) * T_LOC]))(),
                          sem="dbg", reads=[f"f0.xn.{k}"], tag=f"dbg.xn{k}")
            dbg_h = nc.dram_tensor("dbg_h", [128, IC0 * T_LOC], BF16,
                                   kind="ExternalOutput").ap()
            ctx.P.dma("gpsimd",
                      lambda e: e.dma_start(out=dbg_h[:, :], in_=ctx._dbg_h[:, :]),
                      sem="dbg", reads=[f"f0.h.{i}" for i in range(IC0)],
                      tag="dbg.h")
            dbg_rst = nc.dram_tensor("dbg_rst", [128, T_LOC], F32,
                                     kind="ExternalOutput").ap()
            dbg_mr = nc.dram_tensor("dbg_mr", [128, T_LOC], F32,
                                    kind="ExternalOutput").ap()
            ctx.P.dma("gpsimd",
                      lambda e: e.dma_start(out=dbg_rst[:, :], in_=ctx.ln_rst[:, :]),
                      sem="dbg", reads=["lnrst.0", "lnrst.1"], tag="dbg.rst")
            ctx.P.dma("gpsimd",
                      lambda e: e.dma_start(out=dbg_mr[:, :], in_=ctx.ln_mr[:, :]),
                      sem="dbg", reads=["lnmr.0", "lnmr.1"], tag="dbg.mr")
        ctx.P.finalize()
    return nc, ctx.P.stats()


def _shard_x(x):
    shards = []
    for c in range(N_CORES):
        b, h = c // 2, c % 2
        shards.append(np.ascontiguousarray(x[b, h * T_LOC:(h + 1) * T_LOC, :].T))
    return shards


def _unshard_x(shards):
    out = np.empty((4, S_FULL, D), np.float32)
    for c in range(N_CORES):
        b, h = c // 2, c % 2
        out[b, h * T_LOC:(h + 1) * T_LOC, :] = np.asarray(shards[c]).T
    return out


_IDENT = np.eye(128, dtype=np.float32)


def build_fused():
    nc = bass.Bass("TRN2", target_bir_lowering=False)
    x_in = nc.dram_tensor("x_in", [D, T_LOC], F32, kind="ExternalInput").ap()
    x_out = nc.dram_tensor("x_out", [D, T_LOC], F32, kind="ExternalOutput").ap()
    ident = nc.dram_tensor("ident_in", [128, 128], F32, kind="ExternalInput").ap()
    consts = nc.dram_tensor("consts_in", [128, 16], F32, kind="ExternalInput").ap()
    x1 = nc.dram_tensor("x1_mid", [D, T_LOC], F32, kind="Internal").ap()
    x2 = nc.dram_tensor("x2_mid", [D, T_LOC], F32, kind="Internal").ap()
    x3 = nc.dram_tensor("x3_mid", [D, T_LOC], F32, kind="Internal").ap()
    wa0 = _dram_a0(nc)
    wf0 = _dram_f0(nc)
    wa1 = _dram_a1(nc)
    wm1 = _dram_m1(nc)
    scr_a0 = _scratch_a0(nc)
    scr_a1 = _scratch_a1(nc)
    y_f0 = nc.dram_tensor("f0_y_scratch", [D, T_LOC], F32, kind="Internal").ap()
    y_a1 = nc.dram_tensor("a1_y_scratch", [D, T_LOC], F32, kind="Internal").ap()
    comb_dram = nc.dram_tensor("m1_comb_scratch", [16, T_LOC], F32,
                               kind="Internal").ap()
    with ExitStack() as stack:
        ctx = Ctx(nc, stack)
        ctx.load_consts(ident, consts)
        main_stack = ctx.stack
        with ExitStack() as s0:
            ctx.stack = s0
            emit_l0_attn(ctx, x_in, x1, wa0, scr_a0)
        ctx.P.fence(ctx.ln_sq[0:1, 0:1])
        with ExitStack() as s1:
            ctx.stack = s1
            emit_l0_ffn(ctx, x1, x2, wf0, y_f0)
        ctx.P.fence(ctx.ln_sq[0:1, 0:1])
        with ExitStack() as s2:
            ctx.stack = s2
            emit_l1_attn(ctx, x2, x3, wa1, y_a1, scr_a1)
        ctx.P.fence(ctx.ln_sq[0:1, 0:1])
        with ExitStack() as s3:
            ctx.stack = s3
            emit_l1_moe(ctx, x3, x_out, wm1, comb_dram)
        ctx.stack = main_stack
        ctx.P.finalize()
    return nc, ctx.P.stats()


_BUILD_CACHE = {}


def _get_nc(name, builder):
    if name not in _BUILD_CACHE:
        _BUILD_CACHE[name] = builder()
    return _BUILD_CACHE[name]


def kernel(**inputs):
    inputs = {k: np.ascontiguousarray(np.asarray(v)) for k, v in inputs.items()}
    shards = _shard_x(inputs["L0_attn_x"])
    common = {"ident_in": _IDENT, "consts_in": _CONSTS_IN}
    nc = _get_nc("fused", lambda: build_fused()[0])
    wf0 = _prep_f0(inputs)
    wa1 = _prep_a1(inputs)
    wm1 = _prep_m1(inputs)
    in_maps = []
    for c in range(N_CORES):
        m = {**common, **wf0, **wa1, **wm1, **_prep_a0(inputs, c)}
        m["x_in"] = shards[c]
        in_maps.append(m)
    res = run_bass_kernel_spmd(nc, in_maps, core_ids=list(range(N_CORES)))
    return _unshard_x([res.results[c]["x_out"] for c in range(N_CORES)])


def kernel_unfused(**inputs):
    inputs = {k: np.ascontiguousarray(np.asarray(v)) for k, v in inputs.items()}
    shards = _shard_x(inputs["L0_attn_x"])
    common = {"ident_in": _IDENT, "consts_in": _CONSTS_IN}

    nc = _get_nc("a0", lambda: build_a0_only()[0])
    wa0 = [_prep_a0(inputs, c) for c in range(N_CORES)]
    in_maps = [{**common, **wa0[c], "x_in": shards[c]} for c in range(N_CORES)]
    res = run_bass_kernel_spmd(nc, in_maps, core_ids=list(range(N_CORES)))
    shards = [np.ascontiguousarray(np.asarray(res.results[c]["x_out"]))
              for c in range(N_CORES)]

    nc = _get_nc("f0", lambda: build_f0_only()[0])
    wf0 = _prep_f0(inputs)
    in_maps = [{**common, **wf0, "x_in": shards[c]} for c in range(N_CORES)]
    res = run_bass_kernel_spmd(nc, in_maps, core_ids=list(range(N_CORES)))
    shards = [np.ascontiguousarray(np.asarray(res.results[c]["x_out"]))
              for c in range(N_CORES)]

    nc = _get_nc("a1", lambda: build_a1_only()[0])
    wa1 = _prep_a1(inputs)
    in_maps = [{**common, **wa1, "x_in": shards[c]} for c in range(N_CORES)]
    res = run_bass_kernel_spmd(nc, in_maps, core_ids=list(range(N_CORES)))
    shards = [np.ascontiguousarray(np.asarray(res.results[c]["x_out"]))
              for c in range(N_CORES)]

    nc = _get_nc("m1", lambda: build_m1_only()[0])
    wm1 = _prep_m1(inputs)
    in_maps = [{**common, **wm1, "x_in": shards[c]} for c in range(N_CORES)]
    res = run_bass_kernel_spmd(nc, in_maps, core_ids=list(range(N_CORES)))
    shards = [res.results[c]["x_out"] for c in range(N_CORES)]
    return _unshard_x(shards)


# version 07 (final)
# speedup vs baseline: 1.0334x; 1.0334x over previous
import numpy as np
import ml_dtypes
import concourse.bass as bass
import concourse.mybir as mybir
from concourse.bass_utils import run_bass_kernel_spmd
from contextlib import ExitStack

F32 = mybir.dt.float32
BF16 = mybir.dt.bfloat16
AF = mybir.ActivationFunctionType
ALU = mybir.AluOpType
AX = mybir.AxisListType

N_CORES = 8
D = 2048
KC = 16
T_LOC = 1024
TW = 512
S_FULL = 2048
NH = 16
HD = 128
CH = 64
EPS = 1e-5

C_EPS, C_NEGHALF, C_G16, C_NEGG16, C_ONE, C_QKSCALE, C_R1E6, C_SCALE25,     C_NEG1, C_HALF = range(10)
_CONSTS = np.zeros(16, np.float32)
_CONSTS[C_EPS] = 1e-5
_CONSTS[C_NEGHALF] = -0.5
_CONSTS[C_G16] = 1.0 / 16.0
_CONSTS[C_NEGG16] = -1.0 / 16.0
_CONSTS[C_ONE] = 1.0
_CONSTS[C_QKSCALE] = 1.0 / np.sqrt(128.0)
_CONSTS[C_R1E6] = 1e-6
_CONSTS[C_SCALE25] = 2.5
_CONSTS[C_NEG1] = -1.0
_CONSTS[C_HALF] = 0.5
_CONSTS_IN = np.ascontiguousarray(np.broadcast_to(_CONSTS, (128, 16)))


class _Op:
    __slots__ = ("engine", "fn", "dma", "coll", "idx", "deps", "needed", "tag",
                 "inc_no", "sem")

    def __init__(self, engine, fn, dma, coll, tag):
        self.engine = engine
        self.fn = fn
        self.dma = dma
        self.coll = coll
        self.deps = []
        self.needed = False
        self.tag = tag
        self.inc_no = None


class Prog:
    def __init__(self, nc):
        self.nc = nc
        self.q = {e: [] for e in ("tensor", "vector", "scalar", "sync", "gpsimd")}
        self.res = {}
        self._sems = {}

    def op(self, engine, fn, reads=(), writes=(), dma=False, coll=False, tag="",
           sem=None):
        o = _Op(engine, fn, dma, coll, tag)
        o.sem = sem
        if dma:
            assert sem is not None, f"dma op {tag} needs sem="
        if getattr(self, "fence_op", None) is not None:
            fo = self.fence_op
        else:
            fo = None
        o.idx = len(self.q[engine])
        self.q[engine].append(o)
        deps = set()
        for r in reads:
            e = self.res.get(r)
            if e is not None and e[0] is not None:
                deps.add(e[0])
        for w in writes:
            e = self.res.get(w)
            if e is not None:
                if e[0] is not None:
                    deps.add(e[0])
                deps.update(e[1])
        for r in reads:
            self.res.setdefault(r, [None, []])[1].append(o)
        for w in writes:
            self.res[w] = [o, []]
        if fo is not None:
            deps.add(fo)
        for d in deps:
            if d is o:
                continue
            if (o.engine == "tensor" and d.engine == "tensor"
                    and not d.dma and not d.coll):
                continue
            o.deps.append(d)
            d.needed = True
        return o

    def fence(self, fence_tile):
        deps = set()
        for key, (wr, readers) in self.res.items():
            if wr is not None:
                deps.add(wr)
            deps.update(readers)
        old_fence = getattr(self, "fence_op", None)
        self.fence_op = None
        o = self.op("vector", lambda e: e.memset(fence_tile, 0.0),
                    reads=(), writes=(), tag="fence")
        for d in deps:
            if d is o:
                continue
            o.deps.append(d)
            d.needed = True
        if old_fence is not None and old_fence not in deps:
            o.deps.append(old_fence)
            old_fence.needed = True
        self.res = {}
        self.fence_op = o
        return o

    def dma(self, engine, fn, sem, reads=(), writes=(), tag=""):
        return self.op(engine, fn, reads=reads, writes=writes, dma=True, tag=tag,
                       sem=sem)

    def _sem_key(self, o):
        if o.coll:
            return "coll"
        if o.dma:
            return "q_" + o.sem
        return "c_" + o.engine

    def finalize(self):
        nc = self.nc
        counts = {}
        for eng in self.q:
            for o in self.q[eng]:
                if o.dma or o.coll:
                    o.needed = True
                if o.needed:
                    k = self._sem_key(o)
                    step = 16 if o.dma else 1
                    counts[k] = counts.get(k, 0) + step
                    o.inc_no = counts[k]
        stack = ExitStack()
        for k in counts:
            self._sems[k] = stack.enter_context(nc.semaphore(k))
        blk = stack.enter_context(nc.Block())
        prog = self

        def run_engine(ops):
            def body(e):
                hw = {}
                for o in ops:
                    for d in o.deps:
                        k = prog._sem_key(d)
                        v = d.inc_no
                        assert v is not None, (d.tag, d.engine)
                        if hw.get(k, -1) < v:
                            e.wait_ge(prog._sems[k], v)
                            hw[k] = v
                    ins = o.fn(e)
                    if o.needed:
                        assert ins is not None, (o.tag, o.engine)
                        step = 16 if o.dma else 1
                        ins.then_inc(prog._sems[prog._sem_key(o)], step)
            return body

        for eng, deco in (("sync", blk.sync), ("tensor", blk.tensor),
                          ("vector", blk.vector), ("scalar", blk.scalar),
                          ("gpsimd", blk.gpsimd)):
            if self.q[eng]:
                deco(run_engine(self.q[eng]))

        def fin(e):
            for k, v in counts.items():
                e.wait_ge(self._sems[k], v)
        blk.gpsimd(fin)
        stack.close()
        return nc

    def stats(self):
        return {e: len(v) for e, v in self.q.items()}


class Ctx:
    def __init__(self, nc, stack):
        self.nc = nc
        self.P = Prog(nc)
        self.stack = stack
        self.pb = [stack.enter_context(nc.psum_tensor(f"pb{i}", [128, 512], F32))
                   for i in range(8)]
        self.ident = stack.enter_context(nc.sbuf_tensor("ident", [128, 128], F32))
        self.ones1 = stack.enter_context(nc.sbuf_tensor("ones1", [1, 128], F32))
        self.onesc = stack.enter_context(nc.sbuf_tensor("onesc", [128, 1], F32))
        self.ln_x = stack.enter_context(nc.sbuf_tensor("ln_x", [128, 4 * TW], F32))
        self.ln_sq = stack.enter_context(nc.sbuf_tensor("ln_sq", [128, 2 * TW], F32))
        self.ln_row = stack.enter_context(nc.sbuf_tensor("ln_row", [1, 4 * T_LOC], F32))
        self.ln_rst = stack.enter_context(nc.sbuf_tensor("ln_rst", [128, T_LOC], F32))
        self.ln_mr = stack.enter_context(nc.sbuf_tensor("ln_mr", [128, T_LOC], F32))
        self.ln_y = stack.enter_context(nc.sbuf_tensor("ln_y", [128, 4 * TW], F32))
        self.consts = stack.enter_context(nc.sbuf_tensor("consts", [128, 16], F32))
        self.ln_bf = stack.enter_context(nc.sbuf_tensor("ln_bf", [128, 4 * TW], BF16))
        self.sbuf_used = 0

    def sb(self, name, shape, dtype):
        return self.stack.enter_context(self.nc.sbuf_tensor(name, shape, dtype))

    def cst(self, j):
        return self.consts[:, j:j + 1]

    def cstr(self, j):
        return self.consts[0:1, j:j + 1]

    def load_consts(self, ident_dram, consts_dram):
        P = self.P
        P.dma("sync", lambda e: e.dma_start(out=self.ident[:, :],
                                            in_=ident_dram[:, :]),
              sem="ident", writes=["ident"], tag="ld_ident")
        P.dma("sync", lambda e: e.dma_start(out=self.consts[:, :],
                                            in_=consts_dram[:, :]),
              sem="consts", writes=["consts"], tag="ld_consts")
        P.op("vector", lambda e: e.memset(self.ones1[:, :], 1.0),
             writes=["ones1"], tag="ones1")
        P.op("vector", lambda e: e.memset(self.onesc[:, :], 1.0),
             writes=["onesc"], tag="onesc")



class LNStats:

    def __init__(self, ctx, tag, t_tiles=2, tw=TW):
        self.ctx, self.tag = ctx, tag
        self.t_tiles, self.tw = t_tiles, tw
        self.T = t_tiles * tw
        self.n = 0

    def add(self, src, keys):
        ctx, P, tag, tw = self.ctx, self.ctx.P, self.tag, self.tw
        idx = self.n
        self.n += 1
        t, k = divmod(idx, KC)
        pbS, pbQ = ctx.pb[5], ctx.pb[6]
        pb_base = 32 * t
        sqk = ctx.ln_sq[:, (idx % 2) * tw:(idx % 2 + 1) * tw]
        P.op("scalar",
             (lambda src=src, sqk=sqk: lambda e: e.activation(
                 sqk, src, AF.Square))(),
             reads=keys, writes=["lnsq.%d" % (idx % 2)], tag=f"{tag}.sq{idx}")
        P.op("tensor",
             (lambda src=src, k=k, b=pb_base: lambda e: e.matmul(
                 pbS[b:b + 1, 0:tw], ctx.onesc[:, :], src,
                 start=(k == 0), stop=(k == KC - 1)))(),
             reads=list(keys) + ["onesc"], writes=["pb5"], tag=f"{tag}.mmS{idx}")
        P.op("tensor",
             (lambda sqk=sqk, k=k, b=pb_base: lambda e: e.matmul(
                 pbQ[b:b + 1, 0:tw], ctx.onesc[:, :], sqk,
                 start=(k == 0), stop=(k == KC - 1)))(),
             reads=["lnsq.%d" % (idx % 2), "onesc"], writes=["pb6"],
             tag=f"{tag}.mmQ{idx}")

    def finish(self):
        assert self.n == self.t_tiles * KC
        ctx, P, tag, tw, T = self.ctx, self.ctx.P, self.tag, self.tw, self.T
        pbS, pbQ, pbB = ctx.pb[5], ctx.pb[6], ctx.pb[7]
        row = ctx.ln_row
        m_r, v_r = row[:, 0:T], row[:, T:2 * T]
        rstd_r, mr_r = row[:, 2 * T:3 * T], row[:, 3 * T:4 * T]
        for t in range(self.t_tiles):
            b = 32 * t
            P.op("vector",
                 (lambda t=t, b=b: lambda e: e.tensor_scalar(
                     m_r[:, t * tw:(t + 1) * tw], pbS[b:b + 1, 0:tw], 1.0 / D,
                     None, ALU.mult))(),
                 reads=["pb5"], writes=["lnrow.m"], tag=f"{tag}.m{t}")
        P.op("vector", lambda e: e.tensor_mul(v_r, m_r, m_r),
             reads=["lnrow.m"], writes=["lnrow.v"], tag=f"{tag}.msq")
        for t in range(self.t_tiles):
            b = 32 * t
            P.op("vector",
                 (lambda t=t, b=b: lambda e: e.scalar_tensor_tensor(
                     v_r[:, t * tw:(t + 1) * tw], pbQ[b:b + 1, 0:tw], 1.0 / D,
                     v_r[:, t * tw:(t + 1) * tw], ALU.mult, ALU.subtract))(),
                 reads=["pb6", "lnrow.v"], writes=["lnrow.v"],
                 tag=f"{tag}.var{t}")
        P.op("scalar", lambda e: e.activation(rstd_r, v_r, AF.Ln,
                                              bias=ctx.cstr(C_EPS)),
             reads=["lnrow.v", "consts"], writes=["lnrow.rstd"],
             tag=f"{tag}.lnv")
        P.op("scalar", lambda e: e.activation(rstd_r, rstd_r, AF.Exp,
                                              scale=ctx.cstr(C_NEGHALF)),
             reads=["lnrow.rstd", "consts"], writes=["lnrow.rstd"],
             tag="lnrow.rstd")
        P.op("vector", lambda e: e.tensor_mul(mr_r, m_r, rstd_r),
             reads=["lnrow.m", "lnrow.rstd"], writes=["lnrow.mr"],
             tag="lnrow.mr")
        for t in range(self.t_tiles):
            P.op("tensor",
                 (lambda t=t: lambda e: e.matmul(
                     pbB[:, 0:tw], ctx.ones1[:, :],
                     rstd_r[:, t * tw:(t + 1) * tw], start=True, stop=True))(),
                 reads=["lnrow.rstd", "ones1"], writes=["pb5"],
                 tag=f"{tag}.bcR{t}")
            P.op("scalar",
                 (lambda t=t: lambda e: e.copy(
                     self.ctx.ln_rst[:, t * tw:(t + 1) * tw], pbB[:, 0:tw]))(),
                 reads=["pb5"], writes=[f"lnrst.{t}"], tag=f"{tag}.evR{t}")
            P.op("tensor",
                 (lambda t=t: lambda e: e.matmul(
                     pbB[:, 0:tw], ctx.ones1[:, :],
                     mr_r[:, t * tw:(t + 1) * tw], start=True, stop=True))(),
                 reads=["lnrow.mr", "ones1"], writes=["pb5"],
                 tag=f"{tag}.bcM{t}")
            P.op("scalar",
                 (lambda t=t: lambda e: e.copy(
                     self.ctx.ln_mr[:, t * tw:(t + 1) * tw], pbB[:, 0:tw]))(),
                 reads=["pb5"], writes=[f"lnmr.{t}"], tag=f"{tag}.evM{t}")


def emit_ln_x_to_xn(ctx, x_dram, xn_sb, xkey, tag, xdram_keys=None):
    P = ctx.P
    T = T_LOC
    xk = xdram_keys or (lambda k: [])
    st = LNStats(ctx, tag)
    for t in range(2):
        for k in range(KC):
            r = k % 2
            P.dma("sync",
                  (lambda k=k, r=r, t=t: lambda e: e.dma_start(
                      out=ctx.ln_x[:, r * TW:(r + 1) * TW],
                      in_=x_dram[k * 128:(k + 1) * 128, t * TW:(t + 1) * TW]))(),
                  sem=f"lnx.{r}", reads=xk(k), writes=[f"lnx.{r}"],
                  tag=f"{tag}.ld1.{k}.{t}")
            st.add(ctx.ln_x[:, r * TW:(r + 1) * TW], [f"lnx.{r}"])
    st.finish()
    for k in range(KC):
        for t in range(2):
            r = (2 * k + t) % 4
            src = ctx.ln_x[:, r * TW:(r + 1) * TW]
            col = k * T + t * TW
            P.dma("sync",
                  (lambda k=k, t=t, src=src: lambda e: e.dma_start(
                      out=src,
                      in_=x_dram[k * 128:(k + 1) * 128, t * TW:(t + 1) * TW]))(),
                  sem=f"lnx.{r}", reads=xk(k), writes=[f"lnx.{r}"],
                  tag=f"{tag}.ld2.{k}.{t}")
            P.op("vector",
                 (lambda src=src, t=t: lambda e: e.tensor_mul(
                     src, src, ctx.ln_rst[:, t * TW:(t + 1) * TW]))(),
                 reads=[f"lnx.{r}", f"lnrst.{t}"],
                 writes=[f"lnx.{r}"], tag=f"{tag}.ap1.{k}.{t}")
            P.op("vector",
                 (lambda src=src, col=col, t=t: lambda e: e.tensor_sub(
                     xn_sb[:, col:col + TW], src,
                     ctx.ln_mr[:, t * TW:(t + 1) * TW]))(),
                 reads=[f"lnx.{r}", f"lnmr.{t}"],
                 writes=[f"{xkey}.{k}"], tag=f"{tag}.ap2.{k}.{t}")


def emit_apply_residual(ctx, y_dram, ykeys, x_dram, xout_dram, tag,
                        xout_key=None, xdram_keys=None):
    P = ctx.P
    xk = xdram_keys or (lambda k: [])
    for k in range(KC):
        for t in range(2):
            r = (2 * k + t) % 4
            yv = ctx.ln_x[:, r * TW:(r + 1) * TW]
            xv = ctx.ln_y[:, r * TW:(r + 1) * TW]
            P.dma("sync",
                  (lambda k=k, t=t, yv=yv: lambda e: e.dma_start(
                      out=yv, in_=y_dram[k * 128:(k + 1) * 128,
                                         t * TW:(t + 1) * TW]))(),
                  sem=f"lnx.{r}", reads=ykeys(k, t), writes=[f"lnx.{r}"],
                  tag=f"{tag}.ldy.{k}.{t}")
            P.dma("sync",
                  (lambda k=k, t=t, xv=xv: lambda e: e.dma_start(
                      out=xv, in_=x_dram[k * 128:(k + 1) * 128,
                                         t * TW:(t + 1) * TW]))(),
                  sem=f"lny.{r}", reads=xk(k), writes=[f"lny.{r}"],
                  tag=f"{tag}.ldx.{k}.{t}")
            P.op("vector",
                 (lambda yv=yv, t=t: lambda e: e.tensor_mul(
                     yv, yv, ctx.ln_rst[:, t * TW:(t + 1) * TW]))(),
                 reads=[f"lnx.{r}", f"lnrst.{t}"],
                 writes=[f"lnx.{r}"], tag=f"{tag}.a1.{k}.{t}")
            P.op("vector",
                 (lambda yv=yv, t=t: lambda e: e.tensor_sub(
                     yv, yv, ctx.ln_mr[:, t * TW:(t + 1) * TW]))(),
                 reads=[f"lnx.{r}", f"lnmr.{t}"],
                 writes=[f"lnx.{r}"], tag=f"{tag}.a2.{k}.{t}")
            P.op("vector",
                 (lambda yv=yv, xv=xv: lambda e: e.tensor_add(xv, yv, xv))(),
                 reads=[f"lnx.{r}", f"lny.{r}"],
                 writes=[f"lny.{r}"], tag=f"{tag}.a3.{k}.{t}")
            P.dma("gpsimd",
                  (lambda k=k, t=t, xv=xv: lambda e: e.dma_start(
                      out=xout_dram[k * 128:(k + 1) * 128, t * TW:(t + 1) * TW],
                      in_=xv))(),
                  sem=f"{tag}.xost", reads=[f"lny.{r}"],
                  writes=[f"{tag}.xog"] + ([f"{xout_key}.{k}"] if xout_key else []),
                  tag=f"{tag}.st{k}.{t}")



IN0 = 5632
IC0 = IN0 // 128


def emit_l0_ffn(ctx, x_dram, xout_dram, w, y_dram, uid="f0", xkeys=None,
                xout_key=None):
    nc, P = ctx.nc, ctx.P
    u = uid
    xn = ctx.sb(f"{u}_xn", [128, KC * T_LOC], BF16)
    h_all = ctx.sb(f"{u}_h", [128, IC0 * T_LOC], BF16)
    ctx._dbg_xn, ctx._dbg_h = xn, h_all
    yev = ctx.sb(f"{u}_yev", [128, 2 * TW], F32)
    wg_s = ctx.sb(f"{u}_wg", [128, 2 * KC * 128], BF16)
    wu_s = ctx.sb(f"{u}_wu", [128, 2 * KC * 128], BF16)
    wd_s = ctx.sb(f"{u}_wd", [128, IC0 * 128], BF16)
    gs = ctx.sb(f"{u}_gs", [128, IC0], F32)
    us = ctx.sb(f"{u}_us", [128, IC0], F32)
    ds = ctx.sb(f"{u}_ds", [128, KC], F32)
    silu_t = ctx.sb(f"{u}_st", [128, 2 * TW], BF16)

    for nm, t in ((f"{u}_gate_s", gs), (f"{u}_up_s", us), (f"{u}_down_s", ds)):
        P.dma("sync",
              (lambda nm=nm, t=t: lambda e: e.dma_start(out=t[:, :],
                                                        in_=w[nm][:, :]))(),
              sem=f"{u}.scl.{nm}", writes=[f"{u}.{nm}"], tag=f"{u}.{nm}")

    emit_ln_x_to_xn(ctx, x_dram, xn, f"{u}.xn", f"{u}ln", xdram_keys=xkeys)

    st = LNStats(ctx, f"{u}pl")
    for i in range(IC0):
        buf = i % 2
        P.dma("gpsimd",
              (lambda i=i, buf=buf: lambda e: e.dma_start(
                  out=wg_s[:, buf * KC * 128:(buf + 1) * KC * 128],
                  in_=w[f"{u}_gate_wP"][i, :, :]))(),
              sem=f"{u}.wgb.{buf}", writes=[f"{u}.wgb.{buf}"],
              tag=f"{u}.ldg{i}")
        P.dma("gpsimd",
              (lambda i=i, buf=buf: lambda e: e.dma_start(
                  out=wu_s[:, buf * KC * 128:(buf + 1) * KC * 128],
                  in_=w[f"{u}_up_wP"][i, :, :]))(),
              sem=f"{u}.wub.{buf}", writes=[f"{u}.wub.{buf}"],
              tag=f"{u}.ldu{i}")
        for t in range(2):
            tcol = t * TW
            pg, pu = ctx.pb[0 + t], ctx.pb[2 + t]
            for k in range(KC):
                P.op("tensor",
                     (lambda k=k, buf=buf, pg=pg, tcol=tcol: lambda e:
                      e.matmul(pg[:, 0:TW],
                               wg_s[:, (buf * KC + k) * 128:(buf * KC + k + 1) * 128],
                               xn[:, k * T_LOC + tcol:k * T_LOC + tcol + TW],
                               start=(k == 0), stop=(k == KC - 1)))(),
                     reads=[f"{u}.wgb.{buf}", f"{u}.xn.{k}"],
                     writes=[f"pb{0 + t}"], tag=f"{u}.g{i}.{k}.{t}")
                P.op("tensor",
                     (lambda k=k, buf=buf, pu=pu, tcol=tcol: lambda e:
                      e.matmul(pu[:, 0:TW],
                               wu_s[:, (buf * KC + k) * 128:(buf * KC + k + 1) * 128],
                               xn[:, k * T_LOC + tcol:k * T_LOC + tcol + TW],
                               start=(k == 0), stop=(k == KC - 1)))(),
                     reads=[f"{u}.wub.{buf}", f"{u}.xn.{k}"],
                     writes=[f"pb{2 + t}"], tag=f"{u}.u{i}.{k}.{t}")
            sti = silu_t[:, t * TW:(t + 1) * TW]
            P.op("scalar",
                 (lambda sti=sti, pg=pg, i=i: lambda e: e.activation(
                     sti, pg[:, 0:TW], AF.Silu, scale=gs[:, i:i + 1]))(),
                 reads=[f"pb{0 + t}", f"{u}.{u}_gate_s"],
                 writes=[f"{u}.st.{t}"], tag=f"{u}.silu{i}.{t}")
            P.op("vector",
                 (lambda sti=sti, pu=pu, i=i, tcol=tcol: lambda e:
                  e.scalar_tensor_tensor(
                     h_all[:, i * T_LOC + tcol:i * T_LOC + tcol + TW],
                     pu[:, 0:TW], us[:, i:i + 1], sti, ALU.mult, ALU.mult))(),
                 reads=[f"pb{2 + t}", f"{u}.st.{t}", f"{u}.{u}_up_s"],
                 writes=[f"{u}.h.{i}.{t}"], tag=f"{u}.h{i}.{t}")
    for o in range(KC):
        P.dma("gpsimd",
              (lambda o=o: lambda e: e.dma_start(
                  out=wd_s[:, 0:IC0 * 128],
                  in_=w[f"{u}_down_wP"][o, :, :]))(),
              sem=f"{u}.wdb", writes=[f"{u}.wdb"], tag=f"{u}.lddw{o}")
        for t in range(2):
            tcol = t * TW
            po = ctx.pb[4 if t == 0 else 6]
            for i in range(IC0):
                P.op("tensor",
                     (lambda i=i, po=po, tcol=tcol: lambda e: e.matmul(
                         po[:, 0:TW],
                         wd_s[:, i * 128:(i + 1) * 128],
                         h_all[:, i * T_LOC + tcol:i * T_LOC + tcol + TW],
                         start=(i == 0), stop=(i == IC0 - 1)))(),
                     reads=[f"{u}.wdb", f"{u}.h.{i}.{t}"],
                     writes=[f"pb{4 if t == 0 else 6}"],
                     tag=f"{u}.d{o}.{i}.{t}")
            r = (2 * o + t) % 2
            yv = yev[:, r * TW:(r + 1) * TW]
            P.op("scalar",
                 (lambda o=o, po=po, yv=yv: lambda e: e.activation(
                     yv, po[:, 0:TW], AF.Identity, scale=ds[:, o:o + 1]))(),
                 reads=[f"pb{4 if t == 0 else 6}", f"{u}.{u}_down_s"],
                 writes=[f"{u}.ye.{r}"], tag=f"{u}.ev{o}.{t}")
            P.dma("gpsimd",
                  (lambda o=o, tcol=tcol, yv=yv: lambda e: e.dma_start(
                      out=y_dram[o * 128:(o + 1) * 128, tcol:tcol + TW],
                      in_=yv))(),
                  sem=f"{u}.yst", reads=[f"{u}.ye.{r}"],
                  writes=[f"{u}.ydrg"], tag=f"{u}.sty{o}.{t}")
    for t in range(2):
        for k in range(KC):
            r = k % 2
            P.dma("sync",
                  (lambda k=k, r=r, t=t: lambda e: e.dma_start(
                      out=ctx.ln_x[:, r * TW:(r + 1) * TW],
                      in_=y_dram[k * 128:(k + 1) * 128, t * TW:(t + 1) * TW]))(),
                  sem=f"lnx.{r}", reads=[f"{u}.ydrg"], writes=[f"lnx.{r}"],
                  tag=f"{u}pl.ldst.{k}.{t}")
            st.add(ctx.ln_x[:, r * TW:(r + 1) * TW], [f"lnx.{r}"])
    st.finish()
    emit_apply_residual(ctx, y_dram, lambda k, t: [f"{u}.ydrg"], x_dram,
                        xout_dram, f"{u}pl", xout_key=xout_key,
                        xdram_keys=xkeys)


NCH = 32


def emit_ln_to_dram(ctx, x_dram, xn_dram, tag, xkeys=None, out_group=None):
    P = ctx.P
    xk = xkeys or (lambda k: [])
    st = LNStats(ctx, tag)
    for t in range(2):
        for k in range(KC):
            r = k % 2
            P.dma("sync",
                  (lambda k=k, r=r, t=t: lambda e: e.dma_start(
                      out=ctx.ln_x[:, r * TW:(r + 1) * TW],
                      in_=x_dram[k * 128:(k + 1) * 128, t * TW:(t + 1) * TW]))(),
                  sem=f"lnx.{r}", reads=xk(k), writes=[f"lnx.{r}"],
                  tag=f"{tag}.ld1.{k}.{t}")
            st.add(ctx.ln_x[:, r * TW:(r + 1) * TW], [f"lnx.{r}"])
    st.finish()
    for k in range(KC):
        for t in range(2):
            r = (2 * k + t) % 4
            src_ = ctx.ln_x[:, r * TW:(r + 1) * TW]
            dst = ctx.ln_y[:, (r % 2) * TW + 2 * TW:(r % 2) * TW + 2 * TW]
            P.dma("sync",
                  (lambda k=k, t=t, src_=src_: lambda e: e.dma_start(
                      out=src_,
                      in_=x_dram[k * 128:(k + 1) * 128, t * TW:(t + 1) * TW]))(),
                  sem=f"lnx.{r}", reads=xk(k), writes=[f"lnx.{r}"],
                  tag=f"{tag}.ld2.{k}.{t}")
            P.op("vector",
                 (lambda src_=src_, t=t: lambda e: e.tensor_mul(
                     src_, src_, ctx.ln_rst[:, t * TW:(t + 1) * TW]))(),
                 reads=[f"lnx.{r}", f"lnrst.{t}"], writes=[f"lnx.{r}"],
                 tag=f"{tag}.ap1.{k}.{t}")
            bdst = ctx.ln_bf[:, (r % 4) * TW:(r % 4 + 1) * TW]
            P.op("vector",
                 (lambda src_=src_, bdst=bdst, t=t: lambda e: e.tensor_sub(
                     bdst, src_, ctx.ln_mr[:, t * TW:(t + 1) * TW]))(),
                 reads=[f"lnx.{r}", f"lnmr.{t}"], writes=[f"lnbf.{r % 4}"],
                 tag=f"{tag}.ap2.{k}.{t}")
            P.dma("gpsimd",
                  (lambda k=k, t=t, bdst=bdst: lambda e: e.dma_start(
                      out=xn_dram[k * 128:(k + 1) * 128, t * TW:(t + 1) * TW],
                      in_=bdst))(),
                  sem=f"{tag}.xnst", reads=[f"lnbf.{r % 4}"],
                  writes=[f"{tag}.xng"] + ([out_group] if out_group else []),
                  tag=f"{tag}.st{k}.{t}")


def emit_l0_attn(ctx, x_dram, xout_dram, w, scratch, uid="a0", xkeys=None,
                 xout_key=None):
    nc, P = ctx.nc, ctx.P
    u = uid
    xng_loc, xng_gath = scratch["xng_loc"], scratch["xng_gath"]
    wop_loc, wor_out = scratch["wop_loc"], scratch["wor_out"]

    og = ctx.sb(f"{u}_og", [128, 8 * S_FULL], BF16)
    ctx._a0_og = og
    xs = ctx.sb(f"{u}_xs", [128, KC * T_LOC], BF16)
    xlow = ctx.sb(f"{u}_xlow", [16, S_FULL], BF16)
    qf = ctx.sb(f"{u}_qf", [128, S_FULL], BF16)
    kf = ctx.sb(f"{u}_kf", [128, S_FULL], BF16)
    gf = ctx.sb(f"{u}_gf", [128, S_FULL], BF16)
    vtok = ctx.sb(f"{u}_vtok", [64, NCH * 128], BF16)
    zb = ctx.sb(f"{u}_zb", [128, S_FULL], F32)
    of_b = zb
    t0b = ctx.sb(f"{u}_t0b", [128, S_FULL], F32)
    eb = t0b
    gb = ctx.sb(f"{u}_gb", [128, S_FULL], F32)
    dqb = ctx.sb(f"{u}_dqb", [128, S_FULL], F32)
    qh_b = ctx.sb(f"{u}_qhb", [128, S_FULL], BF16)
    kh_b = ctx.sb(f"{u}_khb", [128, S_FULL], BF16)
    qg_b = ctx.sb(f"{u}_qgb", [128, S_FULL], BF16)
    dec = ctx.sb(f"{u}_dec", [128, 2 * NCH], F32)
    S_sb = ctx.sb(f"{u}_S", [128, 128], F32)
    S_bf = ctx.sb(f"{u}_Sbf", [128, 128], BF16)
    atb = ctx.sb(f"{u}_atb", [64, 2 * CH], BF16)
    kst = ctx.sb(f"{u}_kst", [64, 2 * 128], BF16)
    sqs = ctx.sb(f"{u}_sqs", [128, TW], F32)
    rrow = ctx.sb(f"{u}_rrow", [1, TW], F32)
    rb = ctx.sb(f"{u}_rb", [128, TW], F32)
    silb = ctx.sb(f"{u}_silb", [128, TW], BF16)
    wqkg = ctx.sb(f"{u}_wqkg", [128, 2 * 3 * 128], BF16)
    wvs = ctx.ln_bf
    wg1s = ctx.sb(f"{u}_wg1s", [128, 256], BF16)
    wg2s = ctx.sb(f"{u}_wg2s", [16, 8 * 128], BF16)
    bg2s = ctx.sb(f"{u}_bg2s", [128, 8], F32)
    nrmw = ctx.sb(f"{u}_nrmw", [128, 1], F32)
    trilT = ctx.sb(f"{u}s_trilT", [64, CH], F32)
    wos = ctx.sb(f"{u}_wos", [128, 2 * TW], BF16)
    yst = ctx.sb(f"{u}_yst", [128, 4 * TW], F32)
    ksf = dqb
    ctx._a0_kst = kst
    ctx._a0_d = dict(qf=qf, kf=kf, gf=gf, gb=gb, qh_b=qh_b, kh_b=kh_b,
                     qg_b=qg_b, ksf=ksf, of_b=of_b, S_sb=S_sb, vtok=vtok,
                     dec=dec)

    P.dma("sync", lambda e: e.dma_start(out=trilT[:, :], in_=w[f"{u}_trilT"][:, :]),
          sem=f"{u}.trilT", writes=[f"{u}.trilT"], tag=f"{u}.ldtril")
    P.dma("gpsimd", lambda e: e.dma_start(out=wg1s[:, :], in_=w[f"{u}_wg1P"][0, :, :]),
          sem=f"{u}.wg1", writes=[f"{u}.wg1"], tag=f"{u}.ldwg1")
    P.dma("gpsimd", lambda e: e.dma_start(
        out=wg2s[:, :].rearrange("p (h m) -> p h m", m=128),
        in_=w[f"{u}_wg2"].rearrange("h p m -> p h m")),
        sem=f"{u}.wg2", writes=[f"{u}.wg2"], tag=f"{u}.ldwg2")
    P.dma("sync", lambda e: e.dma_start(out=bg2s[:, :], in_=w[f"{u}_bg2"][:, :]),
          sem=f"{u}.bg2", writes=[f"{u}.bg2"], tag=f"{u}.ldbg2")
    P.dma("sync", lambda e: e.dma_start(out=nrmw[:, :], in_=w[f"{u}_norm_w"][:, :]),
          sem=f"{u}.nrmw", writes=[f"{u}.nrmw"], tag=f"{u}.ldnrmw")

    emit_ln_to_dram(ctx, x_dram, xng_loc, f"{u}ln", xkeys=xkeys)
    P.op("gpsimd",
         lambda e: e.collective_compute(
             "AllGather", ALU.bypass,
             replica_groups=[[0, 1], [2, 3], [4, 5], [6, 7]],
             ins=[xng_loc.opt()], outs=[xng_gath.opt()]),
         reads=[f"{u}ln.xng"], writes=[f"{u}.xg"], coll=True, tag=f"{u}.ccxn")

    for k in range(KC):
        for half in range(2):
            s = (2 * k + half) % 4
            P.dma("sync",
                  (lambda k=k, s=s, half=half: lambda e: e.dma_start(
                      out=xs[:, s * T_LOC:(s + 1) * T_LOC],
                      in_=xng_gath[half * D + k * 128:half * D + (k + 1) * 128,
                                   :]))(),
                  sem=f"{u}.xs.{s}", reads=[f"{u}.xg"], writes=[f"{u}.xs.{s}"],
                  tag=f"{u}.xl.ldx{k}.{half}")
            for tc in range(2):
                P.op("tensor",
                     (lambda k=k, s=s, half=half, tc=tc: lambda e: e.matmul(
                         ctx.pb[half * 2 + tc][0:16, 0:TW],
                         wg1s[:, k * 16:(k + 1) * 16],
                         xs[:, s * T_LOC + tc * TW:s * T_LOC + (tc + 1) * TW],
                         start=(k == 0), stop=(k == KC - 1)))(),
                     reads=[f"{u}.wg1", f"{u}.xs.{s}"],
                     writes=[f"pb{half * 2 + tc}"], tag=f"{u}.xl{k}.{half}.{tc}")
    for g4 in range(4):
        P.op("scalar",
             (lambda g4=g4: lambda e: e.copy(
                 xlow[:, g4 * TW:(g4 + 1) * TW], ctx.pb[g4][0:16, 0:TW]))(),
             reads=[f"pb{g4}"], writes=[f"{u}.xlow"], tag=f"{u}.xlev{g4}")

    for g in range(8):
        for th in range(2):
            tc0 = th * T_LOC
            for k in range(KC):
                P.dma("sync",
                      (lambda k=k, th=th: lambda e: e.dma_start(
                          out=xs[:, k * T_LOC:(k + 1) * T_LOC],
                          in_=xng_gath[th * D + k * 128:th * D + (k + 1) * 128,
                                       :]))(),
                      sem=f"{u}.xs.{k}", reads=[f"{u}.xg"],
                      writes=[f"{u}.xs.{k}"], tag=f"{u}.pj{g}.{th}.ldx{k}")
                buf = k % 2
                P.dma("gpsimd",
                      (lambda g=g, k=k, buf=buf: lambda e: e.dma_start(
                          out=wqkg[:, buf * 384:buf * 384 + 384],
                          in_=w[f"{u}_wqkgP"][g, k, :, :]))(),
                      sem=f"{u}.wqkg.{buf}", writes=[f"{u}.wqkg.{buf}"],
                      tag=f"{u}.ldw{g}.{th}.{k}")
                for tc in range(2):
                    for j, nmq in ((0, "q"), (1, "k"), (2, "g")):
                        P.op("tensor",
                             (lambda k=k, tc=tc, j=j, buf=buf: lambda e:
                              e.matmul(
                                 ctx.pb[j * 2 + tc][:, 0:TW],
                                 wqkg[:, buf * 384 + j * 128:buf * 384 + (j + 1) * 128],
                                 xs[:, k * T_LOC + tc * TW:k * T_LOC + (tc + 1) * TW],
                                 start=(k == 0), stop=(k == KC - 1)))(),
                             reads=[f"{u}.wqkg.{buf}", f"{u}.xs.{k}"],
                             writes=[f"pb{j * 2 + tc}"],
                             tag=f"{u}.p{nmq}{g}.{th}.{k}.{tc}")
            for tc in range(2):
                col = tc0 + tc * TW
                for j, (dst, key) in enumerate(
                        ((qf, "qf"), (kf, "kf"), (gf, "gf"))):
                    P.op("scalar",
                         (lambda j=j, tc=tc, dst=dst, col=col: lambda e:
                          e.copy(dst[:, col:col + TW], ctx.pb[j * 2 + tc][:, 0:TW]))(),
                         reads=[f"pb{j * 2 + tc}"], writes=[f"{u}.{key}"],
                         tag=f"{u}.ev{key}{g}.{th}.{tc}")
            if th == 0:
                P.dma("gpsimd",
                      (lambda g=g: lambda e: e.dma_start(
                          out=wvs[:, :].rearrange("p (k m) -> p k m", m=128),
                          in_=w[f"{u}_wv"].rearrange("(k p) m -> p k m", p=128)
                          [:, :, g * 128:(g + 1) * 128]))(),
                      sem=f"{u}.wvs", writes=[f"{u}.wvs", "lnbf.0", "lnbf.1",
                                              "lnbf.2", "lnbf.3"],
                      tag=f"{u}.ldwv{g}")
            for vt in range(8):
                pbk = 6 + (vt % 2)
                for k in range(KC):
                    P.op("tensor",
                         (lambda k=k, vt=vt, pbk=pbk: lambda e:
                          e.matmul(
                             ctx.pb[pbk][:, 0:128],
                             xs[:, k * T_LOC + vt * 128:k * T_LOC + (vt + 1) * 128],
                             wvs[:, k * 128:(k + 1) * 128],
                             start=(k == 0), stop=(k == KC - 1)))(),
                         reads=[f"{u}.wvs", f"{u}.xs.{k}"],
                         writes=[f"pb{pbk}"], tag=f"{u}.pv{g}.{th}.{vt}.{k}")
                gtile = th * 8 + vt
                for hh2 in range(2):
                    cch = gtile * 2 + hh2
                    P.op("scalar",
                         (lambda hh2=hh2, cch=cch, pbk=pbk: lambda e: e.copy(
                             vtok[:, cch * 128:(cch + 1) * 128],
                             ctx.pb[pbk][hh2 * 64:(hh2 + 1) * 64, 0:128]))(),
                         reads=[f"pb{pbk}"], writes=[f"{u}.vtok"],
                         tag=f"{u}.evv{g}.{th}.{vt}.{hh2}")

        for tc in range(4):
            P.op("tensor",
                 (lambda g=g, tc=tc: lambda e: e.matmul(
                     ctx.pb[0][:, 0:TW], wg2s[:, g * 128:(g + 1) * 128],
                     xlow[:, tc * TW:(tc + 1) * TW], start=True, stop=True))(),
                 reads=[f"{u}.wg2", f"{u}.xlow"], writes=["pb0"],
                 tag=f"{u}.zmm{g}.{tc}")
            P.op("scalar",
                 (lambda g=g, tc=tc: lambda e: e.activation(
                     zb[:, tc * TW:(tc + 1) * TW], ctx.pb[0][:, 0:TW],
                     AF.Identity, bias=bg2s[:, g:g + 1]))(),
                 reads=["pb0", f"{u}.bg2"], writes=[f"{u}.zb", f"{u}.ofb"],
                 tag=f"{u}.zev{g}.{tc}")
        P.op("scalar", lambda e: e.activation(t0b[:, :], zb[:, :], AF.Abs),
             reads=[f"{u}.zb"], writes=[f"{u}.t0b"], tag=f"{u}.abs{g}")
        P.op("scalar", lambda e: e.activation(t0b[:, :], t0b[:, :], AF.Exp,
                                              scale=ctx.cst(C_NEG1)),
             reads=[f"{u}.t0b", "consts"], writes=[f"{u}.t0b"], tag=f"{u}.ex{g}")
        P.op("scalar", lambda e: e.activation(t0b[:, :], t0b[:, :], AF.Ln,
                                              bias=ctx.cst(C_ONE)),
             reads=[f"{u}.t0b", "consts"], writes=[f"{u}.t0b"], tag=f"{u}.ln{g}")
        P.op("vector", lambda e: e.tensor_scalar(zb[:, :], zb[:, :], 0.0, None,
                                                 ALU.min),
             reads=[f"{u}.zb"], writes=[f"{u}.zb"], tag=f"{u}.min{g}")
        P.op("vector", lambda e: e.tensor_sub(zb[:, :], zb[:, :], t0b[:, :]),
             reads=[f"{u}.zb", f"{u}.t0b"], writes=[f"{u}.zb"], tag=f"{u}.d{g}")
        P.op("vector", lambda e: e.tensor_tensor_scan(gb[:, :], zb[:, :],
                                                      zb[:, :], 0.0, ALU.add,
                                                      ALU.bypass),
             reads=[f"{u}.zb"], writes=[f"{u}.gb"], tag=f"{u}.scan{g}")
        for c in range(NCH - 1, 0, -1):
            P.op("vector",
                 (lambda c=c: lambda e: e.tensor_scalar(
                     gb[:, c * CH:(c + 1) * CH], gb[:, c * CH:(c + 1) * CH],
                     gb[:, c * CH - 1:c * CH], None, ALU.subtract))(),
                 reads=[f"{u}.gb"], writes=[f"{u}.gb"], tag=f"{u}.gsub{g}.{c}")
        P.op("vector",
             (lambda g=g: lambda e: e.tensor_copy(
                 dec[:, (g % 2) * NCH:(g % 2 + 1) * NCH],
                 gb[:, CH - 1::CH]))(),
             reads=[f"{u}.gb"], writes=[f"{u}.dec.{g % 2}"], tag=f"{u}.dcp{g}")
        P.op("scalar",
             (lambda g=g: lambda e: e.activation(
                 dec[:, (g % 2) * NCH:(g % 2 + 1) * NCH],
                 dec[:, (g % 2) * NCH:(g % 2 + 1) * NCH], AF.Exp,
                 scale=ctx.cst(C_G16)))(),
             reads=[f"{u}.dec.{g % 2}", "consts"], writes=[f"{u}.dec.{g % 2}"],
             tag=f"{u}.dex{g}")
        for c in range(NCH):
            P.op("vector",
                 (lambda c=c: lambda e: e.tensor_scalar(
                     dqb[:, c * CH:(c + 1) * CH], gb[:, c * CH:(c + 1) * CH],
                     gb[:, c * CH:c * CH + 1], None, ALU.subtract))(),
                 reads=[f"{u}.gb"], writes=[f"{u}.dqb"], tag=f"{u}.dq{g}.{c}")
        P.op("scalar", lambda e: e.activation(eb[:, :], dqb[:, :], AF.Exp,
                                              scale=ctx.cst(C_G16)),
             reads=[f"{u}.dqb", "consts"], writes=[f"{u}.t0b"], tag=f"{u}.e1{g}")
        P.op("vector", lambda e: e.scalar_tensor_tensor(qh_b[:, :], qf[:, :],
                                                        1.0 / np.sqrt(128.0),
                                                        eb[:, :], ALU.mult,
                                                        ALU.mult),
             reads=[f"{u}.qf", f"{u}.t0b"], writes=[f"{u}.qhb"], tag=f"{u}.qh{g}")
        P.op("scalar", lambda e: e.activation(eb[:, :], dqb[:, :], AF.Exp,
                                              scale=ctx.cst(C_NEGG16)),
             reads=[f"{u}.dqb", "consts"], writes=[f"{u}.t0b"], tag=f"{u}.e2{g}")
        P.op("vector", lambda e: e.tensor_mul(kh_b[:, :], kf[:, :], eb[:, :]),
             reads=[f"{u}.kf", f"{u}.t0b"], writes=[f"{u}.khb"], tag=f"{u}.kh{g}")
        P.op("scalar", lambda e: e.activation(eb[:, :], gb[:, :], AF.Exp,
                                              scale=ctx.cst(C_G16)),
             reads=[f"{u}.gb", "consts"], writes=[f"{u}.t0b"], tag=f"{u}.e3{g}")
        P.op("vector", lambda e: e.scalar_tensor_tensor(qg_b[:, :], qf[:, :],
                                                        1.0 / np.sqrt(128.0),
                                                        eb[:, :], ALU.mult,
                                                        ALU.mult),
             reads=[f"{u}.qf", f"{u}.t0b"], writes=[f"{u}.qgb"], tag=f"{u}.qg{g}")
        for c in range(NCH):
            P.op("vector",
                 (lambda c=c: lambda e: e.tensor_scalar(
                     dqb[:, c * CH:(c + 1) * CH], gb[:, c * CH:(c + 1) * CH],
                     gb[:, c * CH + CH - 1:c * CH + CH], None, ALU.subtract))(),
                 reads=[f"{u}.gb"], writes=[f"{u}.dqb"], tag=f"{u}.dl{g}.{c}")
        P.op("scalar", lambda e: e.activation(eb[:, :], dqb[:, :], AF.Exp,
                                              scale=ctx.cst(C_NEGG16)),
             reads=[f"{u}.dqb", "consts"], writes=[f"{u}.t0b"], tag=f"{u}.e4{g}")
        P.op("vector", lambda e: e.tensor_mul(ksf[:, :], kf[:, :], eb[:, :]),
             reads=[f"{u}.kf", f"{u}.t0b"], writes=[f"{u}.dqb"], tag=f"{u}.ks{g}")

        P.op("vector", lambda e: e.memset(S_sb[:, :], 0.0),
             writes=[f"{u}.S"], tag=f"{u}.S0{g}")
        P.op("vector", lambda e: e.memset(S_bf[:, :], 0.0),
             writes=[f"{u}.Sbf"], tag=f"{u}.Sb0{g}")
        for c in range(NCH):
            cb = c % 2
            vsl = vtok[:, c * 128:(c + 1) * 128]
            P.op("tensor",
                 (lambda c=c: lambda e: e.matmul(
                     ctx.pb[0][0:64, 0:CH], kh_b[:, c * CH:(c + 1) * CH],
                     qh_b[:, c * CH:(c + 1) * CH], start=True, stop=True))(),
                 reads=[f"{u}.khb", f"{u}.qhb"], writes=["pb0"],
                 tag=f"{u}.at{g}.{c}")
            P.op("vector",
                 (lambda cb=cb: lambda e: e.tensor_mul(
                     atb[:, cb * CH:(cb + 1) * CH], ctx.pb[0][0:64, 0:CH],
                     trilT[:, :]))(),
                 reads=["pb0", f"{u}.trilT"], writes=[f"{u}.atb.{cb}"],
                 tag=f"{u}.atm{g}.{c}")
            ocol = (c % 8) * CH
            P.op("tensor",
                 (lambda c=c, cb=cb, ocol=ocol, vsl=vsl: lambda e: e.matmul(
                     ctx.pb[1][:, ocol:ocol + CH], vsl,
                     atb[:, cb * CH:(cb + 1) * CH], start=True, stop=False))(),
                 reads=[f"{u}.atb.{cb}", f"{u}.vtok"], writes=["pb1"],
                 tag=f"{u}.oi{g}.{c}")
            P.op("tensor",
                 (lambda c=c, ocol=ocol: lambda e: e.matmul(
                     ctx.pb[1][:, ocol:ocol + CH], S_bf[:, :],
                     qg_b[:, c * CH:(c + 1) * CH], start=False, stop=True))(),
                 reads=[f"{u}.Sbf", f"{u}.qgb"], writes=["pb1"],
                 tag=f"{u}.oe{g}.{c}")
            P.op("tensor",
                 (lambda c=c: lambda e: e.transpose(
                     ctx.pb[2][0:64, 0:128], ksf[:, c * CH:(c + 1) * CH],
                     ctx.ident[:, :]))(),
                 reads=[f"{u}.dqb", "ident"], writes=["pb2"],
                 tag=f"{u}.kt{g}.{c}")
            P.op("scalar",
                 (lambda cb=cb: lambda e: e.copy(
                     kst[:, cb * 128:(cb + 1) * 128], ctx.pb[2][0:64, 0:128]))(),
                 reads=["pb2"], writes=[f"{u}.kst.{cb}"], tag=f"{u}.ktc{g}.{c}")
            P.op("tensor",
                 (lambda cb=cb, vsl=vsl: lambda e: e.matmul(
                     ctx.pb[3][:, 0:128], kst[:, cb * 128:(cb + 1) * 128], vsl,
                     start=True, stop=True))(),
                 reads=[f"{u}.kst.{cb}", f"{u}.vtok"], writes=["pb3"],
                 tag=f"{u}.sc{g}.{c}")
            P.op("vector",
                 (lambda c=c, g=g: lambda e: e.scalar_tensor_tensor(
                     S_sb[:, :], S_sb[:, :],
                     dec[:, (g % 2) * NCH + c:(g % 2) * NCH + c + 1],
                     ctx.pb[3][:, 0:128], ALU.mult, ALU.add))(),
                 reads=[f"{u}.S", f"{u}.dec.{g % 2}", "pb3"], writes=[f"{u}.S"],
                 tag=f"{u}.su{g}.{c}")
            P.op("vector", lambda e: e.tensor_copy(S_bf[:, :], S_sb[:, :]),
                 reads=[f"{u}.S"], writes=[f"{u}.Sbf"], tag=f"{u}.sb{g}.{c}")
            if c % 8 == 7:
                blk8 = c // 8
                P.op("scalar",
                     (lambda blk8=blk8: lambda e: e.copy(
                         of_b[:, blk8 * TW:(blk8 + 1) * TW],
                         ctx.pb[1][:, 0:TW]))(),
                     reads=["pb1"], writes=[f"{u}.ofb"], tag=f"{u}.oev{g}.{blk8}")

        for tc in range(4):
            col = tc * TW
            P.op("scalar",
                 (lambda col=col: lambda e: e.activation(
                     sqs[:, :], of_b[:, col:col + TW], AF.Square))(),
                 reads=[f"{u}.ofb"], writes=[f"{u}.sqs"], tag=f"{u}.sq{g}.{tc}")
            P.op("tensor",
                 (lambda: lambda e: e.matmul(
                     ctx.pb[4][0:1, 0:TW], ctx.onesc[:, :], sqs[:, :],
                     start=True, stop=True))(),
                 reads=[f"{u}.sqs", "onesc"], writes=["pb4"],
                 tag=f"{u}.sqm{g}.{tc}")
            P.op("vector",
                 (lambda: lambda e: e.tensor_scalar(
                     rrow[:, :], ctx.pb[4][0:1, 0:TW], 1.0 / 128.0, None,
                     ALU.mult))(),
                 reads=["pb4"], writes=[f"{u}.rrow"], tag=f"{u}.ms{g}.{tc}")
            P.op("scalar",
                 (lambda: lambda e: e.activation(rrow[:, :], rrow[:, :], AF.Ln,
                                                 bias=ctx.cstr(C_EPS)))(),
                 reads=[f"{u}.rrow", "consts"], writes=[f"{u}.rrow"],
                 tag=f"{u}.msl{g}.{tc}")
            P.op("scalar",
                 (lambda: lambda e: e.activation(rrow[:, :], rrow[:, :], AF.Exp,
                                                 scale=ctx.cstr(C_NEGHALF)))(),
                 reads=[f"{u}.rrow", "consts"], writes=[f"{u}.rrow"],
                 tag=f"{u}.mse{g}.{tc}")
            P.op("tensor",
                 (lambda: lambda e: e.matmul(
                     ctx.pb[5][:, 0:TW], ctx.ones1[:, :], rrow[:, :],
                     start=True, stop=True))(),
                 reads=[f"{u}.rrow", "ones1"], writes=["pb5"],
                 tag=f"{u}.msb{g}.{tc}")
            P.op("scalar",
                 (lambda: lambda e: e.copy(rb[:, :], ctx.pb[5][:, 0:TW]))(),
                 reads=["pb5"], writes=[f"{u}.rb"], tag=f"{u}.msc{g}.{tc}")
            P.op("scalar",
                 (lambda col=col: lambda e: e.activation(
                     silb[:, :], gf[:, col:col + TW], AF.Silu))(),
                 reads=[f"{u}.gf"], writes=[f"{u}.silb"], tag=f"{u}.sil{g}.{tc}")
            P.op("vector",
                 (lambda col=col: lambda e: e.scalar_tensor_tensor(
                     sqs[:, :], of_b[:, col:col + TW], nrmw[:, 0:1], rb[:, :],
                     ALU.mult, ALU.mult))(),
                 reads=[f"{u}.ofb", f"{u}.nrmw", f"{u}.rb"], writes=[f"{u}.sqs"],
                 tag=f"{u}.on{g}.{tc}")
            P.op("vector",
                 (lambda g=g, col=col: lambda e: e.tensor_mul(
                     og[:, g * S_FULL + col:g * S_FULL + col + TW], sqs[:, :],
                     silb[:, :]))(),
                 reads=[f"{u}.sqs", f"{u}.silb"], writes=[f"{u}.og.{g}"],
                 tag=f"{u}.ogm{g}.{tc}")

    for tt in range(KC):
        for oc in range(4):
            for fk in range(8):
                buf = fk % 2
                P.dma("gpsimd",
                      (lambda fk=fk, oc=oc, buf=buf: lambda e: e.dma_start(
                          out=wos[:, buf * TW:(buf + 1) * TW],
                          in_=w[f"{u}_wo"][fk * 128:(fk + 1) * 128,
                                           oc * TW:(oc + 1) * TW]))(),
                      sem=f"{u}.wos.{buf}", writes=[f"{u}.wos.{buf}"],
                      tag=f"{u}.ldwo{tt}.{oc}.{fk}")
                P.op("tensor",
                     (lambda fk=fk, tt=tt, buf=buf: lambda e: e.matmul(
                         ctx.pb[6][:, 0:TW],
                         og[:, fk * S_FULL + tt * 128:fk * S_FULL + tt * 128 + 128],
                         wos[:, buf * TW:(buf + 1) * TW],
                         start=(fk == 0), stop=(fk == 7)))(),
                     reads=[f"{u}.wos.{buf}", f"{u}.og.{fk}"], writes=["pb6"],
                     tag=f"{u}.wo{tt}.{oc}.{fk}")
            r = (4 * tt + oc) % 4
            yv = yst[:, r * TW:(r + 1) * TW]
            P.op("scalar",
                 (lambda yv=yv: lambda e: e.copy(yv, ctx.pb[6][:, 0:TW]))(),
                 reads=["pb6"], writes=[f"{u}.yste.{r}"], tag=f"{u}.wev{tt}.{oc}")
            P.dma("gpsimd",
                  (lambda tt=tt, oc=oc, yv=yv: lambda e: e.dma_start(
                      out=wop_loc[tt * 128:(tt + 1) * 128, oc * TW:(oc + 1) * TW],
                      in_=yv))(),
                  sem=f"{u}.wopst", reads=[f"{u}.yste.{r}"],
                  writes=[f"{u}.wopg"], tag=f"{u}.wst{tt}.{oc}")
    P.op("gpsimd",
         lambda e: e.collective_compute(
             "ReduceScatter", ALU.add,
             replica_groups=[[0, 1], [2, 3], [4, 5], [6, 7]],
             ins=[wop_loc.opt()], outs=[wor_out.opt()]),
         reads=[f"{u}.wopg"], writes=[f"{u}.rsc"], coll=True, tag=f"{u}.ccrs")

    y_dram = scratch["y_dram"]
    st = LNStats(ctx, f"{u}pl")
    for t in range(2):
        for k in range(KC):
            for j in range(4):
                sslot = j % 2
                P.dma("sync",
                      (lambda t=t, k=k, j=j, sslot=sslot: lambda e: e.dma_start(
                          out=yst[:, sslot * TW:sslot * TW + 128],
                          in_=wor_out[(t * 4 + j) * 128:(t * 4 + j + 1) * 128,
                                      k * 128:(k + 1) * 128]))(),
                      sem=f"{u}.ytl.{sslot}", reads=[f"{u}.rsc"],
                      writes=[f"{u}.ytl.{sslot}"], tag=f"{u}.ldy{t}.{k}.{j}")
                P.op("tensor",
                     (lambda j=j, sslot=sslot: lambda e: e.transpose(
                         ctx.pb[7][:, j * 128:(j + 1) * 128],
                         yst[:, sslot * TW:sslot * TW + 128],
                         ctx.ident[:, :]))(),
                     reads=[f"{u}.ytl.{sslot}", "ident"], writes=["pb7"],
                     tag=f"{u}.ytr{t}.{k}.{j}")
            r = 2 + (k % 2)
            yv = yst[:, r * TW:(r + 1) * TW]
            P.op("scalar",
                 (lambda yv=yv: lambda e: e.copy(yv, ctx.pb[7][:, 0:TW]))(),
                 reads=["pb7"], writes=[f"{u}.yste.{r}"], tag=f"{u}.yev{t}.{k}")
            st.add(yv, [f"{u}.yste.{r}"])
            P.dma("gpsimd",
                  (lambda t=t, k=k, yv=yv: lambda e: e.dma_start(
                      out=y_dram[k * 128:(k + 1) * 128, t * TW:(t + 1) * TW],
                      in_=yv))(),
                  sem=f"{u}.yst2", reads=[f"{u}.yste.{r}"], writes=[f"{u}.ydrg"],
                  tag=f"{u}.sty{t}.{k}")
    st.finish()
    emit_apply_residual(ctx, y_dram, lambda k, t: [f"{u}.ydrg"], x_dram,
                        xout_dram, f"{u}pl", xout_key=xout_key, xdram_keys=xkeys)


def _prep_a0(inputs, c):
    half = c % 2
    sl = slice(half * 1024, (half + 1) * 1024)
    w = {}
    wq = inputs["L0_attn_Wq"][:, sl]
    wk = inputs["L0_attn_Wk"][:, sl]
    wg = inputs["L0_attn_Wgate"][:, sl]
    qkg = np.zeros((8, KC, 128, 384), np.float32)
    for g in range(8):
        for k in range(KC):
            qkg[g, k, :, 0:128] = wq[k * 128:(k + 1) * 128, g * 128:(g + 1) * 128]
            qkg[g, k, :, 128:256] = wk[k * 128:(k + 1) * 128, g * 128:(g + 1) * 128]
            qkg[g, k, :, 256:384] = wg[k * 128:(k + 1) * 128, g * 128:(g + 1) * 128]
    w["a0_wqkgP"] = np.ascontiguousarray(qkg)
    w["a0_wv"] = np.ascontiguousarray(inputs["L0_attn_Wv"][:, sl])
    w["a0_wg1P"] = pack_lhsT(inputs["L0_attn_Wg1"], 16)
    w["a0_wg2"] = np.ascontiguousarray(
        inputs["L0_attn_Wg2"][:, sl].reshape(16, 8, 128).transpose(1, 0, 2))
    w["a0_bg2"] = col128(inputs["L0_attn_bg2"][sl])
    w["a0_norm_w"] = np.ascontiguousarray(
        inputs["L0_attn_norm_w"].reshape(128, 1))
    w["a0_wo"] = np.ascontiguousarray(inputs["L0_attn_Wo"][sl, :])
    w["a0_trilT"] = np.ascontiguousarray(np.triu(np.ones((64, 64), np.float32)))
    return w


def _dram_a0(nc):
    w = {}
    w["a0_wqkgP"] = nc.dram_tensor("a0_wqkgP", [8, KC, 128, 384], F32,
                                   kind="ExternalInput").ap()
    w["a0_wv"] = nc.dram_tensor("a0_wv", [D, T_LOC], F32,
                                kind="ExternalInput").ap()
    w["a0_wg1P"] = nc.dram_tensor("a0_wg1P", [1, 128, KC * 16], F32,
                                  kind="ExternalInput").ap()
    w["a0_wg2"] = nc.dram_tensor("a0_wg2", [8, 16, 128], F32,
                                 kind="ExternalInput").ap()
    w["a0_bg2"] = nc.dram_tensor("a0_bg2", [128, 8], F32,
                                 kind="ExternalInput").ap()
    w["a0_norm_w"] = nc.dram_tensor("a0_norm_w", [128, 1], F32,
                                    kind="ExternalInput").ap()
    w["a0_wo"] = nc.dram_tensor("a0_wo", [T_LOC, D], F32,
                                kind="ExternalInput").ap()
    w["a0_trilT"] = nc.dram_tensor("a0_trilT", [64, 64], F32,
                                   kind="ExternalInput").ap()
    return w


def _scratch_a0(nc, uid="a0"):
    s = {}
    s["xng_loc"] = nc.dram_tensor(f"{uid}_xng_loc", [D, T_LOC], BF16,
                                  kind="Internal").ap()
    s["xng_gath"] = nc.dram_tensor(f"{uid}_xng_gath", [2 * D, T_LOC], BF16,
                                   kind="Internal").ap()
    s["wop_loc"] = nc.dram_tensor(f"{uid}_wop_loc", [S_FULL, D], F32,
                                  kind="Internal").ap()
    s["wor_out"] = nc.dram_tensor(f"{uid}_wor_out", [T_LOC, D], F32,
                                  kind="Internal").ap()
    s["y_dram"] = nc.dram_tensor(f"{uid}_y_scratch", [D, T_LOC], F32,
                                 kind="Internal").ap()
    return s


def build_a0_only(debug=False):
    nc = bass.Bass("TRN2", target_bir_lowering=False)
    x_in = nc.dram_tensor("x_in", [D, T_LOC], F32, kind="ExternalInput").ap()
    x_out = nc.dram_tensor("x_out", [D, T_LOC], F32, kind="ExternalOutput").ap()
    ident = nc.dram_tensor("ident_in", [128, 128], F32, kind="ExternalInput").ap()
    consts = nc.dram_tensor("consts_in", [128, 16], F32, kind="ExternalInput").ap()
    with ExitStack() as stack:
        ctx = Ctx(nc, stack)
        ctx.load_consts(ident, consts)
        w = _dram_a0(nc)
        scratch = _scratch_a0(nc)
        emit_l0_attn(ctx, x_in, x_out, w, scratch)
        if debug:
            P = ctx.P
            og_dbg = nc.dram_tensor("dbg_og", [128, 8 * S_FULL], BF16,
                                    kind="ExternalOutput").ap()
            P.dma("gpsimd", lambda e: e.dma_start(out=og_dbg[:, :],
                                                  in_=ctx._a0_og[:, :]),
                  sem="dbg", reads=[f"a0.og.{g}" for g in range(8)], tag="dbg.og")
            for nm, tile, key, dt in (
                    ("dbg_qf", ctx._a0_d["qf"], "a0.qf", BF16),
                    ("dbg_kf", ctx._a0_d["kf"], "a0.kf", BF16),
                    ("dbg_gf", ctx._a0_d["gf"], "a0.gf", BF16),
                    ("dbg_gb", ctx._a0_d["gb"], "a0.gb", F32),
                    ("dbg_qhb", ctx._a0_d["qh_b"], "a0.qhb", BF16),
                    ("dbg_khb", ctx._a0_d["kh_b"], "a0.khb", BF16),
                    ("dbg_qgb", ctx._a0_d["qg_b"], "a0.qgb", BF16),
                    ("dbg_ksf", ctx._a0_d["ksf"], "a0.ksf", F32),
                    ("dbg_ofb", ctx._a0_d["of_b"], "a0.ofb", F32),
                    ("dbg_S", ctx._a0_d["S_sb"], "a0.S", F32)):
                dd = nc.dram_tensor(nm, list(tile.shape), dt,
                                    kind="ExternalOutput").ap()
                P.dma("gpsimd",
                      (lambda dd=dd, tile=tile: lambda e: e.dma_start(
                          out=dd[:, :], in_=tile[:, :]))(),
                      sem="dbg", reads=[key], tag=f"dbg.{nm}")
            dv = nc.dram_tensor("dbg_vtok", [64, NCH * 128], BF16,
                                kind="ExternalOutput").ap()
            P.dma("gpsimd", lambda e: e.dma_start(out=dv[:, :],
                                                  in_=ctx._a0_d["vtok"][:, :]),
                  sem="dbg", reads=["a0.vtok"], tag="dbg.vtok")
            dd2 = nc.dram_tensor("dbg_dec", [128, 2 * NCH], F32,
                                 kind="ExternalOutput").ap()
            P.dma("gpsimd", lambda e: e.dma_start(out=dd2[:, :],
                                                  in_=ctx._a0_d["dec"][:, :]),
                  sem="dbg", reads=["a0.dec.1"], tag="dbg.dec")
            dk2 = nc.dram_tensor("dbg_kst", [64, 256], BF16,
                                 kind="ExternalOutput").ap()
            P.dma("gpsimd", lambda e: e.dma_start(out=dk2[:, :],
                                                  in_=ctx._a0_kst[:, :]),
                  sem="dbg", reads=["a0.kst.0", "a0.kst.1"], tag="dbg.kst")
        ctx.P.finalize()
    return nc, ctx.P.stats()



def emit_l1_attn(ctx, x_dram, xout_dram, w, y_dram, scratch, uid="a1",
                 xkeys=None, xout_key=None):
    nc, P = ctx.nc, ctx.P
    u = uid
    xn = ctx.sb(f"{u}_xn", [128, KC * T_LOC], BF16)
    o_sb = ctx.sb(f"{u}_o", [128, KC * T_LOC], BF16)
    q_h = ctx.sb(f"{u}_qh", [128, 2 * T_LOC], BF16)
    kh = ctx.sb(f"{u}_kh", [128, 2 * S_FULL], BF16)
    vh = ctx.sb(f"{u}_vh", [128, 2 * S_FULL], BF16)
    p_sb = ctx.sb(f"{u}_p", [128, KC * TW], BF16)
    kv_st = ctx.sb(f"{u}_kvst", [128, 2 * T_LOC], BF16)
    yst = ctx.sb(f"{u}_yst", [128, 4 * TW], F32)
    wl = ctx.sb(f"{u}_wl", [128, 2 * KC * 128], BF16)
    wv_s = ctx.sb(f"{u}_wv", [128, KC * TW], BF16)
    bq = ctx.sb(f"{u}s_bq", [128, KC], F32)
    bk = ctx.sb(f"{u}s_bk", [128, KC], F32)
    bo = ctx.sb(f"{u}s_bo", [128, KC], F32)
    bvr = ctx.sb(f"{u}s_bvr", [1, D], F32)
    onesb = ctx.sb(f"{u}_onesb", [128, 1], BF16)
    colsc = ctx.sb(f"{u}_cols", [128, 2], F32)
    rden = ctx.sb(f"{u}_rden", [1, TW], F32)
    rdenb = ctx.sb(f"{u}_rdenb", [128, TW], F32)
    ctx._a1_osb, ctx._a1_qh, ctx._a1_kh, ctx._a1_vh, ctx._a1_colsc = \
        o_sb, q_h, kh, vh, colsc
    k_loc, k_gath = scratch["k_loc"], scratch["k_gath"]
    v_loc, v_gath = scratch["v_loc"], scratch["v_gath"]

    P.op("vector", lambda e: e.memset(onesb[:, :], 1.0), writes=[f"{u}.onesb"],
         tag=f"{u}.onesb")
    for nm, t in ((f"{u}_bq", bq), (f"{u}_bk", bk), (f"{u}_bo", bo)):
        P.dma("sync",
              (lambda nm=nm, t=t: lambda e: e.dma_start(out=t[:, :],
                                                        in_=w[nm][:, :]))(),
              sem=f"{u}.b.{nm}", writes=[f"{u}.{nm}"], tag=f"{u}.ld{nm}")
    P.dma("sync", lambda e: e.dma_start(out=bvr[:, :], in_=w[f"{u}_bv_row"][:, :]),
          sem=f"{u}.bvr", writes=[f"{u}.bvr"], tag=f"{u}.ldbvr")

    emit_ln_x_to_xn(ctx, x_dram, xn, f"{u}.xn", f"{u}ln", xdram_keys=xkeys)

    def proj_fm(nm, which, bias_t, bias_key, out_cb, n_groups=KC):
        for o in range(n_groups):
            buf = o % 2
            P.dma("gpsimd",
                  (lambda o=o, buf=buf, which=which: lambda e: e.dma_start(
                      out=wl[:, buf * KC * 128:(buf + 1) * KC * 128],
                      in_=w[which][o, :, :]))(),
                  sem=f"{u}.wl.{buf}", writes=[f"{u}.wl.{buf}"],
                  tag=f"{u}.{nm}.ldw{o}")
            for t in range(2):
                pq = ctx.pb[t]
                for k in range(KC):
                    P.op("tensor",
                         (lambda k=k, buf=buf, pq=pq, o=o, t=t: lambda e:
                          e.matmul(pq[:, 0:TW],
                                   wl[:, (buf * KC + k) * 128:(buf * KC + k + 1) * 128],
                                   xn[:, k * T_LOC + t * TW:k * T_LOC + t * TW + TW],
                                   start=(k == 0), stop=(k == KC - 1)))(),
                         reads=[f"{u}.wl.{buf}", f"{u}.xn.{k}"],
                         writes=[f"pb{t}"], tag=f"{u}.{nm}.mm{o}.{k}.{t}")
                out_cb(o, t, pq)

    def k_out(o, t, pq):
        r = (2 * o + t) % 4
        yv = kv_st[:, r * TW:(r + 1) * TW]
        P.op("scalar",
             (lambda yv=yv, pq=pq, o=o: lambda e: e.activation(
                 yv, pq[:, 0:TW], AF.Identity, bias=bk[:, o:o + 1]))(),
             reads=[f"pb{t}", f"{u}.{u}_bk"], writes=[f"{u}.kvst.{r}"],
             tag=f"{u}.kev{o}.{t}")
        P.dma("gpsimd",
              (lambda o=o, t=t, yv=yv: lambda e: e.dma_start(
                  out=k_loc[o * 128:(o + 1) * 128, t * TW:(t + 1) * TW],
                  in_=yv))(),
              sem=f"{u}.kst", reads=[f"{u}.kvst.{r}"], writes=[f"{u}.kstg"],
              tag=f"{u}.kst{o}.{t}")
    proj_fm("k", f"{u}_wkP", bk, f"{u}.{u}_bk", k_out)

    for fc in range(4):
        P.dma("gpsimd",
              (lambda fc=fc: lambda e: e.dma_start(
                  out=wv_s[:, :].rearrange("p (k m) -> p k m", m=TW),
                  in_=w[f"{u}_wvT"].rearrange("(k p) m -> p k m", p=128)
                  [:, :, fc * TW:(fc + 1) * TW]))(),
              sem=f"{u}.wv", writes=[f"{u}.wv"], tag=f"{u}.vldw{fc}")
        for tt in range(8):
            pv = ctx.pb[2 + (tt % 2)]
            for k in range(KC):
                P.op("tensor",
                     (lambda k=k, tt=tt, pv=pv: lambda e: e.matmul(
                         pv[:, 0:TW],
                         xn[:, k * T_LOC + tt * 128:k * T_LOC + (tt + 1) * 128],
                         wv_s[:, k * TW:(k + 1) * TW],
                         start=(k == 0), stop=False))(),
                     reads=[f"{u}.wv", f"{u}.xn.{k}"],
                     writes=[f"pb{2 + (tt % 2)}"], tag=f"{u}.vmm{tt}.{fc}.{k}")
            P.op("tensor",
                 (lambda fc=fc, pv=pv: lambda e: e.matmul(
                     pv[:, 0:TW], ctx.ones1[:, :], bvr[:, fc * TW:(fc + 1) * TW],
                     start=False, stop=True))(),
                 reads=["ones1", f"{u}.bvr"], writes=[f"pb{2 + (tt % 2)}"],
                 tag=f"{u}.vbias{tt}.{fc}")
            r = tt % 4
            yv = kv_st[:, r * TW:(r + 1) * TW]
            P.op("scalar",
                 (lambda yv=yv, pv=pv: lambda e: e.copy(yv, pv[:, 0:TW]))(),
                 reads=[f"pb{2 + (tt % 2)}"], writes=[f"{u}.kvst.{r}"],
                 tag=f"{u}.vev{tt}.{fc}")
            P.dma("gpsimd",
                  (lambda tt=tt, fc=fc, yv=yv: lambda e: e.dma_start(
                      out=v_loc[tt * 128:(tt + 1) * 128, fc * TW:(fc + 1) * TW],
                      in_=yv))(),
                  sem=f"{u}.vst", reads=[f"{u}.kvst.{r}"], writes=[f"{u}.vstg"],
                  tag=f"{u}.vst{tt}.{fc}")

    P.op("gpsimd",
         lambda e: e.collective_compute(
             "AllGather", ALU.bypass,
             replica_groups=[[0, 1], [2, 3], [4, 5], [6, 7]],
             ins=[k_loc.opt()], outs=[k_gath.opt()]),
         reads=[f"{u}.kstg"], writes=[f"{u}.kvg"], coll=True, tag=f"{u}.cck")
    P.op("gpsimd",
         lambda e: e.collective_compute(
             "AllGather", ALU.bypass,
             replica_groups=[[0, 1], [2, 3], [4, 5], [6, 7]],
             ins=[v_loc.opt()], outs=[v_gath.opt()]),
         reads=[f"{u}.vstg"], writes=[f"{u}.kvg"], coll=True, tag=f"{u}.ccv")

    ACC = 1.0 / np.sqrt(128.0)
    for h in range(NH):
        hb = h % 2
        P.dma("sync",
              (lambda h=h, hb=hb: lambda e: e.dma_start(
                  out=kh[:, hb * S_FULL:hb * S_FULL + T_LOC],
                  in_=k_gath[h * 128:(h + 1) * 128, :]))(),
              sem=f"{u}.kh.{hb}", reads=[f"{u}.kvg"], writes=[f"{u}.kh.{hb}"],
              tag=f"{u}.ldkh{h}")
        P.dma("sync",
              (lambda h=h, hb=hb: lambda e: e.dma_start(
                  out=kh[:, hb * S_FULL + T_LOC:(hb + 1) * S_FULL],
                  in_=k_gath[D + h * 128:D + (h + 1) * 128, :]))(),
              sem=f"{u}.kh2.{hb}", reads=[f"{u}.kvg"], writes=[f"{u}.kh2.{hb}"],
              tag=f"{u}.ldkh2{h}")
        P.dma("sync",
              (lambda h=h, hb=hb: lambda e: e.dma_start(
                  out=vh[:, hb * S_FULL:(hb + 1) * S_FULL]
                  .rearrange("p (c f) -> p c f", f=128),
                  in_=v_gath[:, h * 128:(h + 1) * 128]
                  .rearrange("(c p) f -> p c f", p=128)))(),
              sem=f"{u}.vh.{hb}", reads=[f"{u}.kvg"], writes=[f"{u}.vh.{hb}"],
              tag=f"{u}.ldvh{h}")
        khs = kh[:, hb * S_FULL:(hb + 1) * S_FULL]
        khkeys = [f"{u}.kh.{hb}", f"{u}.kh2.{hb}"]
        vhs = vh[:, hb * S_FULL:(hb + 1) * S_FULL]
        P.dma("gpsimd",
              (lambda h=h, hb=hb: lambda e: e.dma_start(
                  out=wl[:, hb * KC * 128:(hb + 1) * KC * 128],
                  in_=w[f"{u}_wqP"][h, :, :]))(),
              sem=f"{u}.wl.{hb}", writes=[f"{u}.wl.{hb}"], tag=f"{u}.qldw{h}")
        for t in range(2):
            pq = ctx.pb[4]
            for k in range(KC):
                P.op("tensor",
                     (lambda k=k, hb=hb, t=t: lambda e: e.matmul(
                         ctx.pb[4][:, 0:TW],
                         wl[:, (hb * KC + k) * 128:(hb * KC + k + 1) * 128],
                         xn[:, k * T_LOC + t * TW:k * T_LOC + t * TW + TW],
                         start=(k == 0), stop=(k == KC - 1)))(),
                     reads=[f"{u}.wl.{hb}", f"{u}.xn.{k}"], writes=["pb4"],
                     tag=f"{u}.qmm{h}.{k}.{t}")
            P.op("scalar",
                 (lambda hb=hb, t=t, h=h: lambda e: e.activation(
                     q_h[:, hb * T_LOC + t * TW:hb * T_LOC + (t + 1) * TW],
                     ctx.pb[4][:, 0:TW], AF.Identity, bias=bq[:, h:h + 1]))(),
                 reads=["pb4", f"{u}.{u}_bq"], writes=[f"{u}.qh.{hb}"],
                 tag=f"{u}.qev{h}.{t}")
        qhs = q_h[:, hb * T_LOC:(hb + 1) * T_LOC]
        for qt in range(2):
            qv = qhs[:, qt * TW:(qt + 1) * TW]
            for qc in range(4):
                mcol = colsc[:, 0:1]
                for kg in range(4):
                    pm = ctx.pb[0 + (kg % 2)]
                    P.op("tensor",
                         (lambda qc=qc, kg=kg, pm=pm, qt=qt, qhs=qhs, khs=khs:
                          lambda e: e.matmul(
                             pm[:, 0:TW],
                             qhs[:, qt * TW + qc * 128:qt * TW + (qc + 1) * 128],
                             khs[:, kg * TW:(kg + 1) * TW],
                             start=True, stop=True))(),
                         reads=khkeys + [f"{u}.qh.{hb}"],
                         writes=[f"pb{0 + (kg % 2)}"],
                         tag=f"{u}.qmm{h}.{qt}.{qc}.{kg}")
                    P.op("vector",
                         (lambda pm=pm: lambda e: e.reduce_max(
                             colsc[:, 1:2], pm[:, 0:TW], AX.X))(),
                         reads=[f"pb{0 + (kg % 2)}"], writes=[f"{u}.r1"],
                         tag=f"{u}.qrm{h}.{qt}.{qc}.{kg}")
                    if kg == 0:
                        P.op("vector",
                             (lambda mcol=mcol: lambda e: e.tensor_copy(
                                 mcol, colsc[:, 1:2]))(),
                             reads=[f"{u}.r1"], writes=[f"{u}.mcol"],
                             tag=f"{u}.qmc{h}.{qt}.{qc}.{kg}")
                    else:
                        P.op("vector",
                             (lambda mcol=mcol: lambda e: e.tensor_max(
                                 mcol, mcol, colsc[:, 1:2]))(),
                             reads=[f"{u}.r1", f"{u}.mcol"], writes=[f"{u}.mcol"],
                             tag=f"{u}.qmc{h}.{qt}.{qc}.{kg}")
                P.op("vector",
                     (lambda mcol=mcol: lambda e: e.tensor_scalar(
                         mcol, mcol, -1.0, None, ALU.mult))(),
                     reads=[f"{u}.mcol"], writes=[f"{u}.mcol"],
                     tag=f"{u}.qneg{h}.{qt}.{qc}")
                P.op("tensor",
                     (lambda qc=qc, mcol=mcol: lambda e: e.transpose(
                         ctx.pb[4][0:1, qc * 128:(qc + 1) * 128], mcol,
                         ctx.ident[:, :]))(),
                     reads=[f"{u}.mcol", "ident"], writes=["pb4"],
                     tag=f"{u}.qtr{h}.{qt}.{qc}")
            P.op("scalar",
                 (lambda: lambda e: e.copy(rden[:, :], ctx.pb[4][0:1, 0:TW]))(),
                 reads=["pb4"], writes=[f"{u}.rrow"], tag=f"{u}.negm{h}.{qt}")
            for kc in range(KC):
                ps = ctx.pb[0 + (kc % 2)]
                P.op("tensor",
                     (lambda kc=kc, ps=ps, qv=qv, khs=khs: lambda e: e.matmul(
                         ps[:, 0:TW], khs[:, kc * 128:(kc + 1) * 128], qv,
                         start=True, stop=False))(),
                     reads=khkeys + [f"{u}.qh.{hb}"],
                     writes=[f"pb{0 + (kc % 2)}"], tag=f"{u}.smm{h}.{qt}.{kc}")
                P.op("tensor",
                     (lambda kc=kc, ps=ps: lambda e: e.matmul(
                         ps[:, 0:TW], ctx.ones1[:, :], rden[:, :],
                         start=False, stop=True))(),
                     reads=[f"{u}.rrow", "ones1"],
                     writes=[f"pb{0 + (kc % 2)}"], tag=f"{u}.sin{h}.{qt}.{kc}")
                P.op("scalar",
                     (lambda kc=kc, ps=ps: lambda e: e.activation(
                         p_sb[:, kc * TW:(kc + 1) * TW], ps[:, 0:TW], AF.Exp,
                         scale=ctx.cst(C_QKSCALE)))(),
                     reads=[f"pb{0 + (kc % 2)}", "consts"],
                     writes=[f"{u}.p.{kc}"], tag=f"{u}.exp{h}.{qt}.{kc}")
            for kc in range(KC):
                P.op("tensor",
                     (lambda kc=kc: lambda e: e.matmul(
                         ctx.pb[3][0:1, 0:TW], onesb[:, :],
                         p_sb[:, kc * TW:(kc + 1) * TW],
                         start=(kc == 0), stop=(kc == KC - 1)))(),
                     reads=[f"{u}.p.{kc}", f"{u}.onesb"], writes=["pb3"],
                     tag=f"{u}.den{h}.{qt}.{kc}")
                P.op("tensor",
                     (lambda kc=kc, vhs=vhs: lambda e: e.matmul(
                         ctx.pb[2][:, 0:TW], vhs[:, kc * 128:(kc + 1) * 128],
                         p_sb[:, kc * TW:(kc + 1) * TW],
                         start=(kc == 0), stop=(kc == KC - 1)))(),
                     reads=[f"{u}.p.{kc}", f"{u}.vh.{hb}"], writes=["pb2"],
                     tag=f"{u}.omm{h}.{qt}.{kc}")
            P.op("scalar",
                 (lambda: lambda e: e.activation(rden[:, :], ctx.pb[3][0:1, 0:TW],
                                                 AF.Ln))(),
                 reads=["pb3"], writes=[f"{u}.rrow"], tag=f"{u}.rdl{h}.{qt}")
            P.op("scalar",
                 (lambda: lambda e: e.activation(rden[:, :], rden[:, :], AF.Exp,
                                                 scale=ctx.cstr(C_NEG1)))(),
                 reads=[f"{u}.rrow", "consts"], writes=[f"{u}.rrow"],
                 tag=f"{u}.rde{h}.{qt}")
            P.op("tensor",
                 (lambda: lambda e: e.matmul(ctx.pb[5][:, 0:TW], ctx.ones1[:, :],
                                             rden[:, :], start=True, stop=True))(),
                 reads=[f"{u}.rrow", "ones1"], writes=["pb5"],
                 tag=f"{u}.rdb{h}.{qt}")
            P.op("scalar",
                 (lambda: lambda e: e.copy(rdenb[:, :], ctx.pb[5][:, 0:TW]))(),
                 reads=["pb5"], writes=[f"{u}.rdenb"], tag=f"{u}.rdbe{h}.{qt}")
            P.op("vector",
                 (lambda h=h, qt=qt: lambda e: e.tensor_mul(
                     o_sb[:, h * T_LOC + qt * TW:h * T_LOC + (qt + 1) * TW],
                     ctx.pb[2][:, 0:TW], rdenb[:, :]))(),
                 reads=["pb2", f"{u}.rdenb"], writes=[f"{u}.o.{h}"],
                 tag=f"{u}.onrm{h}.{qt}")

    st = LNStats(ctx, f"{u}pl")
    for t in range(2):
        for o in range(KC):
            buf = o % 2
            P.dma("gpsimd",
                  (lambda o=o, buf=buf: lambda e: e.dma_start(
                      out=wl[:, buf * KC * 128:(buf + 1) * KC * 128],
                      in_=w[f"{u}_woP"][o, :, :]))(),
                  sem=f"{u}.wl.{buf}", writes=[f"{u}.wl.{buf}"],
                  tag=f"{u}.oldw{o}.{t}")
            po = ctx.pb[6 if False else 4]
            for k in range(KC):
                P.op("tensor",
                     (lambda k=k, buf=buf, t=t: lambda e: e.matmul(
                         ctx.pb[4][:, 0:TW],
                         wl[:, (buf * KC + k) * 128:(buf * KC + k + 1) * 128],
                         o_sb[:, k * T_LOC + t * TW:k * T_LOC + t * TW + TW],
                         start=(k == 0), stop=(k == KC - 1)))(),
                     reads=[f"{u}.wl.{buf}", f"{u}.o.{k}"], writes=["pb4"],
                     tag=f"{u}.wmm{o}.{k}.{t}")
            r = (2 * o + t) % 4
            yv = yst[:, r * TW:(r + 1) * TW]
            P.op("scalar",
                 (lambda o=o, yv=yv: lambda e: e.activation(
                     yv, ctx.pb[4][:, 0:TW], AF.Identity, bias=bo[:, o:o + 1]))(),
                 reads=["pb4", f"{u}.{u}_bo"], writes=[f"{u}.yste.{r}"],
                 tag=f"{u}.oev{o}.{t}")
            st.add(yv, [f"{u}.yste.{r}"])
            P.dma("gpsimd",
                  (lambda o=o, t=t, yv=yv: lambda e: e.dma_start(
                      out=y_dram[o * 128:(o + 1) * 128, t * TW:(t + 1) * TW],
                      in_=yv))(),
                  sem=f"{u}.yst", reads=[f"{u}.yste.{r}"], writes=[f"{u}.ydrg"],
                  tag=f"{u}.sty{o}.{t}")
    st.finish()
    emit_apply_residual(ctx, y_dram, lambda k, t: [f"{u}.ydrg"], x_dram,
                        xout_dram, f"{u}pl", xout_key=xout_key, xdram_keys=xkeys)


def _prep_a1(inputs):
    w = {}
    w["a1_wqP"] = pack_lhsT(np.ascontiguousarray(inputs["L1_attn_Wq"].T), 128)
    w["a1_wkP"] = pack_lhsT(np.ascontiguousarray(inputs["L1_attn_Wk"].T), 128)
    w["a1_wvT"] = np.ascontiguousarray(inputs["L1_attn_Wv"].T)
    w["a1_woP"] = pack_lhsT(np.ascontiguousarray(inputs["L1_attn_Wo"].T), 128)
    w["a1_bq"] = col128(inputs["L1_attn_bq"])
    w["a1_bk"] = col128(inputs["L1_attn_bk"])
    w["a1_bv_row"] = np.ascontiguousarray(inputs["L1_attn_bv"].reshape(1, -1))
    w["a1_bo"] = col128(inputs["L1_attn_bo"])
    return w


def _dram_a1(nc):
    w = {}
    w["a1_wqP"] = nc.dram_tensor("a1_wqP", [KC, 128, KC * 128], F32,
                                 kind="ExternalInput").ap()
    w["a1_wkP"] = nc.dram_tensor("a1_wkP", [KC, 128, KC * 128], F32,
                                 kind="ExternalInput").ap()
    w["a1_wvT"] = nc.dram_tensor("a1_wvT", [D, D], F32, kind="ExternalInput").ap()
    w["a1_woP"] = nc.dram_tensor("a1_woP", [KC, 128, KC * 128], F32,
                                 kind="ExternalInput").ap()
    w["a1_bq"] = nc.dram_tensor("a1_bq", [128, KC], F32, kind="ExternalInput").ap()
    w["a1_bk"] = nc.dram_tensor("a1_bk", [128, KC], F32, kind="ExternalInput").ap()
    w["a1_bv_row"] = nc.dram_tensor("a1_bv_row", [1, D], F32,
                                    kind="ExternalInput").ap()
    w["a1_bo"] = nc.dram_tensor("a1_bo", [128, KC], F32, kind="ExternalInput").ap()
    return w


def _scratch_a1(nc, uid="a1"):
    s = {}
    s["k_loc"] = nc.dram_tensor(f"{uid}_k_loc", [D, T_LOC], BF16, kind="Internal").ap()
    s["k_gath"] = nc.dram_tensor(f"{uid}_k_gath", [2 * D, T_LOC], BF16,
                                 kind="Internal").ap()
    s["v_loc"] = nc.dram_tensor(f"{uid}_v_loc", [T_LOC, D], BF16, kind="Internal").ap()
    s["v_gath"] = nc.dram_tensor(f"{uid}_v_gath", [S_FULL, D], BF16,
                                 kind="Internal").ap()
    s["nm_dram"] = nc.dram_tensor(f"{uid}_nm", [NH, T_LOC], F32,
                                  kind="Internal").ap()
    return s


def build_a1_only(debug=False):
    nc = bass.Bass("TRN2", target_bir_lowering=False)
    x_in = nc.dram_tensor("x_in", [D, T_LOC], F32, kind="ExternalInput").ap()
    x_out = nc.dram_tensor("x_out", [D, T_LOC], F32, kind="ExternalOutput").ap()
    ident = nc.dram_tensor("ident_in", [128, 128], F32, kind="ExternalInput").ap()
    consts = nc.dram_tensor("consts_in", [128, 16], F32, kind="ExternalInput").ap()
    with ExitStack() as stack:
        ctx = Ctx(nc, stack)
        ctx.load_consts(ident, consts)
        w = _dram_a1(nc)
        scratch = _scratch_a1(nc)
        y_dram = nc.dram_tensor("a1_y_scratch", [D, T_LOC], F32,
                                kind="ExternalOutput" if debug else "Internal").ap()
        emit_l1_attn(ctx, x_in, x_out, w, y_dram, scratch)
        if debug:
            P = ctx.P
            o_dbg = nc.dram_tensor("dbg_o", [128, KC * T_LOC], BF16,
                                   kind="ExternalOutput").ap()
            P.dma("gpsimd", lambda e: e.dma_start(out=o_dbg[:, :],
                                                  in_=ctx._a1_osb[:, :]),
                  sem="dbg", reads=[f"a1.o.{h}" for h in range(NH)], tag="dbg.o")
            q_dbg = nc.dram_tensor("dbg_q", [128, NH * T_LOC], BF16,
                                   kind="ExternalOutput").ap()
            P.dma("gpsimd", lambda e: e.dma_start(out=q_dbg[:, :],
                                                  in_=ctx._a1_qh[:, :]),
                  sem="dbg", reads=[f"a1.qh.{h}" for h in range(NH)], tag="dbg.q")
            kh_dbg = nc.dram_tensor("dbg_kh", [128, 2 * S_FULL], BF16,
                                    kind="ExternalOutput").ap()
            P.dma("gpsimd", lambda e: e.dma_start(out=kh_dbg[:, :],
                                                  in_=ctx._a1_kh[:, :]),
                  sem="dbg", reads=["a1.kh.0", "a1.kh.1", "a1.kh2.0", "a1.kh2.1"],
                  tag="dbg.kh")
            vh_dbg = nc.dram_tensor("dbg_vh", [128, 2 * S_FULL], BF16,
                                    kind="ExternalOutput").ap()
            P.dma("gpsimd", lambda e: e.dma_start(out=vh_dbg[:, :],
                                                  in_=ctx._a1_vh[:, :]),
                  sem="dbg", reads=["a1.vh.0", "a1.vh.1"], tag="dbg.vh")
            c_dbg = nc.dram_tensor("dbg_c", [128, 2], F32,
                                   kind="ExternalOutput").ap()
            P.dma("gpsimd", lambda e: e.dma_start(out=c_dbg[:, :],
                                                  in_=ctx._a1_colsc[:, :]),
                  sem="dbg", reads=["a1.negcol"], tag="dbg.c")
        ctx.P.finalize()
    return nc, ctx.P.stats()



NE = 16
IM = 1024
IME = IM // 128
ISC = 16


def emit_l1_moe(ctx, x_dram, xout_dram, w, comb_dram, uid="m1", xkeys=None,
                xout_key=None):
    nc, P = ctx.nc, ctx.P
    u = uid
    crow = ctx.sb(f"{u}_crow", [1, T_LOC], F32)
    xn = ctx.sb(f"{u}_xn", [128, KC * T_LOC], BF16)
    acc = ctx.sb(f"{u}_acc", [128, KC * T_LOC], F32)
    hsl = ctx.sb(f"{u}_hsl", [128, ISC * T_LOC], BF16)
    wg_s = ctx.sb(f"{u}_wg", [128, 2 * KC * 128], BF16)
    wu_s = ctx.sb(f"{u}_wu", [128, 2 * KC * 128], BF16)
    wd_s = ctx.sb(f"{u}_wd", [128, 2 * ISC * 128], BF16)
    silu_t = ctx.sb(f"{u}_st", [128, 2 * TW], BF16)
    wr = ctx.sb(f"{u}_wr", [128, 8 * 16], F32)
    wcur = ctx.sb(f"{u}_wcur", [128, 16], F32)
    eqs = ctx.sb(f"{u}_eqs", [128, 16], F32)
    tsum = ctx.sb(f"{u}_tsum", [128, 4], F32)
    comb_f = ctx.ln_rst[0:16, :]
    cbb = ctx.ln_mr

    ctx._m1_combf, ctx._m1_acc = comb_f, acc
    emit_ln_x_to_xn(ctx, x_dram, xn, f"{u}.xn", f"{u}ln", xdram_keys=xkeys)

    P.dma("gpsimd",
          lambda e: e.dma_start(out=wg_s[:, 0:256], in_=w[f"{u}_gwP"][0, :, :]),
          sem=f"{u}.gw", writes=[f"{u}.gw"], tag=f"{u}.ldgw")
    for t in range(2):
        for k in range(KC):
            P.op("tensor",
                 (lambda k=k, t=t: lambda e: e.matmul(
                     ctx.pb[0][0:16, t * TW:(t + 1) * TW] if False else
                     ctx.pb[0][0:16, 0:TW],
                     wg_s[:, k * 16:(k + 1) * 16],
                     xn[:, k * T_LOC + t * TW:k * T_LOC + t * TW + TW],
                     start=(k == 0), stop=(k == KC - 1)))(),
                 reads=[f"{u}.gw", f"{u}.xn.{k}"], writes=["pb0"],
                 tag=f"{u}.rmm{k}.{t}")
        P.op("scalar",
             (lambda t=t: lambda e: e.activation(
                 comb_f[:, t * TW:(t + 1) * TW], ctx.pb[0][0:16, 0:TW],
                 AF.Sigmoid))(),
             reads=["pb0"], writes=[f"{u}.wsig", "lnrst.0", "lnrst.1"],
             tag=f"{u}.sig{t}")
    for c in range(8):
        P.op("tensor",
             (lambda c=c: lambda e: e.transpose(
                 ctx.pb[1][0:128, 0:16], comb_f[:, c * 128:(c + 1) * 128],
                 ctx.ident[0:16, 0:16]))(),
             reads=[f"{u}.wsig", "ident"], writes=["pb1"], tag=f"{u}.rtr{c}")
        P.op("scalar",
             (lambda c=c: lambda e: e.copy(wr[:, c * 16:(c + 1) * 16],
                                           ctx.pb[1][0:128, 0:16]))(),
             reads=["pb1"], writes=[f"{u}.wr.{c}"], tag=f"{u}.rtc{c}")
    for c in range(8):
        P.op("vector",
             (lambda c=c: lambda e: e.tensor_copy(wcur[:, :],
                                                  wr[:, c * 16:(c + 1) * 16]))(),
             reads=[f"{u}.wr.{c}"], writes=[f"{u}.wcur"], tag=f"{u}.tc{c}")
        P.op("vector", lambda e: e.memset(tsum[:, 0:1], 0.0),
             writes=[f"{u}.tsum"], tag=f"{u}.ts0.{c}")
        for it in range(8):
            P.op("vector",
                 lambda e: e.reduce_max(tsum[:, 1:2], wcur[:, :], AX.X),
                 reads=[f"{u}.wcur"], writes=[f"{u}.m1r"], tag=f"{u}.rm{c}.{it}")
            P.op("vector",
                 lambda e: e.tensor_scalar(eqs[:, :], wcur[:, :], tsum[:, 1:2],
                                           None, ALU.is_equal),
                 reads=[f"{u}.wcur", f"{u}.m1r"], writes=[f"{u}.eqs"],
                 tag=f"{u}.eq{c}.{it}")
            P.op("vector",
                 lambda e: e.scalar_tensor_tensor(wcur[:, :], eqs[:, :], -1e30,
                                                  wcur[:, :], ALU.mult, ALU.add),
                 reads=[f"{u}.eqs", f"{u}.wcur"], writes=[f"{u}.wcur"],
                 tag=f"{u}.msk{c}.{it}")
            P.op("vector",
                 lambda e: e.tensor_add(tsum[:, 0:1], tsum[:, 0:1], tsum[:, 1:2]),
                 reads=[f"{u}.m1r", f"{u}.tsum"], writes=[f"{u}.tsum"],
                 tag=f"{u}.tsa{c}.{it}")
        P.op("vector",
             lambda e: e.tensor_scalar(eqs[:, :], wcur[:, :], -1e29, None,
                                       ALU.is_lt),
             reads=[f"{u}.wcur"], writes=[f"{u}.eqs"], tag=f"{u}.sel{c}")
        P.op("vector",
             (lambda c=c: lambda e: e.tensor_mul(eqs[:, :], eqs[:, :],
                                                 wr[:, c * 16:(c + 1) * 16]))(),
             reads=[f"{u}.eqs", f"{u}.wr.{c}"], writes=[f"{u}.eqs"],
             tag=f"{u}.cmb{c}")
        P.op("vector",
             lambda e: e.tensor_scalar(tsum[:, 2:3], tsum[:, 0:1], 1e-6, None,
                                       ALU.add),
             reads=[f"{u}.tsum"], writes=[f"{u}.t2"], tag=f"{u}.eps{c}")
        P.op("vector",
             lambda e: e.reciprocal(tsum[:, 3:4], tsum[:, 2:3]),
             reads=[f"{u}.t2"], writes=[f"{u}.t3"], tag=f"{u}.rcp{c}")
        P.op("vector",
             lambda e: e.tensor_scalar(tsum[:, 3:4], tsum[:, 3:4], 2.5, None,
                                       ALU.mult),
             reads=[f"{u}.t3"], writes=[f"{u}.t3"], tag=f"{u}.s25{c}")
        P.op("vector",
             lambda e: e.tensor_scalar(eqs[:, :], eqs[:, :], tsum[:, 3:4], None,
                                       ALU.mult),
             reads=[f"{u}.eqs", f"{u}.t3"], writes=[f"{u}.eqs"],
             tag=f"{u}.cf{c}")
        P.op("tensor",
             (lambda c=c: lambda e: e.transpose(
                 ctx.pb[1][0:16, 0:128], eqs[:, :], ctx.ident[:, :]))(),
             reads=[f"{u}.eqs", "ident"], writes=["pb1"], tag=f"{u}.ctr{c}")
        P.op("scalar",
             (lambda c=c: lambda e: e.copy(comb_f[:, c * 128:(c + 1) * 128],
                                           ctx.pb[1][0:16, 0:128]))(),
             reads=["pb1"], writes=[f"{u}.combf", "lnrst.0", "lnrst.1"],
             tag=f"{u}.cev{c}")

    P.dma("gpsimd",
          lambda e: e.dma_start(out=comb_dram[:, :], in_=comb_f[:, :]),
          sem=f"{u}.cst", reads=[f"{u}.combf", "lnrst.0", "lnrst.1"],
          writes=[f"{u}.combd"], tag=f"{u}.cst")

    for ex in range(-1, NE):
        nic = ISC if ex < 0 else IME
        if ex >= 0:
            cr = crow[:, 0:T_LOC]
            P.dma("sync",
                  (lambda ex=ex, cr=cr: lambda e: e.dma_start(
                      out=cr, in_=comb_dram[ex:ex + 1, :]))(),
                  sem=f"{u}.crow", reads=[f"{u}.combd"],
                  writes=[f"{u}.crow"], tag=f"{u}.ldcr{ex}")
            for t in range(2):
                tcol = t * TW
                P.op("tensor",
                     (lambda cr=cr, tcol=tcol: lambda e: e.matmul(
                         ctx.pb[5][:, 0:TW], ctx.ones1[:, :],
                         cr[:, tcol:tcol + TW], start=True, stop=True))(),
                     reads=[f"{u}.crow", "ones1"], writes=["pb5"],
                     tag=f"{u}.cbm{ex}.{t}")
                P.op("scalar",
                     (lambda tcol=tcol: lambda e: e.copy(
                         cbb[:, tcol:tcol + TW], ctx.pb[5][:, 0:TW]))(),
                     reads=["pb5"], writes=[f"{u}.cbb.{t}", f"lnmr.{t}"],
                     tag=f"{u}.cbe{ex}.{t}")
        for i in range(nic):
            buf = i % 2
            gsrc = (f"{u}_swgP" if ex < 0 else f"{u}_wgP")
            usrc = (f"{u}_swuP" if ex < 0 else f"{u}_wuP")
            gidx = (i,) if ex < 0 else (ex, i)
            P.dma("gpsimd",
                  (lambda gsrc=gsrc, gidx=gidx, buf=buf: lambda e:
                   e.dma_start(
                      out=wg_s[:, buf * KC * 128:(buf + 1) * KC * 128],
                      in_=w[gsrc][gidx]))(),
                  sem=f"{u}.wgb.{buf}", writes=[f"{u}.wgb.{buf}"],
                  tag=f"{u}.ldg{ex}.{i}")
            P.dma("gpsimd",
                  (lambda usrc=usrc, gidx=gidx, buf=buf: lambda e:
                   e.dma_start(
                      out=wu_s[:, buf * KC * 128:(buf + 1) * KC * 128],
                      in_=w[usrc][gidx]))(),
                  sem=f"{u}.wub.{buf}", writes=[f"{u}.wub.{buf}"],
                  tag=f"{u}.ldu{ex}.{i}")
            for t in range(2):
                tcol = t * TW
                pg, pu = ctx.pb[0 + t], ctx.pb[2 + t]
                for k in range(KC):
                    P.op("tensor",
                         (lambda k=k, buf=buf, pg=pg, tcol=tcol: lambda e:
                          e.matmul(pg[:, 0:TW],
                                   wg_s[:, (buf * KC + k) * 128:
                                        (buf * KC + k + 1) * 128],
                                   xn[:, k * T_LOC + tcol:k * T_LOC + tcol + TW],
                                   start=(k == 0), stop=(k == KC - 1)))(),
                         reads=[f"{u}.wgb.{buf}", f"{u}.xn.{k}"],
                         writes=[f"pb{0 + t}"], tag=f"{u}.g{ex}.{i}.{k}.{t}")
                    P.op("tensor",
                         (lambda k=k, buf=buf, pu=pu, tcol=tcol: lambda e:
                          e.matmul(pu[:, 0:TW],
                                   wu_s[:, (buf * KC + k) * 128:
                                        (buf * KC + k + 1) * 128],
                                   xn[:, k * T_LOC + tcol:k * T_LOC + tcol + TW],
                                   start=(k == 0), stop=(k == KC - 1)))(),
                         reads=[f"{u}.wub.{buf}", f"{u}.xn.{k}"],
                         writes=[f"pb{2 + t}"], tag=f"{u}.u{ex}.{i}.{k}.{t}")
                sti = silu_t[:, t * TW:(t + 1) * TW]
                P.op("scalar",
                     (lambda sti=sti, pg=pg: lambda e: e.activation(
                         sti, pg[:, 0:TW], AF.Silu))(),
                     reads=[f"pb{0 + t}"], writes=[f"{u}.st.{t}"],
                     tag=f"{u}.silu{ex}.{i}.{t}")
                hv = hsl[:, i * T_LOC + tcol:i * T_LOC + tcol + TW]
                P.op("vector",
                     (lambda sti=sti, pu=pu, hv=hv: lambda e: e.tensor_mul(
                         hv, pu[:, 0:TW], sti))(),
                     reads=[f"pb{2 + t}", f"{u}.st.{t}"],
                     writes=[f"{u}.h.{i}.{t}"], tag=f"{u}.hm{ex}.{i}.{t}")
                if ex >= 0:
                    P.op("vector",
                         (lambda hv=hv, tcol=tcol, t=t: lambda e: e.tensor_mul(
                             hv, hv, cbb[:, tcol:tcol + TW]))(),
                         reads=[f"{u}.h.{i}.{t}", f"{u}.cbb.{t}",
                                f"lnmr.{t}"],
                         writes=[f"{u}.h.{i}.{t}"], tag=f"{u}.hc{ex}.{i}.{t}")
        dsrc = f"{u}_swdP" if ex < 0 else f"{u}_wdP"
        for o in range(KC):
            buf = o % 2
            didx = (o,) if ex < 0 else (ex, o)
            P.dma("gpsimd",
                  (lambda dsrc=dsrc, didx=didx, buf=buf, nic=nic: lambda e:
                   e.dma_start(
                      out=wd_s[:, buf * ISC * 128:buf * ISC * 128 + nic * 128],
                      in_=w[dsrc][didx]))(),
                  sem=f"{u}.wdb.{buf}", writes=[f"{u}.wdb.{buf}"],
                  tag=f"{u}.lddw{ex}.{o}")
            for t in range(2):
                tcol = t * TW
                po = ctx.pb[4 if t == 0 else 6]
                for i in range(nic):
                    P.op("tensor",
                         (lambda i=i, buf=buf, po=po, nic=nic, tcol=tcol:
                          lambda e: e.matmul(
                             po[:, 0:TW],
                             wd_s[:, (buf * ISC + i) * 128:
                                  (buf * ISC + i + 1) * 128],
                             hsl[:, i * T_LOC + tcol:i * T_LOC + tcol + TW],
                             start=(i == 0), stop=(i == nic - 1)))(),
                         reads=[f"{u}.wdb.{buf}", f"{u}.h.{i}.{t}"],
                         writes=[f"pb{4 if t == 0 else 6}"],
                         tag=f"{u}.d{ex}.{o}.{i}.{t}")
                av = acc[:, o * T_LOC + tcol:o * T_LOC + tcol + TW]
                if ex < 0:
                    P.op("scalar",
                         (lambda av=av, po=po: lambda e: e.copy(av,
                                                                po[:, 0:TW]))(),
                         reads=[f"pb{4 if t == 0 else 6}"],
                         writes=[f"{u}.acc.{o}.{t}"],
                         tag=f"{u}.ai{ex}.{o}.{t}")
                else:
                    P.op("vector",
                         (lambda av=av, po=po: lambda e: e.tensor_add(
                             av, av, po[:, 0:TW]))(),
                         reads=[f"pb{4 if t == 0 else 6}", f"{u}.acc.{o}.{t}"],
                         writes=[f"{u}.acc.{o}.{t}"], tag=f"{u}.aa{ex}.{o}.{t}")

    st = LNStats(ctx, f"{u}pl")
    for t in range(2):
        for o in range(KC):
            st.add(acc[:, o * T_LOC + t * TW:o * T_LOC + (t + 1) * TW],
                   [f"{u}.acc.{o}.{t}"])
    st.finish()
    for k in range(KC):
        for t in range(2):
            r = (2 * k + t) % 4
            av = acc[:, k * T_LOC + t * TW:k * T_LOC + (t + 1) * TW]
            xv = ctx.ln_y[:, r * TW:(r + 1) * TW]
            P.dma("sync",
                  (lambda k=k, t=t, xv=xv: lambda e: e.dma_start(
                      out=xv, in_=x_dram[k * 128:(k + 1) * 128,
                                         t * TW:(t + 1) * TW]))(),
                  sem=f"lny.{r}", reads=(xkeys(k) if xkeys else []),
                  writes=[f"lny.{r}"], tag=f"{u}pl.ldx.{k}.{t}")
            P.op("vector",
                 (lambda av=av, t=t: lambda e: e.tensor_mul(
                     av, av, ctx.ln_rst[:, t * TW:(t + 1) * TW]))(),
                 reads=[f"{u}.acc.{k}.{t}", f"lnrst.{t}"],
                 writes=[f"{u}.acc.{k}.{t}"], tag=f"{u}pl.a1.{k}.{t}")
            P.op("vector",
                 (lambda av=av, t=t: lambda e: e.tensor_sub(
                     av, av, ctx.ln_mr[:, t * TW:(t + 1) * TW]))(),
                 reads=[f"{u}.acc.{k}.{t}", f"lnmr.{t}"],
                 writes=[f"{u}.acc.{k}.{t}"], tag=f"{u}pl.a2.{k}.{t}")
            P.op("vector",
                 (lambda av=av, xv=xv: lambda e: e.tensor_add(av, av, xv))(),
                 reads=[f"{u}.acc.{k}.{t}", f"lny.{r}"],
                 writes=[f"{u}.acc.{k}.{t}"], tag=f"{u}pl.a3.{k}.{t}")
            P.dma("gpsimd",
                  (lambda k=k, t=t, av=av: lambda e: e.dma_start(
                      out=xout_dram[k * 128:(k + 1) * 128, t * TW:(t + 1) * TW],
                      in_=av))(),
                  sem=f"{u}pl.xost", reads=[f"{u}.acc.{k}.{t}"],
                  writes=[f"{u}pl.xog"] + ([f"{xout_key}.{k}"] if xout_key else []),
                  tag=f"{u}pl.st{k}.{t}")


def _prep_m1(inputs):
    w = {}
    w["m1_gwP"] = pack_lhsT(np.ascontiguousarray(inputs["L1_ffn_gate_w"].T), 16)
    w["m1_swgP"] = pack_lhsT(inputs["L1_ffn_sWg"], 128)
    w["m1_swuP"] = pack_lhsT(inputs["L1_ffn_sWu"], 128)
    w["m1_swdP"] = pack_lhsT(inputs["L1_ffn_sWd"], 128)
    w["m1_wgP"] = np.stack([pack_lhsT(inputs["L1_ffn_Wg"][e], 128)
                            for e in range(NE)])
    w["m1_wuP"] = np.stack([pack_lhsT(inputs["L1_ffn_Wu"][e], 128)
                            for e in range(NE)])
    w["m1_wdP"] = np.stack([pack_lhsT(inputs["L1_ffn_Wd"][e], 128)
                            for e in range(NE)])
    return w


def _dram_m1(nc):
    w = {}
    w["m1_gwP"] = nc.dram_tensor("m1_gwP", [1, 128, KC * 16], F32,
                                 kind="ExternalInput").ap()
    w["m1_swgP"] = nc.dram_tensor("m1_swgP", [ISC, 128, KC * 128], F32,
                                  kind="ExternalInput").ap()
    w["m1_swuP"] = nc.dram_tensor("m1_swuP", [ISC, 128, KC * 128], F32,
                                  kind="ExternalInput").ap()
    w["m1_swdP"] = nc.dram_tensor("m1_swdP", [KC, 128, ISC * 128], F32,
                                  kind="ExternalInput").ap()
    w["m1_wgP"] = nc.dram_tensor("m1_wgP", [NE, IME, 128, KC * 128], F32,
                                 kind="ExternalInput").ap()
    w["m1_wuP"] = nc.dram_tensor("m1_wuP", [NE, IME, 128, KC * 128], F32,
                                 kind="ExternalInput").ap()
    w["m1_wdP"] = nc.dram_tensor("m1_wdP", [NE, KC, 128, IME * 128], F32,
                                 kind="ExternalInput").ap()
    return w


def build_m1_only(debug=False):
    nc = bass.Bass("TRN2", target_bir_lowering=False)
    x_in = nc.dram_tensor("x_in", [D, T_LOC], F32, kind="ExternalInput").ap()
    x_out = nc.dram_tensor("x_out", [D, T_LOC], F32, kind="ExternalOutput").ap()
    ident = nc.dram_tensor("ident_in", [128, 128], F32, kind="ExternalInput").ap()
    consts = nc.dram_tensor("consts_in", [128, 16], F32, kind="ExternalInput").ap()
    with ExitStack() as stack:
        ctx = Ctx(nc, stack)
        ctx.load_consts(ident, consts)
        w = _dram_m1(nc)
        comb_dram = nc.dram_tensor("m1_comb_scratch", [16, T_LOC], F32,
                                   kind="Internal").ap()
        emit_l1_moe(ctx, x_in, x_out, w, comb_dram)
        if debug:
            P = ctx.P
            cf_dbg = nc.dram_tensor("dbg_combf", [16, T_LOC], F32,
                                    kind="ExternalOutput").ap()
            P.dma("gpsimd", lambda e: e.dma_start(out=cf_dbg[:, :],
                                                  in_=ctx._m1_combf[:, :]),
                  sem="dbg", reads=["m1.combf"], tag="dbg.cf")
            acc_dbg = nc.dram_tensor("dbg_acc", [128, KC * T_LOC], F32,
                                     kind="ExternalOutput").ap()
            P.dma("gpsimd", lambda e: e.dma_start(out=acc_dbg[:, :],
                                                  in_=ctx._m1_acc[:, :]),
                  sem="dbg",
                  reads=[f"m1.acc.{o}.{t}" for o in range(KC) for t in range(2)],
                  tag="dbg.acc")
        ctx.P.finalize()
    return nc, ctx.P.stats()



def pack_lhsT(W, mg):
    K, M = W.shape
    kc = K // 128
    G = M // mg
    return np.ascontiguousarray(
        W.reshape(kc, 128, G, mg).transpose(2, 1, 0, 3).reshape(G, 128, kc * mg))


def col128(v):
    n = v.shape[0]
    return np.ascontiguousarray(v.reshape(n // 128, 128).T)


def _prep_f0(inputs):
    w = {}
    w["f0_gate_wP"] = pack_lhsT(np.ascontiguousarray(inputs["L0_ffn_gate_w"].T), 128)
    w["f0_up_wP"] = pack_lhsT(np.ascontiguousarray(inputs["L0_ffn_up_w"].T), 128)
    w["f0_down_wP"] = pack_lhsT(np.ascontiguousarray(inputs["L0_ffn_down_w"].T), 128)
    w["f0_gate_s"] = col128(inputs["L0_ffn_gate_s"])
    w["f0_up_s"] = col128(inputs["L0_ffn_up_s"])
    w["f0_down_s"] = col128(inputs["L0_ffn_down_s"])
    return w


def _dram_f0(nc):
    w = {}
    w["f0_gate_wP"] = nc.dram_tensor("f0_gate_wP", [IC0, 128, KC * 128], F32,
                                     kind="ExternalInput").ap()
    w["f0_up_wP"] = nc.dram_tensor("f0_up_wP", [IC0, 128, KC * 128], F32,
                                   kind="ExternalInput").ap()
    w["f0_down_wP"] = nc.dram_tensor("f0_down_wP", [KC, 128, IC0 * 128], F32,
                                     kind="ExternalInput").ap()
    w["f0_gate_s"] = nc.dram_tensor("f0_gate_s", [128, IC0], F32,
                                    kind="ExternalInput").ap()
    w["f0_up_s"] = nc.dram_tensor("f0_up_s", [128, IC0], F32,
                                  kind="ExternalInput").ap()
    w["f0_down_s"] = nc.dram_tensor("f0_down_s", [128, KC], F32,
                                    kind="ExternalInput").ap()
    return w



def build_f0_only(debug=False):
    nc = bass.Bass("TRN2", target_bir_lowering=False)
    x_in = nc.dram_tensor("x_in", [D, T_LOC], F32, kind="ExternalInput").ap()
    x_out = nc.dram_tensor("x_out", [D, T_LOC], F32, kind="ExternalOutput").ap()
    ident = nc.dram_tensor("ident_in", [128, 128], F32, kind="ExternalInput").ap()
    consts = nc.dram_tensor("consts_in", [128, 16], F32, kind="ExternalInput").ap()
    with ExitStack() as stack:
        ctx = Ctx(nc, stack)
        ctx.load_consts(ident, consts)
        w = _dram_f0(nc)
        y_dram = nc.dram_tensor("f0_y_scratch", [D, T_LOC], F32,
                                kind="ExternalOutput" if debug else "Internal").ap()
        emit_l0_ffn(ctx, x_in, x_out, w, y_dram)
        if debug:
            xn_t = ctx.nc.find_tensor if False else None
            dbg_xn = nc.dram_tensor("dbg_xn", [128, KC * T_LOC], BF16,
                                    kind="ExternalOutput").ap()
            xn_sb = ctx._dbg_xn
            for k in range(KC):
                ctx.P.dma("gpsimd",
                          (lambda k=k: lambda e: e.dma_start(
                              out=dbg_xn[:, k * T_LOC:(k + 1) * T_LOC],
                              in_=xn_sb[:, k * T_LOC:(k + 1) * T_LOC]))(),
                          sem="dbg", reads=[f"f0.xn.{k}"], tag=f"dbg.xn{k}")
            dbg_h = nc.dram_tensor("dbg_h", [128, IC0 * T_LOC], BF16,
                                   kind="ExternalOutput").ap()
            ctx.P.dma("gpsimd",
                      lambda e: e.dma_start(out=dbg_h[:, :], in_=ctx._dbg_h[:, :]),
                      sem="dbg", reads=[f"f0.h.{i}" for i in range(IC0)],
                      tag="dbg.h")
            dbg_rst = nc.dram_tensor("dbg_rst", [128, T_LOC], F32,
                                     kind="ExternalOutput").ap()
            dbg_mr = nc.dram_tensor("dbg_mr", [128, T_LOC], F32,
                                    kind="ExternalOutput").ap()
            ctx.P.dma("gpsimd",
                      lambda e: e.dma_start(out=dbg_rst[:, :], in_=ctx.ln_rst[:, :]),
                      sem="dbg", reads=["lnrst.0", "lnrst.1"], tag="dbg.rst")
            ctx.P.dma("gpsimd",
                      lambda e: e.dma_start(out=dbg_mr[:, :], in_=ctx.ln_mr[:, :]),
                      sem="dbg", reads=["lnmr.0", "lnmr.1"], tag="dbg.mr")
        ctx.P.finalize()
    return nc, ctx.P.stats()


def _shard_x(x):
    shards = []
    for c in range(N_CORES):
        b, h = c // 2, c % 2
        shards.append(np.ascontiguousarray(x[b, h * T_LOC:(h + 1) * T_LOC, :].T))
    return shards


def _unshard_x(shards):
    out = np.empty((4, S_FULL, D), np.float32)
    for c in range(N_CORES):
        b, h = c // 2, c % 2
        out[b, h * T_LOC:(h + 1) * T_LOC, :] = np.asarray(shards[c]).T
    return out


_IDENT = np.eye(128, dtype=np.float32)


def build_fused():
    nc = bass.Bass("TRN2", target_bir_lowering=False)
    x_in = nc.dram_tensor("x_in", [D, T_LOC], F32, kind="ExternalInput").ap()
    x_out = nc.dram_tensor("x_out", [D, T_LOC], F32, kind="ExternalOutput").ap()
    ident = nc.dram_tensor("ident_in", [128, 128], F32, kind="ExternalInput").ap()
    consts = nc.dram_tensor("consts_in", [128, 16], F32, kind="ExternalInput").ap()
    x1 = nc.dram_tensor("x1_mid", [D, T_LOC], F32, kind="Internal").ap()
    x2 = nc.dram_tensor("x2_mid", [D, T_LOC], F32, kind="Internal").ap()
    x3 = nc.dram_tensor("x3_mid", [D, T_LOC], F32, kind="Internal").ap()
    wa0 = _dram_a0(nc)
    wf0 = _dram_f0(nc)
    wa1 = _dram_a1(nc)
    wm1 = _dram_m1(nc)
    scr_a0 = _scratch_a0(nc)
    scr_a1 = _scratch_a1(nc)
    y_f0 = nc.dram_tensor("f0_y_scratch", [D, T_LOC], F32, kind="Internal").ap()
    y_a1 = nc.dram_tensor("a1_y_scratch", [D, T_LOC], F32, kind="Internal").ap()
    comb_dram = nc.dram_tensor("m1_comb_scratch", [16, T_LOC], F32,
                               kind="Internal").ap()
    with ExitStack() as stack:
        ctx = Ctx(nc, stack)
        ctx.load_consts(ident, consts)
        main_stack = ctx.stack
        with ExitStack() as s0:
            ctx.stack = s0
            emit_l0_attn(ctx, x_in, x1, wa0, scr_a0)
        ctx.P.fence(ctx.ln_sq[0:1, 0:1])
        with ExitStack() as s1:
            ctx.stack = s1
            emit_l0_ffn(ctx, x1, x2, wf0, y_f0)
        ctx.P.fence(ctx.ln_sq[0:1, 0:1])
        with ExitStack() as s2:
            ctx.stack = s2
            emit_l1_attn(ctx, x2, x3, wa1, y_a1, scr_a1)
        ctx.P.fence(ctx.ln_sq[0:1, 0:1])
        with ExitStack() as s3:
            ctx.stack = s3
            emit_l1_moe(ctx, x3, x_out, wm1, comb_dram)
        ctx.stack = main_stack
        ctx.P.finalize()
    return nc, ctx.P.stats()


_BUILD_CACHE = {}


def _get_nc(name, builder):
    if name not in _BUILD_CACHE:
        _BUILD_CACHE[name] = builder()
    return _BUILD_CACHE[name]


def kernel(**inputs):
    inputs = {k: np.ascontiguousarray(np.asarray(v)) for k, v in inputs.items()}
    shards = _shard_x(inputs["L0_attn_x"])
    common = {"ident_in": _IDENT, "consts_in": _CONSTS_IN}
    nc = _get_nc("fused", lambda: build_fused()[0])
    wf0 = _prep_f0(inputs)
    wa1 = _prep_a1(inputs)
    wm1 = _prep_m1(inputs)
    in_maps = []
    for c in range(N_CORES):
        m = {**common, **wf0, **wa1, **wm1, **_prep_a0(inputs, c)}
        m["x_in"] = shards[c]
        in_maps.append(m)
    res = run_bass_kernel_spmd(nc, in_maps, core_ids=list(range(N_CORES)))
    return _unshard_x([res.results[c]["x_out"] for c in range(N_CORES)])


def kernel_unfused(**inputs):
    inputs = {k: np.ascontiguousarray(np.asarray(v)) for k, v in inputs.items()}
    shards = _shard_x(inputs["L0_attn_x"])
    common = {"ident_in": _IDENT, "consts_in": _CONSTS_IN}

    nc = _get_nc("a0", lambda: build_a0_only()[0])
    wa0 = [_prep_a0(inputs, c) for c in range(N_CORES)]
    in_maps = [{**common, **wa0[c], "x_in": shards[c]} for c in range(N_CORES)]
    res = run_bass_kernel_spmd(nc, in_maps, core_ids=list(range(N_CORES)))
    shards = [np.ascontiguousarray(np.asarray(res.results[c]["x_out"]))
              for c in range(N_CORES)]

    nc = _get_nc("f0", lambda: build_f0_only()[0])
    wf0 = _prep_f0(inputs)
    in_maps = [{**common, **wf0, "x_in": shards[c]} for c in range(N_CORES)]
    res = run_bass_kernel_spmd(nc, in_maps, core_ids=list(range(N_CORES)))
    shards = [np.ascontiguousarray(np.asarray(res.results[c]["x_out"]))
              for c in range(N_CORES)]

    nc = _get_nc("a1", lambda: build_a1_only()[0])
    wa1 = _prep_a1(inputs)
    in_maps = [{**common, **wa1, "x_in": shards[c]} for c in range(N_CORES)]
    res = run_bass_kernel_spmd(nc, in_maps, core_ids=list(range(N_CORES)))
    shards = [np.ascontiguousarray(np.asarray(res.results[c]["x_out"]))
              for c in range(N_CORES)]

    nc = _get_nc("m1", lambda: build_m1_only()[0])
    wm1 = _prep_m1(inputs)
    in_maps = [{**common, **wm1, "x_in": shards[c]} for c in range(N_CORES)]
    res = run_bass_kernel_spmd(nc, in_maps, core_ids=list(range(N_CORES)))
    shards = [res.results[c]["x_out"] for c in range(N_CORES)]
    return _unshard_x(shards)
